# version 05
# baseline (speedup 1.0000x reference)
import contextlib
import math
import numpy as np
import ml_dtypes
import concourse.bass as bass
import concourse.mybir as mybir
from concourse.bass_utils import run_bass_kernel_spmd

F32 = mybir.dt.float32
BF16 = mybir.dt.bfloat16
I32 = mybir.dt.int32
AF = mybir.ActivationFunctionType
ALU = mybir.AluOpType
AX = mybir.AxisListType.X
BF = ml_dtypes.bfloat16

D = 1024
DT = D // 128
H = 16
DH = 64
NHP = H // 2
EPS = 1e-5

ENGS = ("sync", "gpsimd", "tensor", "scalar", "vector")


class Sem:
    def __init__(self, kb, name):
        self.h = kb.st.enter_context(kb.nc.semaphore(name))
        self.n = 0
        self.is_dma = False

    def reserve(self, k=1):
        if k >= 16:
            self.is_dma = True
        self.n += k
        return self.n


class KB:
    def __init__(self, nc, st, T):
        self.nc = nc
        self.st = st
        self.T = T
        self.TC = T // 512
        self.sems = {}
        self.uid = 0

    def sem(self, name):
        if name not in self.sems:
            self.sems[name] = Sem(self, name)
        return self.sems[name]

    def name(self, base):
        self.uid += 1
        return f"{base}_{self.uid}"

    def sb(self, base, shape, dt, st=None, side=None):
        if side is not None:
            return (st or self.st).enter_context(
                self.nc.sbuf_tensor(self.name(base), shape, dt, side=side))
        return (st or self.st).enter_context(
            self.nc.sbuf_tensor(self.name(base), shape, dt))

    def ps(self, base, shape, dt=F32, st=None):
        return (st or self.st).enter_context(
            self.nc.psum_tensor(self.name(base), shape, dt))


class Ph:

    def __init__(self, kb):
        self.kb = kb
        self.ops = {k: [] for k in ENGS}

    def on(self, eng, fn):
        self.ops[eng].append(fn)

    def wait(self, eng, sem, val):
        self.on(eng, lambda e: e.wait_ge(sem.h, val))

    def do(self, eng, fn, sem=None, k=1):
        if sem is None:
            self.on(eng, fn)
            return None
        t = sem.reserve(k)
        self.on(eng, lambda e: fn(e).then_inc(sem.h, k))
        return t

    def emit(self):
        with self.kb.nc.Block() as blk:
            for eng in ENGS:
                fns = self.ops[eng]
                if fns:
                    getattr(blk, eng)(lambda e, fns=fns: [fn(e) for fn in fns])
        self.ops = {k: [] for k in ENGS}


def barrier(kb, ph):
    b = kb.sem("bar")
    for eng in ENGS:
        ph.do(eng, lambda e: e.nop(), b, 1)
    return b.n


def wait_bar(kb, ph, val):
    b = kb.sem("bar")
    for eng in ENGS:
        ph.wait(eng, b, val)


def host_constants(T):
    c = {}
    c["ident"] = np.eye(128, dtype=BF)
    ones = np.ones((128, 128), dtype=BF)
    c["ones"] = ones
    inv = 10000.0 ** (-np.arange(0, DH, 2, dtype=np.float64) / DH)
    t = np.arange(T, dtype=np.float64)
    f = np.outer(t, inv)
    cos0 = np.cos(np.concatenate([f, f], 1)).T
    sin0 = np.sin(np.concatenate([f, f], 1)).T
    c["cos0"] = np.ascontiguousarray(np.tile(cos0, (2, 1)), dtype=None).astype(BF)
    c["sin0"] = np.ascontiguousarray(np.tile(sin0, (2, 1))).astype(BF)
    cosI = np.repeat(np.cos(f), 2, axis=1).T
    sinI = np.repeat(np.sin(f), 2, axis=1).T
    c["cosI"] = np.ascontiguousarray(np.tile(cosI, (2, 1))).astype(BF)
    c["sinI"] = np.ascontiguousarray(np.tile(sinI, (2, 1))).astype(BF)
    M0 = np.zeros((64, 64), dtype=np.float32)
    for i in range(32):
        M0[i + 32, i] = -1.0
        M0[i, i + 32] = 1.0
    MI = np.zeros((64, 64), dtype=np.float32)
    for i in range(0, 64, 2):
        MI[i + 1, i] = -1.0
        MI[i, i + 1] = 1.0
    perm0 = np.zeros((128, 128), dtype=np.float32)
    perm0[0:64, 0:64] = M0
    perm0[64:128, 64:128] = M0
    permI = np.zeros((128, 128), dtype=np.float32)
    permI[0:64, 0:64] = MI
    permI[64:128, 64:128] = MI
    c["perm0"] = perm0.astype(BF)
    c["permI"] = permI.astype(BF)
    for P in (8, 9, 16, 32):
        s = np.zeros((P, P * 128), dtype=np.float32)
        for n in range(P):
            s[n, n * 128:(n + 1) * 128] = 1.0
        c[f"sel{P}"] = s.astype(BF)
    return c


CONST_SHAPES = {
    "ident": [128, 128], "ones": [128, 128],
    "cos0": ("T",), "sin0": ("T",), "cosI": ("T",), "sinI": ("T",),
    "perm0": [128, 128], "permI": [128, 128],
    "sel8": [8, 8 * 128], "sel9": [9, 9 * 128],
    "sel16": [16, 16 * 128], "sel32": [32, 32 * 128],
}


def declare_constants(nc, T, names):
    aps = {}
    for k in names:
        shp = CONST_SHAPES[k]
        if shp == ("T",):
            shp = [128, T]
        aps[k] = nc.dram_tensor("c_" + k, shp, BF16, kind="ExternalInput").ap()
    return aps


class Cst:

    def __init__(self, kb, caps, resident=None):
        T = kb.T
        self.names = list(caps.keys()) if resident is None else list(resident)
        for k in self.names:
            shp = CONST_SHAPES[k]
            if shp == ("T",):
                shp = [128, T]
            setattr(self, k, kb.sb(k, shp, BF16))
        self.cvals = kb.sb("cvals", [128, 8], F32)
        self.cmap = {}
        self.caps = caps

    def cval(self, v, parts=128):
        i = self.cmap[float(v)]
        return self.cvals[0:parts, i:i + 1]

    def load(self, kb, ph, vals=(EPS, -128.0, 1.0 / 127.0, 0.5)):
        ds = kb.sem("dc")
        sv = kb.sem("v")
        last = None
        for n in self.names:
            last = ph.do("sync", lambda e, n=n: e.dma_start(
                getattr(self, n)[:], self.caps[n][:]), ds, 16)
        tms = None
        for i, v in enumerate(vals):
            self.cmap[float(v)] = i
            tms = ph.do("vector", lambda e, i=i, v=v: e.memset(
                self.cvals[:, i:i + 1], float(v)), sv, 1)
        return [(ds, last), (sv, tms)]


def load_sel(kb, cst, ph, name, P, st):
    t = kb.sb(name, [P, P * 128], BF16, st=st)
    dsl = kb.sem("dsel")
    ph.wait("sync", dsl, dsl.n)
    tk = ph.do("sync", lambda e: e.dma_start(t[:], cst.caps[name][:]), dsl, 16)
    for eng in ("tensor", "vector", "scalar"):
        ph.wait(eng, dsl, tk)
    return t


def dma_in(ph, kb, dst, src, cast=False):
    q = "gpsimd" if cast else "sync"
    s = kb.sem("dgp" if cast else "dsync")
    return ph.do(q, lambda e: e.dma_start(dst, src), s, 16), s


def dma_out(ph, kb, dst, src):
    s = kb.sem("dout")
    return ph.do("sync", lambda e: e.dma_start(dst, src), s, 16), s


def emit_ln(kb, ph, sub, xin_ap, xn, cst, gains=None, slot_out=None,
            xsrc_sb=None, extra_scale=None, wait_first=None,
            xsrc_inplace=False):
    T = kb.T
    nc = kb.nc
    dg = kb.sem("dlnx")
    sv = kb.sem("v")
    ss = kb.sem("s")
    stt = kb.sem("t")

    lst = contextlib.ExitStack()
    xb = xn
    xsq = kb.sb("ln_xsq", [128, 2 * T], BF16, st=lst)
    rows = kb.sb("ln_rows", [1, 3 * T], F32, st=lst)
    rows_bf = kb.sb("ln_rows_bf", [1, 2 * T], BF16, st=lst)
    mbc = kb.sb("ln_mbc", [128, T], BF16, st=lst)
    rbc = kb.sb("ln_rbc", [128, T], BF16, st=lst)
    with contextlib.ExitStack() as pst:
        pstat = kb.ps("ln_pstat", [1, T], F32, st=pst)
        if wait_first:
            for s_, t_ in wait_first:
                ph.wait("gpsimd", s_, t_)
        ldt = []
        if xsrc_inplace:
            if wait_first:
                for s_, t_ in wait_first:
                    ph.wait("tensor", s_, t_)
                    ph.wait("scalar", s_, t_)
                    ph.wait("vector", s_, t_)
            ldt.append((sv, kb.sems["v"].n))
        else:
            for j in range(DT):
                if xsrc_sb is None:
                    t0 = ph.do("gpsimd", lambda e, j=j: e.dma_start(
                        xb[:, j * T:(j + 1) * T], xin_ap[j * 128:(j + 1) * 128, :]), dg, 16)
                    ldt.append((dg, t0))
                else:
                    t0 = ph.do("vector", lambda e, j=j: e.tensor_copy(
                        xb[:, j * T:(j + 1) * T], xsrc_sb[j]), sv, 1)
                    ldt.append((sv, t0))
            ph.wait("tensor", ldt[-1][0], ldt[-1][1])
            ph.wait("scalar", ldt[-1][0], ldt[-1][1])
        for j in range(DT):
            for c in range(kb.TC):
                ph.do("tensor", lambda e, j=j, c=c: e.matmul(
                    pstat[:, c * 512:(c + 1) * 512], cst.ones[:, 0:1],
                    xb[:, j * T + c * 512: j * T + (c + 1) * 512],
                    start=(j == 0), stop=(j == DT - 1)),
                    stt if (j == DT - 1 and c == kb.TC - 1) else None)
        tsum = kb.sems["t"].n
        ph.wait("vector", stt, tsum)
        t_evs = ph.do("vector", lambda e: e.tensor_copy(rows[:, 0:T], pstat[:]), sv, 1)
        sq_ticks = []
        for j in range(DT):
            if j >= 2:
                ph.wait("scalar", stt, sq_ticks[j - 2])
            ph.do("scalar", lambda e, j=j: e.activation(
                xsq[:, (j % 2) * T:(j % 2 + 1) * T], xb[:, j * T:(j + 1) * T],
                AF.Square), ss, 1)
            sqs = kb.sems["s"].n
            ph.wait("tensor", ss, sqs)
            if j == 0:
                ph.wait("tensor", sv, t_evs)
            for c in range(kb.TC):
                tick = ph.do("tensor", lambda e, j=j, c=c: e.matmul(
                    pstat[:, c * 512:(c + 1) * 512], cst.ones[:, 0:1],
                    xsq[:, (j % 2) * T + c * 512:(j % 2) * T + (c + 1) * 512],
                    start=(j == 0), stop=(j == DT - 1)),
                    stt, 1)
            sq_ticks.append(tick)
        ph.wait("vector", stt, sq_ticks[-1])
        t_evs2 = ph.do("vector", lambda e: e.tensor_copy(rows[:, T:2 * T], pstat[:]), sv, 1)
        ph.wait("vector", sv, t_evs2)
        t1 = ph.do("vector", lambda e: e.tensor_single_scalar(
            rows[:, 0:T], rows[:, 0:T], 1.0 / D, ALU.mult), sv, 1)
        ph.wait("scalar", sv, t1)
        t2 = ph.do("scalar", lambda e: e.activation(
            rows[:, 2 * T:3 * T], rows[:, 0:T], AF.Square), ss, 1)
        ph.wait("vector", ss, t2)
        t3 = ph.do("vector", lambda e: e.scalar_tensor_tensor(
            rows[:, T:2 * T], rows[:, T:2 * T], 1.0 / D, rows[:, 2 * T:3 * T],
            ALU.mult, ALU.subtract), sv, 1)
        ph.wait("scalar", sv, t3)
        t4 = ph.do("scalar", lambda e: e.activation(
            rows[:, 2 * T:3 * T], rows[:, T:2 * T], AF.Ln,
            bias=cst.cval(EPS, 1)), ss, 1)
        ph.wait("scalar", ss, t4)
        t5 = ph.do("scalar", lambda e: e.activation(
            rows[:, T:2 * T], rows[:, 2 * T:3 * T], AF.Exp, scale=-0.5), ss, 1)
        if extra_scale is not None:
            ph.wait("vector", ss, t5)
            t5v = ph.do("vector", lambda e: e.tensor_single_scalar(
                rows[:, T:2 * T], rows[:, T:2 * T], float(extra_scale),
                ALU.mult), sv, 1)
            ph.wait("vector", sv, t5v)
        else:
            ph.wait("vector", ss, t5)
        t6 = ph.do("vector", lambda e: e.tensor_copy(
            rows_bf[:, 0:T], rows[:, 0:T]), sv, 1)
        t7 = ph.do("vector", lambda e: e.tensor_copy(
            rows_bf[:, T:2 * T], rows[:, T:2 * T]), sv, 1)
        ph.wait("tensor", sv, t7)
        ph.emit()
    with contextlib.ExitStack() as pst:
        pbc = [kb.ps("ln_pbc", [128, 512], F32, st=pst) for _ in range(4)]
        ev_t = {}
        for which, dst in ((0, mbc), (1, rbc)):
            for c in range(kb.TC):
                bi = (which * kb.TC + c) % 4
                if (which * kb.TC + c) >= 4:
                    ph.wait("tensor", sv, ev_t[which * kb.TC + c - 4])
                mt = ph.do("tensor", lambda e, which=which, c=c, bi=bi: e.matmul(
                    pbc[bi][:], cst.ones[0:1, :],
                    rows_bf[:, which * T + c * 512: which * T + (c + 1) * 512],
                    start=True, stop=True), stt, 1)
                ph.wait("vector", stt, mt)
                ev_t[which * kb.TC + c] = ph.do(
                    "vector", lambda e, dst=dst, c=c, bi=bi: e.tensor_copy(
                        dst[:, c * 512:(c + 1) * 512], pbc[bi][:]), sv, 1)
        last_apply = []
        ph.wait("vector", stt, sq_ticks[-1])
        for j in range(DT):
            ph.wait("vector", sv, ev_t[2 * kb.TC - 1])
            ta = ph.do("vector", lambda e, j=j: e.tensor_tensor(
                xn[:, j * T:(j + 1) * T], xb[:, j * T:(j + 1) * T], mbc[:],
                ALU.subtract), sv, 1)
            ph.wait("vector", sv, ta)
            tb = ph.do("vector", lambda e, j=j: e.tensor_tensor(
                xn[:, j * T:(j + 1) * T], xn[:, j * T:(j + 1) * T], rbc[:],
                ALU.mult), sv, 1)
            if gains is not None:
                g_sb, b_sb = gains
                ph.wait("vector", sv, tb)
                tb = ph.do("vector", lambda e, j=j: e.tensor_scalar(
                    xn[:, j * T:(j + 1) * T], xn[:, j * T:(j + 1) * T],
                    g_sb[:, j:j + 1], b_sb[:, j:j + 1], ALU.mult, ALU.add), sv, 1)
            last_apply.append(tb)
        ph.emit()
    lst.close()
    return last_apply


def emit_proj(kb, ph, sub, xn, w_dram, M_out, evac, n_k=DT, xn_stride=None,
              wait_before=None, kslices=None):
    T = kb.T
    dg = kb.sem("dgp")
    stt = kb.sem("t")
    wsb = kb.sb("proj_w", [128, n_k * M_out], BF16, st=sub)
    wticks = []
    for k in range(n_k):
        t0 = ph.do("gpsimd", lambda e, k=k: e.dma_start(
            wsb[:, k * M_out:(k + 1) * M_out], w_dram[k * 128:(k + 1) * 128, :]),
            dg, 16)
        wticks.append(t0)
    bank_gate = {}
    nb = 4
    with contextlib.ExitStack() as pst:
        pp = [kb.ps("proj_pp", [128, 512], F32, st=pst) for _ in range(nb)]
        gi = 0
        for o in range(M_out // 128):
            for c in range(kb.TC):
                bi = gi % nb
                if gi >= nb:
                    s_, t_ = bank_gate[gi - nb]
                    ph.wait("tensor", s_, t_)
                if gi == 0:
                    ph.wait("tensor", dg, wticks[-1])
                    if wait_before is not None:
                        for s_, t_ in wait_before:
                            ph.wait("tensor", s_, t_)
                for k in range(n_k):
                    if kslices is not None:
                        src, off = kslices[k]
                        rhs = lambda e, k=k, c=c, src=src, off=off: src[
                            :, off + c * 512: off + (c + 1) * 512]
                    else:
                        rhs = lambda e, k=k, c=c: xn[:, k * T + c * 512:
                                                     k * T + (c + 1) * 512]
                    tick = ph.do("tensor", lambda e, k=k, o=o, c=c, bi=bi,
                                 rhs=rhs: e.matmul(
                        pp[bi][:], wsb[:, k * M_out + o * 128:
                                       k * M_out + (o + 1) * 128],
                        rhs(e), start=(k == 0), stop=(k == n_k - 1)),
                        stt if k == n_k - 1 else None)
                bank_gate[gi] = evac(o, c, pp[bi], ph, (stt, tick))
                gi += 1
        ph.emit()



def emit_qkv(kb, cst, ph, sub, xn, projs, wait_ln):
    T = kb.T
    dg = kb.sem("dwq")
    dqo = [kb.sem("dqo0"), kb.sem("dqo1")]
    stt = kb.sem("t")
    ss = kb.sem("s")
    sv = kb.sem("v")
    M_out = D

    rope_any = any(pr.get("post") in ("rope0", "ropeI", "elu1_ropeI")
                   for pr in projs)
    with contextlib.ExitStack() as pst:
        if rope_any:
            cosT = kb.sb("qkv_cos", [128, T], BF16, st=pst)
            sinT = kb.sb("qkv_sin", [128, T], BF16, st=pst)
            permM = kb.sb("qkv_perm", [128, 128], BF16, st=pst)
            drt = kb.sem("drope")
            style = [pr.get("post") for pr in projs
                     if pr.get("post") in ("rope0", "ropeI", "elu1_ropeI")][0]
            cn = ("cos0", "sin0", "perm0") if style == "rope0" else \
                 ("cosI", "sinI", "permI")
            rt = None
            for dst_, nmm in ((cosT, cn[0]), (sinT, cn[1]), (permM, cn[2])):
                rt = ph.do("sync", lambda e, dst_=dst_, nmm=nmm: e.dma_start(
                    dst_[:], cst.caps[nmm][:]), drt, 16)
            ph.wait("tensor", drt, rt)
            ph.wait("vector", drt, rt)
        praw = kb.sb("qkv_praw", [128, 2 * T], BF16, st=pst)
        pout = kb.sb("qkv_pout", [128, 2 * T], BF16, st=pst)
        tmpa = kb.sb("qkv_tmpa", [128, 2 * T], BF16, st=pst)
        tmpb = kb.sb("qkv_tmpb", [128, 512], BF16, st=pst)
        bsq = kb.sb("qkv_bsq", [128, 512], BF16, st=pst)
        pp = [kb.ps("qkv_pp", [128, 512], F32, st=pst) for _ in range(4)]
        pperm = [kb.ps("qkv_pperm", [128, 512], F32, st=pst) for _ in range(2)]
        pbnd = [kb.ps("qkv_pbnd", [1, 512], F32, st=pst) for _ in range(2)]
        if wait_ln:
            ph.wait("gpsimd", kb.sems["v"], wait_ln[-1])
        wsbs = []
        wticks = []
        for pi, pr in enumerate(projs):
            w = kb.sb("qkv_wp", [128, DT * M_out], BF16, st=pst)
            wsbs.append(w)
            wt = None
            for k in range(DT):
                wt = ph.do("gpsimd", lambda e, k=k, pr=pr, w=w: e.dma_start(
                    w[:, k * M_out:(k + 1) * M_out],
                    pr["w"][k * 128:(k + 1) * 128, :]), dg, 16)
            wticks.append(wt)

        out_ticks = {}
        gidx_dma = {}
        gi = 0
        bank_gate = {}
        gt = 0
        prev_elu_t3 = None
        bnd_hist = []
        sq_prev = None
        tmpb_prev = None
        for pi, pr in enumerate(projs):
            wsb = wsbs[pi]
            for o in range(DT):
                par = gt % 2
                if gt >= 2:
                    prev = gidx_dma[gt - 2]
                    ph.wait("scalar", dqo[par], prev)
                    ph.wait("vector", dqo[par], prev)
                evac_ticks = []
                for c in range(kb.TC):
                    bi = gi % 4
                    if gi >= 4:
                        s_, t_ = bank_gate[gi - 4]
                        ph.wait("tensor", s_, t_)
                    if gi == 0:
                        for wl in wait_ln:
                            ph.wait("tensor", kb.sems["v"], wl)
                    if c == 0:
                        ph.wait("tensor", dg, wticks[-1])
                    for k in range(DT):
                        mt = ph.do("tensor", lambda e, k=k, o=o, c=c, bi=bi,
                                   wsb=wsb: e.matmul(
                            pp[bi][:],
                            wsb[:, k * M_out + o * 128:k * M_out + (o + 1) * 128],
                            xn[:, k * T + c * 512:k * T + (c + 1) * 512],
                            start=(k == 0), stop=(k == DT - 1)),
                            stt if k == DT - 1 else None)
                    ph.wait("scalar", stt, mt)
                    bias_ap = pr.get("bias")
                    if bias_ap is not None:
                        ev = ph.do("scalar", lambda e, c=c, par=par, bi=bi, o=o,
                                   bias_ap=bias_ap: e.activation(
                            praw[:, par * T + c * 512:par * T + (c + 1) * 512],
                            pp[bi][:], AF.Identity, bias=bias_ap[:, o:o + 1]), ss, 1)
                    else:
                        ev = ph.do("scalar", lambda e, c=c, par=par, bi=bi:
                                   e.activation(
                            praw[:, par * T + c * 512:par * T + (c + 1) * 512],
                            pp[bi][:], AF.Identity), ss, 1)
                    bank_gate[gi] = (ss, ev)
                    evac_ticks.append(ev)
                    gi += 1
                post = pr.get("post")
                src_buf, src_off = praw, par * T
                if post == "elu1_ropeI":
                    ph.wait("vector", ss, evac_ticks[-1])
                    if prev_elu_t3 is not None:
                        ph.wait("vector", sv, prev_elu_t3)
                    t1 = ph.do("vector", lambda e, par=par: e.tensor_single_scalar(
                        tmpa[:, 0:T], praw[:, par * T:par * T + T], 0.0,
                        ALU.min), sv, 1)
                    ph.wait("scalar", sv, t1)
                    t2 = ph.do("scalar", lambda e: e.activation(
                        tmpa[:, T:2 * T], tmpa[:, 0:T], AF.Exp), ss, 1)
                    ph.wait("scalar", ss, t2)
                    t2b = ph.do("scalar", lambda e, par=par: e.activation(
                        tmpa[:, 0:T], praw[:, par * T:par * T + T], AF.Relu),
                        ss, 1)
                    ph.wait("vector", ss, t2b)
                    t3 = ph.do("vector", lambda e, par=par: e.tensor_tensor(
                        praw[:, par * T:par * T + T], tmpa[:, 0:T],
                        tmpa[:, T:2 * T], ALU.add), sv, 1)
                    prev_elu_t3 = t3
                    ph.wait("tensor", sv, t3)
                    ph.wait("vector", sv, t3)
                if post in ("rope0", "ropeI", "elu1_ropeI"):
                    rope_ticks = {}
                    pmts = {}
                    for c in range(kb.TC):
                        pi2 = c % 2
                        ph.wait("tensor", ss, evac_ticks[c])
                        if c >= 2:
                            ph.wait("tensor", sv, rope_ticks[c - 2])
                        pmt = ph.do("tensor", lambda e, c=c, pi2=pi2, par=par,
                                    permM=permM: e.matmul(
                            pperm[pi2][:], permM[:],
                            praw[:, par * T + c * 512:par * T + (c + 1) * 512],
                            start=True, stop=True), stt, 1)
                        pmts[c] = pmt
                        ph.wait("vector", ss, evac_ticks[c])
                        t1 = ph.do("vector", lambda e, c=c, par=par, cosT=cosT:
                                   e.tensor_tensor(
                            tmpa[:, par * T + c * 512:par * T + (c + 1) * 512],
                            praw[:, par * T + c * 512:par * T + (c + 1) * 512],
                            cosT[:, c * 512:(c + 1) * 512], ALU.mult), sv, 1)
                        ph.wait("vector", stt, pmt)
                        if tmpb_prev is not None:
                            ph.wait("vector", sv, tmpb_prev)
                        t2 = ph.do("vector", lambda e, c=c, pi2=pi2, sinT=sinT:
                                   e.tensor_tensor(
                            tmpb[:], pperm[pi2][:],
                            sinT[:, c * 512:(c + 1) * 512], ALU.mult), sv, 1)
                        ph.wait("vector", sv, t2)
                        t3 = ph.do("vector", lambda e, c=c, par=par:
                                   e.tensor_tensor(
                            pout[:, par * T + c * 512:par * T + (c + 1) * 512],
                            tmpa[:, par * T + c * 512:par * T + (c + 1) * 512],
                            tmpb[:], ALU.add), sv, 1)
                        rope_ticks[c] = t3
                        tmpb_prev = t3
                    src_buf, src_off = pout, par * T
                    done_sem, done_tick = sv, rope_ticks[kb.TC - 1]
                elif post == "elu1":
                    ph.wait("vector", ss, evac_ticks[-1])
                    if prev_elu_t3 is not None:
                        ph.wait("vector", sv, prev_elu_t3)
                    t1 = ph.do("vector", lambda e, par=par: e.tensor_single_scalar(
                        tmpa[:, 0:T], praw[:, par * T:par * T + T], 0.0,
                        ALU.min), sv, 1)
                    ph.wait("scalar", sv, t1)
                    t2 = ph.do("scalar", lambda e: e.activation(
                        tmpa[:, T:2 * T], tmpa[:, 0:T], AF.Exp), ss, 1)
                    ph.wait("scalar", ss, t2)
                    t2b = ph.do("scalar", lambda e, par=par: e.activation(
                        tmpa[:, 0:T], praw[:, par * T:par * T + T], AF.Relu),
                        ss, 1)
                    ph.wait("vector", ss, t2b)
                    t3 = ph.do("vector", lambda e, par=par: e.tensor_tensor(
                        pout[:, par * T:par * T + T], tmpa[:, 0:T],
                        tmpa[:, T:2 * T], ALU.add), sv, 1)
                    prev_elu_t3 = t3
                    src_buf, src_off = pout, par * T
                    done_sem, done_tick = sv, t3
                else:
                    done_sem, done_tick = ss, evac_ticks[-1]
                if pr.get("bound") is not None:
                    srow, col0 = pr["bound"]
                    for c in range(kb.TC):
                        ph.wait("scalar", done_sem, done_tick)
                        if sq_prev is not None:
                            ph.wait("scalar", stt, sq_prev)
                        sq = ph.do("scalar", lambda e, c=c, src_buf=src_buf,
                                   src_off=src_off: e.activation(
                            bsq[:], src_buf[:, src_off + c * 512:
                                            src_off + (c + 1) * 512],
                            AF.Square), ss, 1)
                        for side in range(2):
                            bb = (c * 2 + side) % 2
                            ph.wait("tensor", ss, sq)
                            if len(bnd_hist) >= 2:
                                s_, t_ = bnd_hist[-2]
                                ph.wait("tensor", s_, t_)
                            tp = (0, 0) if side == 0 else (64, 0)
                            bm = ph.do("tensor", lambda e, side=side, bb=bb,
                                       tp=tp: e.matmul(
                                pbnd[bb][:],
                                cst.ones[side * 64:(side + 1) * 64, 0:1],
                                bsq[side * 64:(side + 1) * 64, :],
                                start=True, stop=True, tile_position=tp),
                                stt, 1)
                            sq_prev = bm
                            ph.wait("vector", stt, bm)
                            cell = col0 + (o * 2 + side) * kb.TC + c
                            bt = ph.do("vector", lambda e, bb=bb, cell=cell,
                                       srow=srow: e.tensor_reduce(
                                srow[:, cell:cell + 1], pbnd[bb][:], AX,
                                ALU.max), sv, 1)
                            bnd_hist.append((sv, bt))
                    done_sem, done_tick = sv, bt
                ph.wait("sync", done_sem, done_tick)
                ot = ph.do("sync", lambda e, o=o, pr=pr, src_buf=src_buf,
                           src_off=src_off: e.dma_start(
                    pr["dst"][o * 128:(o + 1) * 128, :],
                    src_buf[:, src_off:src_off + T]), dqo[par], 16)
                out_ticks[(pr["name"], o)] = (dqo[par], ot)
                gidx_dma[gt] = ot
                gt += 1
        ph.emit()
    return out_ticks


SCALE_ATTN = 0.125


def emit_bounds_final(kb, cst, ph, srow, bias_tiles, scale):
    TC = kb.TC
    sv = kb.sem("v")
    ss = kb.sem("s")
    stt = kb.sem("t")
    t = None
    for h in range(16):
        t = ph.do("vector", lambda e, h=h: e.tensor_reduce(
            srow[:, 256 + h:256 + h + 1], srow[:, h * TC:(h + 1) * TC], AX,
            ALU.max), sv, 1)
        t = ph.do("vector", lambda e, h=h: e.tensor_reduce(
            srow[:, 288 + h:288 + h + 1],
            srow[:, 16 * TC + h * TC:16 * TC + (h + 1) * TC], AX,
            ALU.max), sv, 1)
    ph.wait("vector", sv, t)
    t = ph.do("vector", lambda e: e.tensor_tensor(
        srow[:, 320:336], srow[:, 256:272], srow[:, 288:304], ALU.mult), sv, 1)
    ph.wait("scalar", sv, t)
    t2 = ph.do("scalar", lambda e: e.activation(
        srow[:, 336:352], srow[:, 320:336], AF.Sqrt), ss, 1)
    ph.wait("vector", ss, t2)
    t3 = ph.do("vector", lambda e: e.tensor_single_scalar(
        srow[:, 352:368], srow[:, 336:352], -float(scale), ALU.mult), sv, 1)
    ph.wait("vector", sv, t3)
    with contextlib.ExitStack() as pst:
        rbf = kb.sb("bnd_rbf", [1, 16], BF16, st=pst)
        pb = kb.ps("bnd_pb", [128, 16], F32, st=pst)
        t4 = ph.do("vector", lambda e: e.tensor_copy(rbf[:], srow[:, 352:368]),
                   sv, 1)
        ph.wait("tensor", sv, t4)
        t5 = ph.do("tensor", lambda e: e.matmul(
            pb[:], cst.ones[0:1, :], rbf[:], start=True, stop=True), stt, 1)
        ph.wait("vector", stt, t5)
        t6 = ph.do("vector", lambda e: e.tensor_copy(bias_tiles[:], pb[:]),
                   sv, 1)
        ph.emit()
    return t6


def emit_attn_core(kb, cst, ph, scrQ, scrK, scrV, scrO, bias_tiles, bias_rdy,
                   scale, wait_first):
    T = kb.T
    NKT = T // 128
    NG = NKT // 2
    dq = kb.sem("daq")
    dk = kb.sem("dak")
    dv = kb.sem("dav")
    dot = kb.sem("dot")
    stt = kb.sem("t")
    ss = kb.sem("s")
    sv = kb.sem("v")
    with contextlib.ExitStack() as pst:
        qt = kb.sb("at_qt", [128, T], BF16, st=pst)
        kt_ = kb.sb("at_kt", [128, T], BF16, st=pst)
        vt = kb.sb("at_vt", [128, T], BF16, st=pst) if scrV is not scrK else None
        VA = kb.sb("at_VA", [128, NKT * 65], BF16, st=pst)
        VB = kb.sb("at_VB", [128, NKT * 65], BF16, st=pst)
        P = kb.sb("at_P", [128, NKT * 1024], BF16, st=pst)
        bp = kb.sb("at_bp", [128, 8], F32, st=pst)
        OT = kb.sb("at_OT", [128, T], BF16, st=pst)
        rrec = kb.sb("at_rrec", [1, 2 * 512], F32, st=pst)
        rbf = kb.sb("at_rbf", [1, 2 * 512], BF16, st=pst)
        rnbc = kb.sb("at_rnbc", [64, 2 * 512], BF16, st=pst)
        otmp = kb.sb("at_otmp", [64, 2 * 512], BF16, st=pst)
        ps2 = [kb.ps("at_ps%d" % p_, [128, 1024], F32, st=pst)
               for p_ in range(2)]
        psOA = kb.ps("at_psOA", [65, 512], F32, st=pst)
        psOB = kb.ps("at_psOB", [65, 512], F32, st=pst)
        pnrm = kb.ps("at_pnrm", [64, 512], F32, st=pst)
        ptr = kb.ps("at_ptr", [128, 128], BF16, st=pst)

        for eng in ("sync", "tensor", "scalar", "vector"):
            for s_, t_ in wait_first:
                ph.wait(eng, s_, t_)
        ph.wait("scalar", kb.sems["v"], bias_rdy)
        ph.wait("vector", kb.sems["v"], bias_rdy)
        bt3 = bias_tiles.rearrange("p (a b) -> p a b", b=2)
        bp_t = ph.do("vector", lambda e: e.tensor_tensor(
            bp[:], bt3[:, :, 0], bt3[:, :, 1], ALU.min), kb.sems["v"], 1)
        ph.wait("scalar", kb.sems["v"], bp_t)

        prev_last_mm = None
        prev_ot_dma = None
        prev_vready = None
        prev_evt = {0: None, 1: None}
        prev_ot2 = {0: None, 1: None}
        last_ot_tick = None
        for hp in range(NHP):
            hA, hB = 2 * hp, 2 * hp + 1
            if prev_last_mm is not None:
                ph.wait("sync", stt, prev_last_mm)
                ph.wait("vector", stt, prev_last_mm)
                ph.wait("scalar", stt, prev_last_mm)
                ph.wait("tensor", sv, prev_vready)
                ph.wait("vector", dot, prev_ot_dma)
            lq = ph.do("sync", lambda e, hp=hp: e.dma_start(
                qt[:], scrQ[hp * 128:(hp + 1) * 128, :]), dq, 16)
            lk = ph.do("sync", lambda e, hp=hp: e.dma_start(
                kt_[:], scrK[hp * 128:(hp + 1) * 128, :]), dk, 16)
            if scrV is not scrK:
                lv = ph.do("sync", lambda e, hp=hp: e.dma_start(
                    vt[:], scrV[hp * 128:(hp + 1) * 128, :]), dv, 16)
                lvs = dv
                vt_use = vt
            else:
                lv = lk
                lvs = dk
                vt_use = kt_
            msA = ph.do("vector", lambda e: e.memset(VA[:], 1.0), sv, 1)
            msB = ph.do("vector", lambda e: e.memset(VB[:], 1.0), sv, 1)
            ph.wait("tensor", lvs, lv)
            ev_ticks = {}
            for kt in range(NKT):
                if kt >= 1:
                    ph.wait("tensor", sv, ev_ticks[kt - 1])
                trt = ph.do("tensor", lambda e, kt=kt, vt_use=vt_use:
                            e.transpose(
                    ptr[:], vt_use[:, kt * 128:(kt + 1) * 128], cst.ident[:]),
                    stt, 1)
                ph.wait("vector", stt, trt)
                if kt == 0:
                    ph.wait("vector", sv, msB)
                ph.do("vector", lambda e, kt=kt: e.tensor_copy(
                    VA[:, kt * 65:kt * 65 + 64], ptr[:, 0:64]), sv, 1)
                ev_ticks[kt] = ph.do("vector", lambda e, kt=kt: e.tensor_copy(
                    VB[:, kt * 65:kt * 65 + 64], ptr[:, 64:128]), sv, 1)
            vready = ev_ticks[NKT - 1]
            prev_vready = vready
            ph.wait("tensor", dq, lq)
            ph.wait("tensor", dk, lk)
            ph.wait("tensor", sv, vready)
            pv_last_qb = None
            ev_qb = None
            prev_exp = None
            prev_pv = None
            ot2_ticks = []
            for qb in range(kb.TC):
                exp_ticks = {}
                pv_ticks = {}

                def emit_pv(kt):
                    for side in range(2):
                        VX = VA if side == 0 else VB
                        psO = psOA if side == 0 else psOB
                        ph.wait("tensor", ss, exp_ticks[kt])
                        if ev_qb is not None and kt == 0:
                            for s_, t_ in ev_qb:
                                ph.wait("tensor", s_, t_)
                        pv_ticks[(side, kt)] = ph.do(
                            "tensor", lambda e, kt=kt, VX=VX, side=side,
                            psO=psO: e.matmul(
                                psO[:], VX[:, kt * 65:(kt + 1) * 65],
                                P[:, kt * 1024 + side * 512:
                                  kt * 1024 + side * 512 + 512],
                                start=(kt == 0), stop=(kt == NKT - 1),
                                skip_group_check=True),
                            stt, 1)

                for kt in range(NKT):
                    par = kt % 2
                    sc_t = {}
                    for side in range(2):
                        tp = (0, 0) if side == 0 else (64, 0)
                        p0 = side * 64
                        if kt >= 2:
                            ph.wait("tensor", ss, exp_ticks[kt - 2])
                        elif prev_exp is not None:
                            ph.wait("tensor", ss, prev_exp[NKT - 2 + kt])
                        sc_t[side] = ph.do(
                            "tensor", lambda e, kt=kt, par=par, tp=tp,
                            p0=p0, side=side, qb=qb: e.matmul(
                                ps2[par][:, side * 512:side * 512 + 512],
                                kt_[p0:p0 + 64, kt * 128:(kt + 1) * 128],
                                qt[p0:p0 + 64, qb * 512:(qb + 1) * 512],
                                start=True, stop=True, tile_position=tp),
                            stt, 1)
                    if kt >= 1:
                        emit_pv(kt - 1)
                    ph.wait("scalar", stt, sc_t[1])
                    if prev_pv is not None:
                        ph.wait("scalar", stt, prev_pv[(1, kt)])
                    exp_ticks[kt] = ph.do(
                        "scalar", lambda e, kt=kt, par=par, hp=hp:
                        e.activation(
                            P[:, kt * 1024:(kt + 1) * 1024],
                            ps2[par][:], AF.Exp, scale=float(scale),
                            bias=bp[:, hp:hp + 1]), ss, 1)
                emit_pv(NKT - 1)
                prev_exp = exp_ticks
                prev_pv = pv_ticks
                pv_last_qb = max(pv_ticks[(0, NKT - 1)], pv_ticks[(1, NKT - 1)])
                ev_qb = []
                for side in range(2):
                    psO = psOA if side == 0 else psOB
                    p0 = side * 64
                    so = side * 512
                    ph.wait("vector", stt, pv_ticks[(side, NKT - 1)])
                    ph.wait("scalar", stt, pv_ticks[(side, NKT - 1)])
                    rt = ph.do("scalar", lambda e, psO=psO, so=so:
                               e.activation(rrec[:, so:so + 512],
                                            psO[64:65, :], AF.Ln), ss, 1)
                    ph.wait("scalar", ss, rt)
                    ct = ph.do("scalar", lambda e, so=so: e.activation(
                        rbf[:, so:so + 512], rrec[:, so:so + 512],
                        AF.Exp, scale=-1.0), ss, 1)
                    ph.wait("tensor", ss, ct)
                    if prev_evt[side] is not None:
                        ph.wait("tensor", sv, prev_evt[side])
                    if prev_evt[1 - side] is not None:
                        ph.wait("tensor", sv, prev_evt[1 - side])
                    bct = ph.do("tensor", lambda e, so=so: e.matmul(
                        pnrm[:], cst.ones[0:1, 0:64], rbf[:, so:so + 512],
                        start=True, stop=True), stt, 1)
                    ph.wait("vector", stt, bct)
                    if prev_ot2[side] is not None:
                        ph.wait("vector", sv, prev_ot2[side])
                    evt = ph.do("vector", lambda e, so=so: e.tensor_copy(
                        rnbc[:, so:so + 512], pnrm[:]), sv, 1)
                    prev_evt[side] = evt
                    o1 = ph.do("vector", lambda e, psO=psO, so=so:
                               e.tensor_copy(otmp[:, so:so + 512],
                                             psO[0:64, :]), sv, 1)
                    ph.wait("vector", sv, o1)
                    o2 = ph.do("vector", lambda e, p0=p0, so=so, qb=qb:
                               e.tensor_tensor(
                        OT[p0:p0 + 64, qb * 512:(qb + 1) * 512],
                        otmp[:, so:so + 512], rnbc[:, so:so + 512],
                        ALU.mult), sv, 1)
                    prev_ot2[side] = o2
                    ot2_ticks.append(o2)
                    ev_qb.append((sv, o1))
                    ev_qb.append((ss, rt))
            ph.wait("sync", sv, ot2_ticks[-1])
            last_ot_tick = ph.do("sync", lambda e, hp=hp: e.dma_start(
                scrO[hp * 128:(hp + 1) * 128, :], OT[:]), dot, 16)
            prev_ot_dma = last_ot_tick
            prev_last_mm = pv_last_qb
        ph.emit()
    return (dot, last_ot_tick), (stt, prev_last_mm)


def emit_outproj_residual(kb, cst, ph, srcO, w_dram, bias_sb, xin, xout,
                          wait_first, scale=None, extra_add=None,
                          xin_bf=False):
    T = kb.T
    dso = kb.sem("dopo")
    dg = kb.sem("dopw")
    if xin_bf:
        dxi = [kb.sem("gxi0"), kb.sem("gxi1")]
    else:
        dxi = [kb.sem("dxi0"), kb.sem("dxi1")]
    dxi2 = [kb.sem("dxj0"), kb.sem("dxj1")]
    dxo = [kb.sem("dxo0"), kb.sem("dxo1")]
    stt = kb.sem("t")
    ss = kb.sem("s")
    sv = kb.sem("v")
    M_out = D
    with contextlib.ExitStack() as pst:
        ot = kb.sb("op_ot", [128, DT * T], BF16, st=pst)
        wsb = kb.sb("op_w", [128, DT * M_out], BF16, st=pst)
        xi = kb.sb("op_xi", [128, 2 * T], F32, st=pst)
        xi2 = kb.sb("op_xi2", [128, 2 * T], F32, st=pst) if extra_add else None
        xo = kb.sb("op_xo", [128, 2 * T], F32, st=pst)
        tf = kb.sb("op_tf", [128, 2 * 512], F32, st=pst)
        pp = [kb.ps("op_pp", [128, 512], F32, st=pst) for _ in range(4)]
        for eng in ("sync", "gpsimd", "tensor", "scalar", "vector"):
            for s_, t_ in wait_first:
                ph.wait(eng, s_, t_)
        lo = []
        for k in range(DT):
            lo.append(ph.do("sync", lambda e, k=k: e.dma_start(
                ot[:, k * T:(k + 1) * T], srcO[k * 128:(k + 1) * 128, :]),
                dso, 16))
        wt = None
        for k in range(DT):
            wt = ph.do("gpsimd", lambda e, k=k: e.dma_start(
                wsb[:, k * M_out:(k + 1) * M_out],
                w_dram[k * 128:(k + 1) * 128, :]), dg, 16)
        xi_t = {}
        xi2_t = {}
        xo_dma = {}
        tt_last = {}
        gi = 0
        bank_gate = {}
        tf_gate = {}
        for o in range(DT):
            par = o % 2
            if o >= 2:
                ph.wait("sync", sv, tt_last[o - 2])
                if xin_bf:
                    ph.wait("gpsimd", sv, tt_last[o - 2])
            xiq = "gpsimd" if xin_bf else "sync"
            xi_t[o] = ph.do(xiq, lambda e, o=o, par=par: e.dma_start(
                xi[:, par * T:(par + 1) * T], xin[o * 128:(o + 1) * 128, :]),
                dxi[par], 16)
            if extra_add is not None:
                xi2_t[o] = ph.do("sync", lambda e, o=o, par=par: e.dma_start(
                    xi2[:, par * T:(par + 1) * T],
                    extra_add[o * 128:(o + 1) * 128, :]), dxi2[par], 16)
            for c in range(kb.TC):
                bi = gi % 4
                if gi >= 4:
                    s_, t_ = bank_gate[gi - 4]
                    ph.wait("tensor", s_, t_)
                if gi == 0:
                    ph.wait("tensor", dg, wt)
                    ph.wait("tensor", dso, lo[-1])
                for k in range(DT):
                    mt = ph.do("tensor", lambda e, k=k, o=o, c=c, bi=bi:
                               e.matmul(
                        pp[bi][:],
                        wsb[:, k * M_out + o * 128:k * M_out + (o + 1) * 128],
                        ot[:, k * T + c * 512:k * T + (c + 1) * 512],
                        start=(k == 0), stop=(k == DT - 1)),
                        stt if k == DT - 1 else None)
                ph.wait("scalar", stt, mt)
                if gi >= 2:
                    ph.wait("scalar", sv, tf_gate[gi - 2])
                kw = {}
                if bias_sb is not None:
                    kw["bias"] = bias_sb[:, o:o + 1]
                if scale is not None:
                    kw["scale"] = float(scale)
                ev = ph.do("scalar", lambda e, bi=bi, gi=gi, kw=kw: e.activation(
                    tf[:, (gi % 2) * 512:(gi % 2 + 1) * 512], pp[bi][:],
                    AF.Identity, **kw), ss, 1)
                bank_gate[gi] = (ss, ev)
                ph.wait("vector", ss, ev)
                ph.wait("vector", dxi[par], xi_t[o])
                if extra_add is not None:
                    ph.wait("vector", dxi2[par], xi2_t[o])
                if o >= 2 and c == 0:
                    ph.wait("vector", dxo[par], xo_dma[o - 2])
                if extra_add is None:
                    tt = ph.do("vector", lambda e, c=c, par=par, gi=gi: e.tensor_tensor(
                        xo[:, par * T + c * 512:par * T + (c + 1) * 512],
                        tf[:, (gi % 2) * 512:(gi % 2 + 1) * 512],
                        xi[:, par * T + c * 512:par * T + (c + 1) * 512],
                        ALU.add), sv, 1)
                else:
                    tt = ph.do("vector", lambda e, c=c, par=par, gi=gi: e.tensor_tensor(
                        xo[:, par * T + c * 512:par * T + (c + 1) * 512],
                        tf[:, (gi % 2) * 512:(gi % 2 + 1) * 512],
                        xi[:, par * T + c * 512:par * T + (c + 1) * 512],
                        ALU.add), sv, 1)
                    ph.wait("vector", sv, tt)
                    tt = ph.do("vector", lambda e, c=c, par=par: e.tensor_tensor(
                        xo[:, par * T + c * 512:par * T + (c + 1) * 512],
                        xo[:, par * T + c * 512:par * T + (c + 1) * 512],
                        xi2[:, par * T + c * 512:par * T + (c + 1) * 512],
                        ALU.add), sv, 1)
                tf_gate[gi] = tt
                gi += 1
            tt_last[o] = tt
            ph.wait("sync", sv, tt)
            xo_dma[o] = ph.do("sync", lambda e, o=o, par=par: e.dma_start(
                xout[o * 128:(o + 1) * 128, :], xo[:, par * T:(par + 1) * T]),
                dxo[par], 16)
        ph.emit()
    return [(dxo[0], xo_dma[DT - 2]), (dxo[1], xo_dma[DT - 1])]


L0A_CONSTS = ["ident", "ones", "cos0", "sin0", "perm0"]


def decl_l0_attn(nc, T, px=""):
    w = {}
    w["wq"] = nc.dram_tensor(px + "wq", [D, D], F32, kind="ExternalInput").ap()
    w["wkv"] = nc.dram_tensor(px + "wkv", [D, D], F32, kind="ExternalInput").ap()
    w["wout"] = nc.dram_tensor(px + "wout", [D, D], F32,
                               kind="ExternalInput").ap()
    w["bout_t"] = nc.dram_tensor(px + "bout_t", [128, DT], F32,
                                 kind="ExternalInput").ap()
    return w


def sub_l0_attn(kb, cst, caps, w, scr, xin, xout):
    T = kb.T
    ph = Ph(kb)
    ds = kb.sem("dc")
    scrQ, scrK, scrO = scr["q"], scr["k"], scr["o"]
    wq, wkv, wout = w["wq"], w["wkv"], w["wout"]
    with contextlib.ExitStack() as st0:
        srow = kb.sb("bstats", [1, 640], F32, st=st0)
        bias_tiles = kb.sb("battn", [128, 16], F32, st=st0)
        bout_sb = kb.sb("bout", [128, DT], F32, st=st0)
        t_b = ph.do("sync", lambda e: e.dma_start(bout_sb[:], w["bout_t"][:]),
                    ds, 16)
        for eng in ("tensor", "scalar", "vector", "gpsimd"):
            ph.wait(eng, ds, t_b)
        with kb.nc.named_scope("lnqkv"), contextlib.ExitStack() as s1:
            xn = kb.sb("xn", [128, DT * T], BF16, st=s1)
            ln_ticks = emit_ln(kb, ph, s1, xin, xn, cst)
            qkv_ticks = emit_qkv(kb, cst, ph, s1, xn, [
                dict(name="q", w=wq, post="rope0", dst=scrQ,
                     bound=(srow, 0)),
                dict(name="kv", w=wkv, post="rope0", dst=scrK,
                     bound=(srow, 16 * kb.TC)),
            ], wait_ln=ln_ticks)
        bias_rdy = emit_bounds_final(kb, cst, ph, srow, bias_tiles, SCALE_ATTN)
        wfm = {}
        for s_, t_ in qkv_ticks.values():
            wfm[s_] = max(wfm.get(s_, 0), t_)
        with kb.nc.named_scope("core"):
            (o_sem, o_tick), _ = emit_attn_core(
                kb, cst, ph, scrQ, scrK, scrK, scrO, bias_tiles, bias_rdy,
                SCALE_ATTN, wait_first=list(wfm.items()))
        fin = emit_outproj_residual(kb, cst, ph, scrO, wout, bout_sb, xin,
                                    xout, wait_first=[(o_sem, o_tick)])
    return fin


SCR_SHAPES = {
    "q": ([D, "T"], BF16), "k": ([D, "T"], BF16), "v": ([D, "T"], BF16),
    "o": ([D, "T"], BF16), "s": ([D, "T"], BF16), "s2": ([D, "T"], F32),
    "a": (["NTB", 4096, "TB"], BF16),
    "xz": ([4096, "T"], BF16), "xc": ([2048, "T"], BF16),
    "dts": ([2048, "T"], BF16), "yz": ([2048, "T"], BF16),
    "bc": ([32, "T"], BF16),
    "w1q": ([D, 4096], BF16), "w2q": ([4096, D], BF16),
}


def make_scratch(nc, T, names):
    scr = {}
    for n in names:
        shp, dt_ = SCR_SHAPES[n]
        shp = [T if s == "T" else
               (max(1, T // 1024) if s == "NTB" else
                (min(1024, T) if s == "TB" else s)) for s in shp]
        scr[n] = nc.dram_tensor("scr_" + n, shp, dt_, kind="Internal").ap()
    return scr


def standalone(decl, sub, cnames, T, scr_names):
    nc = bass.Bass("TRN2", target_bir_lowering=False)
    xin = nc.dram_tensor("xin", [D, T], F32, kind="ExternalInput").ap()
    xout = nc.dram_tensor("xout", [D, T], F32, kind="ExternalOutput").ap()
    w = decl(nc, T, "")
    caps = declare_constants(nc, T, cnames)
    nc.const_names = cnames
    scr = make_scratch(nc, T, scr_names)
    with contextlib.ExitStack() as st:
        kb = KB(nc, st, T)
        cst = Cst(kb, caps, resident=["ident", "ones"])
        ph = Ph(kb)
        cld = cst.load(kb, ph)
        cldm = {}
        for s_, t_ in cld:
            cldm[s_] = max(cldm.get(s_, 0), t_)
        for eng in ENGS:
            for s_, t_ in cldm.items():
                ph.wait(eng, s_, t_)
        ph.emit()
        fin = sub(kb, cst, caps, w, scr, xin, xout)
        phz = Ph(kb)
        for s_, t_ in fin:
            phz.wait("sync", s_, t_)
        phz.do("sync", lambda e: e.nop())
        phz.emit()
    return nc


def build_l0_attn(T):
    return standalone(decl_l0_attn, sub_l0_attn, L0A_CONSTS, T,
                      ["q", "k", "o"])


def emit_mm_big(kb, ph, name, rhs_slice, n_k, w_dram, M_out, n_c, evac,
                og_tiles=2, wmode="f32", wait_rhs=None, col0=0):
    stt = kb.sem("t")
    q = "gpsimd" if wmode in ("f32", "i32") else "sync"
    if q == "gpsimd":
        dwp = [kb.sem("dwm0"), kb.sem("dwm1")]
    else:
        dwp = [kb.sem("dwn0"), kb.sem("dwn1")]
    OG = og_tiles * 128
    n_og = M_out // OG
    nb = og_tiles * n_c
    assert nb <= 4
    nsets = 2
    with contextlib.ExitStack() as pst:
        wog = [kb.sb(f"w{name}", [128, n_k * OG], BF16, st=pst)
               for _ in range(2)]
        pp = [kb.ps(f"pp{name}", [128, 512], F32, st=pst)
              for _ in range(nb * nsets)]
        wr = w_dram.rearrange("(k p) m -> p k m", p=128)
        if wait_rhs:
            for s_, t_ in wait_rhs:
                ph.wait(q, s_, t_)
        mm_last = {}
        ev_hist = {}
        for og in range(n_og):
            par = og % 2
            pset = (og % nsets) * nb
            if og >= 2:
                ph.wait(q, stt, mm_last[og - 2])
            wt = ph.do(q, lambda e, og=og, par=par: e.dma_start(
                wog[par][:].rearrange("p (k m) -> p k m", k=n_k),
                wr[:, :, col0 + og * OG: col0 + (og + 1) * OG]),
                dwp[par], 16)
            ph.wait("tensor", dwp[par], wt)
            if og == 0 and wait_rhs:
                for s_, t_ in wait_rhs:
                    ph.wait("tensor", s_, t_)
            if og >= nsets:
                for s_, t_ in ev_hist[og - nsets]:
                    ph.wait("tensor", s_, t_)
            mmt = {}
            for k in range(n_k):
                for oi in range(og_tiles):
                    for c in range(n_c):
                        bi = pset + oi * n_c + c
                        mm = ph.do("tensor", lambda e, k=k, oi=oi, c=c, bi=bi,
                                   par=par: e.matmul(
                            pp[bi][:],
                            wog[par][:, k * OG + oi * 128:k * OG + (oi + 1) * 128],
                            rhs_slice(k, c),
                            start=(k == 0), stop=(k == n_k - 1)),
                            stt if k == n_k - 1 else None)
                        if k == n_k - 1:
                            mmt[(oi, c)] = mm
            mm_last[og] = mm
            evs = []
            for oi in range(og_tiles):
                for c in range(n_c):
                    bi = pset + oi * n_c + c
                    evs.append(evac(og * og_tiles + oi, c, pp[bi],
                                    (stt, mmt[(oi, c)])))
            ev_hist[og] = evs
        ph.emit()
    return ev_hist[n_og - 1]


def emit_dequant_u8(kb, cst, ph, name, q_dram, scales_dram, scrW, K_in, M,
                    selP, P):
    dqp = [kb.sem("dqq0"), kb.sem("dqq1")]
    dsc = kb.sem("dscq")
    dwo = [kb.sem("dwoq0"), kb.sem("dwoq1")]
    stt = kb.sem("t")
    ss = kb.sem("s")
    sv = kb.sem("v")
    NK = K_in // 128
    with contextlib.ExitStack() as pst:
        ssb = kb.sb(f"dq_s{name}", [P, M], BF16, st=pst)
        qb = [kb.sb(f"dq_q{name}", [128, M], BF16, st=pst) for _ in range(2)]
        sbc = kb.sb(f"dq_bc{name}", [128, M], BF16, st=pst)
        pb = [kb.ps(f"dq_pb{name}", [128, 512], F32, st=pst) for _ in range(2)]
        lt = ph.do("gpsimd", lambda e: e.dma_start(ssb[:], scales_dram[:]),
                   dsc, 16)
        ph.wait("tensor", dsc, lt)
        prev_tt = None
        prev_dma = {0: None, 1: None}
        for k in range(NK):
            par = k % 2
            if prev_dma[par] is not None:
                ph.wait("gpsimd", dwo[par], prev_dma[par])
            qt = ph.do("gpsimd", lambda e, k=k, par=par: e.dma_start(
                qb[par][:], q_dram[k * 128:(k + 1) * 128, :]), dqp[par], 16)
            ev_t = {}
            for c in range(M // 512):
                bi = c % 2
                if c >= 2:
                    ph.wait("tensor", sv, ev_t[c - 2])
                elif prev_tt is not None and c < 2:
                    ph.wait("tensor", sv, prev_tt)
                bm = ph.do("tensor", lambda e, k=k, c=c, bi=bi: e.matmul(
                    pb[bi][:], selP[:, k * 128:(k + 1) * 128],
                    ssb[:, c * 512:(c + 1) * 512], start=True, stop=True),
                    stt, 1)
                ph.wait("vector", stt, bm)
                ev_t[c] = ph.do("vector", lambda e, c=c, bi=bi: e.tensor_copy(
                    sbc[:, c * 512:(c + 1) * 512], pb[bi][:]), sv, 1)
            ph.wait("vector", dqp[par], qt)
            t1 = ph.do("vector", lambda e, par=par: e.tensor_single_scalar(
                qb[par][:], qb[par][:], -128.0, ALU.add), sv, 1)
            ph.wait("vector", sv, t1)
            tt = ph.do("vector", lambda e, par=par: e.tensor_tensor(
                qb[par][:], qb[par][:], sbc[:], ALU.mult), sv, 1)
            prev_tt = tt
            ph.wait("sync", sv, tt)
            prev_dma[par] = ph.do("sync", lambda e, k=k, par=par: e.dma_start(
                scrW[k * 128:(k + 1) * 128, :], qb[par][:]), dwo[par], 16)
        ph.emit()
    return [(dwo[0], prev_dma[0]), (dwo[1], prev_dma[1])]


def emit_ffn(kb, cst, ph, xn, ups, M_hid, down, xin, xout, wait_ln,
             extra_add=None, TBc=2, xin_bf=False, h_src=None):
    T = kb.T
    TBc = min(TBc, kb.TC)
    TB = TBc * 512
    NTB = T // TB
    NH = M_hid // 128
    stt = kb.sem("t")
    ss = kb.sem("s")
    sv = kb.sem("v")
    if xin_bf:
        dxi = [kb.sem("gxi0"), kb.sem("gxi1")]
    else:
        dxi = [kb.sem("fxi0"), kb.sem("fxi1")]
    dxj = [kb.sem("fxj0"), kb.sem("fxj1")]
    dxo = [kb.sem("fxo0"), kb.sem("fxo1")]
    with contextlib.ExitStack() as pst:
        h = kb.sb("ffn_h", [128, NH * TB], BF16, st=pst)
        tmp = kb.sb("ffn_tmp", [128, 2 * 512], BF16, st=pst)
        tf = kb.sb("ffn_tf", [128, 2 * 512], F32, st=pst)
        xi = kb.sb("ffn_xi", [128, 2 * TB], F32, st=pst)
        xi2 = kb.sb("ffn_xi2", [128, 2 * TB], F32, st=pst) if extra_add else None
        xo = kb.sb("ffn_xo", [128, 2 * TB], F32, st=pst)
        fin = []
        up_ev_prev = []
        xi_t = {}
        xi2_t = {}
        xo_dma = {0: None, 1: None}
        h_guard = []
        tmp_guard = {0: None, 1: None}
        tf_guard = {0: None, 1: None}
        xo_guard = {}
        dhs = kb.sem("fhs")
        for tb in range(NTB):
            tpar = tb % 2
            if h_src is not None:
                if tb > 0:
                    for s_, t_ in h_guard:
                        ph.wait("sync", s_, t_)
                else:
                    for s_, t_ in (wait_ln or []):
                        ph.wait("sync", s_, t_)
                ht = None
                for k in range(NH):
                    ht = ph.do("sync", lambda e, k=k, tb=tb: e.dma_start(
                        h[:, k * TB:(k + 1) * TB],
                        h_src[k * 128:(k + 1) * 128, tb * TB:(tb + 1) * TB]),
                        dhs, 16)
                up_ev_prev = [(dhs, ht)] + (h_guard if tb > 0 else [])
                h_guard = []
            for ui, up in enumerate(ups):
                def mk_evac(up=up, ui=ui, tb=tb):
                    def evac(o, c, psum, mmw):
                        ph.wait("scalar", mmw[0], mmw[1])
                        kw = {}
                        if up.get("scale_ap") is not None:
                            kw["scale"] = up["scale_ap"][:, o:o + 1]
                        if up.get("bias_ap") is not None:
                            kw["bias"] = up["bias_ap"][:, o:o + 1]
                        fn = up.get("act") or AF.Identity
                        if up.get("combine") == "mult":
                            par2 = c % 2
                            if tmp_guard[par2] is not None:
                                ph.wait("scalar", sv, tmp_guard[par2])
                            et = ph.do("scalar", lambda e: e.activation(
                                tmp[:, par2 * 512:(par2 + 1) * 512], psum[:],
                                fn, **kw), ss, 1)
                            ph.wait("vector", ss, et)
                            vt = ph.do("vector", lambda e: e.tensor_tensor(
                                h[:, o * TB + c * 512:o * TB + (c + 1) * 512],
                                h[:, o * TB + c * 512:o * TB + (c + 1) * 512],
                                tmp[:, par2 * 512:(par2 + 1) * 512],
                                ALU.mult), sv, 1)
                            tmp_guard[par2] = vt
                            return (sv, vt)
                        else:
                            et = ph.do("scalar", lambda e: e.activation(
                                h[:, o * TB + c * 512:o * TB + (c + 1) * 512],
                                psum[:], fn, **kw), ss, 1)
                            return (ss, et)
                    return evac
                wr = wait_ln if (tb == 0 and ui == 0) else None
                wr = (wr or []) + (h_guard if ui == 0 else list(up_ev_prev))
                ev = emit_mm_big(
                    kb, ph, f"u{ui}", 
                    lambda k, c, tb=tb: xn[:, k * T + tb * TB + c * 512:
                                           k * T + tb * TB + (c + 1) * 512],
                    DT, up["w"], M_hid, TBc, mk_evac(),
                    og_tiles=2, wmode=up.get("wmode", "f32"), wait_rhs=wr)
                up_ev_prev = ev
            h_guard = []
            dn_ev = []

            def devac(o, c, psum, mmw, tb=tb, tpar=tpar):
                ph.wait("scalar", mmw[0], mmw[1])
                gi2 = (o * TBc + c) % 2
                kw = {}
                if down.get("bias_ap") is not None:
                    kw["bias"] = down["bias_ap"][:, o:o + 1]
                if down.get("scale") is not None:
                    kw["scale"] = float(down["scale"])
                if tf_guard[gi2] is not None:
                    ph.wait("scalar", sv, tf_guard[gi2])
                et = ph.do("scalar", lambda e: e.activation(
                    tf[:, gi2 * 512:(gi2 + 1) * 512], psum[:], AF.Identity,
                    **kw), ss, 1)
                if c == 0:
                    xiq = "gpsimd" if xin_bf else "sync"
                    if xo_dma[o % 2] is not None:
                        ph.wait(xiq, dxo[o % 2], xo_dma[o % 2])
                    xi_t[(tb, o)] = ph.do(
                        xiq, lambda e: e.dma_start(
                            xi[:, (o % 2) * TB:(o % 2 + 1) * TB],
                            xin[o * 128:(o + 1) * 128, tb * TB:(tb + 1) * TB]),
                        dxi[o % 2], 16)
                    if extra_add is not None:
                        xi2_t[(tb, o)] = ph.do(
                            "sync", lambda e: e.dma_start(
                                xi2[:, (o % 2) * TB:(o % 2 + 1) * TB],
                                extra_add[o * 128:(o + 1) * 128,
                                          tb * TB:(tb + 1) * TB]),
                            dxj[o % 2], 16)
                ph.wait("vector", ss, et)
                ph.wait("vector", dxi[o % 2], xi_t[(tb, o)])
                if extra_add is not None:
                    ph.wait("vector", dxj[o % 2], xi2_t[(tb, o)])
                if xo_dma[o % 2] is not None:
                    ph.wait("vector", dxo[o % 2], xo_dma[o % 2])
                vt = ph.do("vector", lambda e: e.tensor_tensor(
                    xo[:, (o % 2) * TB + c * 512:(o % 2) * TB + (c + 1) * 512],
                    tf[:, gi2 * 512:(gi2 + 1) * 512],
                    xi[:, (o % 2) * TB + c * 512:(o % 2) * TB + (c + 1) * 512],
                    ALU.add), sv, 1)
                if extra_add is not None:
                    ph.wait("vector", sv, vt)
                    vt = ph.do("vector", lambda e: e.tensor_tensor(
                        xo[:, (o % 2) * TB + c * 512:
                           (o % 2) * TB + (c + 1) * 512],
                        xo[:, (o % 2) * TB + c * 512:
                           (o % 2) * TB + (c + 1) * 512],
                        xi2[:, (o % 2) * TB + c * 512:
                            (o % 2) * TB + (c + 1) * 512],
                        ALU.add), sv, 1)
                tf_guard[gi2] = vt
                if c == TBc - 1:
                    ph.wait("sync", sv, vt)
                    xo_dma[o % 2] = ph.do("sync", lambda e: e.dma_start(
                        xout[o * 128:(o + 1) * 128, tb * TB:(tb + 1) * TB],
                        xo[:, (o % 2) * TB:(o % 2 + 1) * TB]), dxo[o % 2], 16)
                    dn_ev.append((dxo[o % 2], xo_dma[o % 2]))
                return (sv, vt)

            hg = emit_mm_big(
                kb, ph, "dn",
                lambda k, c: h[:, k * TB + c * 512:k * TB + (c + 1) * 512],
                NH, down["w"], D, TBc, devac, og_tiles=2,
                wmode=down.get("wmode", "f32"),
                wait_rhs=up_ev_prev)
            h_guard = hg
            fin = dn_ev
        ph.emit()
    return fin


L0F_CONSTS = ["ident", "ones"]


def decl_l0_ffn(nc, T, px=""):
    w = {}
    w["wg"] = nc.dram_tensor(px + "wg", [D, 3072], I32,
                             kind="ExternalInput").ap()
    w["wu"] = nc.dram_tensor(px + "wu", [D, 3072], I32,
                             kind="ExternalInput").ap()
    w["wd"] = nc.dram_tensor(px + "wd", [3072, D], I32,
                             kind="ExternalInput").ap()
    return w


def sub_l0_ffn(kb, cst, caps, w, scr, xin, xout):
    T = kb.T
    ph = Ph(kb)
    sv = kb.sem("v")
    with contextlib.ExitStack() as st0:
        xn = kb.sb("xn", [128, DT * T], BF16, st=st0)
        ln_ticks = emit_ln(kb, ph, st0, xin, xn, cst,
                           extra_scale=1.0 / 127.0)
        fin = emit_ffn(
            kb, cst, ph, xn,
            ups=[dict(w=w["wg"], wmode="i32", act=AF.Silu, combine="first"),
                 dict(w=w["wu"], wmode="i32", act=None, combine="mult")],
            M_hid=3072,
            down=dict(w=w["wd"], wmode="i32", scale=1.0 / 127.0, og_tiles=2),
            xin=xin, xout=xout,
            wait_ln=[(sv, ln_ticks[-1])])
    return fin


def build_l0_ffn(T):
    return standalone(decl_l0_ffn, sub_l0_ffn, L0F_CONSTS, T, [])


L1A_CONSTS = ["ident", "ones", "cosI", "sinI", "permI"]


def decl_l1_attn(nc, T, px=""):
    w = {}
    for n in ("wq", "wk", "wv", "wo"):
        w[n] = nc.dram_tensor(px + n, [D, D], F32, kind="ExternalInput").ap()
    w["w1"] = nc.dram_tensor(px + "w1", [D, 4096], F32,
                             kind="ExternalInput").ap()
    w["w2"] = nc.dram_tensor(px + "w2", [4096, D], F32,
                             kind="ExternalInput").ap()
    for n, m in (("bq", 8), ("bk", 8), ("bv", 8), ("bo", 8), ("b1", 32),
                 ("b2", 8), ("g1", 8), ("be1", 8), ("g2", 8), ("be2", 8)):
        w[n + "_t"] = nc.dram_tensor(px + n + "_t", [128, m], F32,
                                     kind="ExternalInput").ap()
    return w


def sub_l1_attn(kb, cst, caps, w, scr, xin, xout):
    T = kb.T
    ph = Ph(kb)
    ds = kb.sem("dc")
    sv = kb.sem("v")
    scrS, scrS2 = scr["s"], scr["s2"]
    scrQ, scrK, scrV, scrO = scr["q"], scr["k"], scr["v"], scr["o"]
    with contextlib.ExitStack() as st0:
        srow = kb.sb("bstats", [1, 640], F32, st=st0)
        bias_tiles = kb.sb("battn", [128, 16], F32, st=st0)
        bsb_big = kb.sb("bvecs", [128, 8 * 9 + 32], F32, st=st0)
        offs = {}
        off = 0
        t_b = None
        for n, m in (("bq", 8), ("bk", 8), ("bv", 8), ("bo", 8), ("b1", 32),
                     ("b2", 8), ("g1", 8), ("be1", 8), ("g2", 8), ("be2", 8)):
            offs[n] = off
            t_b = ph.do("sync", lambda e, n=n, off=off, m=m: e.dma_start(
                bsb_big[:, off:off + m], w[n + "_t"][:]), ds, 16)
            off += m
        for eng in ("tensor", "scalar", "vector", "gpsimd"):
            ph.wait(eng, ds, t_b)

        def bs(n, m):
            return bsb_big[:, offs[n]:offs[n] + m]

        dslot = kb.sem("dslot")
        with contextlib.ExitStack() as s1:
            xn0 = kb.sb("xn0", [128, DT * T], BF16, st=s1)
            lt = emit_ln(kb, ph, s1, xin, xn0, cst)
            sl_t = None
            for j in range(DT):
                ph.wait("sync", sv, lt[j])
                sl_t = ph.do("sync", lambda e, j=j: e.dma_start(
                    scrS[j * 128:(j + 1) * 128, :], xn0[:, j * T:(j + 1) * T]),
                    dslot, 16)
            ph.emit()
        with contextlib.ExitStack() as s2:
            xn1 = kb.sb("xn1", [128, DT * T], BF16, st=s2)
            lt1 = emit_ln(kb, ph, s2, scrS, xn1, cst,
                          gains=(bs("g1", 8), bs("be1", 8)),
                          wait_first=[(dslot, sl_t)])
            qkv_ticks = emit_qkv(kb, cst, ph, s2, xn1, [
                dict(name="q", w=w["wq"], bias=bs("bq", 8), post="ropeI",
                     dst=scrQ, bound=(srow, 0)),
                dict(name="k", w=w["wk"], bias=bs("bk", 8), post="ropeI",
                     dst=scrK, bound=(srow, 16 * kb.TC)),
                dict(name="v", w=w["wv"], bias=bs("bv", 8), post=None,
                     dst=scrV),
            ], wait_ln=lt1)
        bias_rdy = emit_bounds_final(kb, cst, ph, srow, bias_tiles, SCALE_ATTN)
        wfm = {}
        for s_, t_ in qkv_ticks.values():
            wfm[s_] = max(wfm.get(s_, 0), t_)
        (o_sem, o_tick), _ = emit_attn_core(
            kb, cst, ph, scrQ, scrK, scrV, scrO, bias_tiles, bias_rdy,
            SCALE_ATTN, wait_first=list(wfm.items()))
        fin5 = emit_outproj_residual(kb, cst, ph, scrO, w["wo"], bs("bo", 8),
                                     scrS, scrS2, wait_first=[(o_sem, o_tick)],
                                     xin_bf=True)
        with contextlib.ExitStack() as s6:
            xn2 = kb.sb("xn2", [128, DT * T], BF16, st=s6)
            lt2 = emit_ln(kb, ph, s6, scrS2, xn2, cst,
                          gains=(bs("g2", 8), bs("be2", 8)),
                          wait_first=fin5)
            fin = emit_ffn(
                kb, cst, ph, xn2,
                ups=[dict(w=w["w1"], wmode="f32", act=AF.Relu,
                          bias_ap=bs("b1", 32), combine="first")],
                M_hid=4096,
                down=dict(w=w["w2"], wmode="f32", bias_ap=bs("b2", 8),
                          og_tiles=1),
                xin=scrS2, xout=xout, extra_add=xin,
                wait_ln=[(sv, lt2[-1])])
    return fin


def build_l1_attn(T):
    return standalone(decl_l1_attn, sub_l1_attn, L1A_CONSTS, T,
                      ["s", "s2", "q", "k", "v", "o"])


def emit_mm_big(kb, ph, name, rhs_slice, n_k, w_dram, M_out, n_c, evac,
                og_tiles=2, wmode="f32", wait_rhs=None, col0=0):
    stt = kb.sem("t")
    q = "gpsimd" if wmode in ("f32", "i32") else "sync"
    if q == "gpsimd":
        dwp = [kb.sem("dwm0"), kb.sem("dwm1")]
    else:
        dwp = [kb.sem("dwn0"), kb.sem("dwn1")]
    OG = og_tiles * 128
    n_og = M_out // OG
    nb = og_tiles * n_c
    assert nb <= 4
    with contextlib.ExitStack() as pst:
        wog = [kb.sb(f"w{name}", [128, n_k * OG], BF16, st=pst)
               for _ in range(2)]
        pp = [kb.ps(f"pp{name}", [128, 512], F32, st=pst) for _ in range(nb)]
        wr = w_dram.rearrange("(k p) m -> p k m", p=128)
        if wait_rhs:
            for s_, t_ in wait_rhs:
                ph.wait(q, s_, t_)
        mm_last = {}
        ev_pend = []
        for og in range(n_og):
            par = og % 2
            if og >= 2:
                ph.wait(q, stt, mm_last[og - 2])
            wt = ph.do(q, lambda e, og=og, par=par: e.dma_start(
                wog[par][:].rearrange("p (k m) -> p k m", k=n_k),
                wr[:, :, col0 + og * OG: col0 + (og + 1) * OG]),
                dwp[par], 16)
            ph.wait("tensor", dwp[par], wt)
            if og == 0 and wait_rhs:
                for s_, t_ in wait_rhs:
                    ph.wait("tensor", s_, t_)
            if og >= 1:
                for s_, t_ in ev_pend:
                    ph.wait("tensor", s_, t_)
                ev_pend = []
            mmt = {}
            for k in range(n_k):
                for oi in range(og_tiles):
                    for c in range(n_c):
                        bi = oi * n_c + c
                        mm = ph.do("tensor", lambda e, k=k, oi=oi, c=c, bi=bi,
                                   par=par: e.matmul(
                            pp[bi][:],
                            wog[par][:, k * OG + oi * 128:k * OG + (oi + 1) * 128],
                            rhs_slice(k, c),
                            start=(k == 0), stop=(k == n_k - 1)),
                            stt if k == n_k - 1 else None)
                        if k == n_k - 1:
                            mmt[(oi, c)] = mm
            mm_last[og] = mm
            for oi in range(og_tiles):
                for c in range(n_c):
                    bi = oi * n_c + c
                    ev_pend.append(evac(og * og_tiles + oi, c, pp[bi],
                                        (stt, mmt[(oi, c)])))
        ph.emit()
    return ev_pend


def emit_dequant_u8(kb, cst, ph, name, q_dram, scales_dram, scrW, K_in, M,
                    selP, P):
    dqp = [kb.sem("dqq0"), kb.sem("dqq1")]
    dsc = kb.sem("dscq")
    dwo = [kb.sem("dwoq0"), kb.sem("dwoq1")]
    stt = kb.sem("t")
    ss = kb.sem("s")
    sv = kb.sem("v")
    NK = K_in // 128
    with contextlib.ExitStack() as pst:
        ssb = kb.sb(f"dq_s{name}", [P, M], BF16, st=pst)
        qb = [kb.sb(f"dq_q{name}", [128, M], BF16, st=pst) for _ in range(2)]
        sbc = kb.sb(f"dq_bc{name}", [128, M], BF16, st=pst)
        pb = [kb.ps(f"dq_pb{name}", [128, 512], F32, st=pst) for _ in range(2)]
        lt = ph.do("gpsimd", lambda e: e.dma_start(ssb[:], scales_dram[:]),
                   dsc, 16)
        ph.wait("tensor", dsc, lt)
        prev_tt = None
        prev_dma = {0: None, 1: None}
        for k in range(NK):
            par = k % 2
            if prev_dma[par] is not None:
                ph.wait("gpsimd", dwo[par], prev_dma[par])
            qt = ph.do("gpsimd", lambda e, k=k, par=par: e.dma_start(
                qb[par][:], q_dram[k * 128:(k + 1) * 128, :]), dqp[par], 16)
            ev_t = {}
            for c in range(M // 512):
                bi = c % 2
                if c >= 2:
                    ph.wait("tensor", sv, ev_t[c - 2])
                elif prev_tt is not None and c < 2:
                    ph.wait("tensor", sv, prev_tt)
                bm = ph.do("tensor", lambda e, k=k, c=c, bi=bi: e.matmul(
                    pb[bi][:], selP[:, k * 128:(k + 1) * 128],
                    ssb[:, c * 512:(c + 1) * 512], start=True, stop=True),
                    stt, 1)
                ph.wait("vector", stt, bm)
                ev_t[c] = ph.do("vector", lambda e, c=c, bi=bi: e.tensor_copy(
                    sbc[:, c * 512:(c + 1) * 512], pb[bi][:]), sv, 1)
            ph.wait("vector", dqp[par], qt)
            t1 = ph.do("vector", lambda e, par=par: e.tensor_single_scalar(
                qb[par][:], qb[par][:], -128.0, ALU.add), sv, 1)
            ph.wait("vector", sv, t1)
            tt = ph.do("vector", lambda e, par=par: e.tensor_tensor(
                qb[par][:], qb[par][:], sbc[:], ALU.mult), sv, 1)
            prev_tt = tt
            ph.wait("sync", sv, tt)
            prev_dma[par] = ph.do("sync", lambda e, k=k, par=par: e.dma_start(
                scrW[k * 128:(k + 1) * 128, :], qb[par][:]), dwo[par], 16)
        ph.emit()
    return [(dwo[0], prev_dma[0]), (dwo[1], prev_dma[1])]


def emit_ffn(kb, cst, ph, xn, ups, M_hid, down, xin, xout, wait_ln,
             extra_add=None, TBc=2, xin_bf=False, h_src=None):
    T = kb.T
    TBc = min(TBc, kb.TC)
    TB = TBc * 512
    NTB = T // TB
    NH = M_hid // 128
    stt = kb.sem("t")
    ss = kb.sem("s")
    sv = kb.sem("v")
    if xin_bf:
        dxi = [kb.sem("gxi0"), kb.sem("gxi1")]
    else:
        dxi = [kb.sem("fxi0"), kb.sem("fxi1")]
    dxj = [kb.sem("fxj0"), kb.sem("fxj1")]
    dxo = [kb.sem("fxo0"), kb.sem("fxo1")]
    with contextlib.ExitStack() as pst:
        h = kb.sb("ffn_h", [128, NH * TB], BF16, st=pst)
        tmp = kb.sb("ffn_tmp", [128, 2 * 512], BF16, st=pst)
        tf = kb.sb("ffn_tf", [128, 2 * 512], F32, st=pst)
        xi = kb.sb("ffn_xi", [128, 2 * TB], F32, st=pst)
        xi2 = kb.sb("ffn_xi2", [128, 2 * TB], F32, st=pst) if extra_add else None
        xo = kb.sb("ffn_xo", [128, 2 * TB], F32, st=pst)
        fin = []
        up_ev_prev = []
        xi_t = {}
        xi2_t = {}
        xo_dma = {0: None, 1: None}
        h_guard = []
        tmp_guard = {0: None, 1: None}
        tf_guard = {0: None, 1: None}
        xo_guard = {}
        dhs = kb.sem("fhs")
        for tb in range(NTB):
            tpar = tb % 2
            if h_src is not None:
                if tb > 0:
                    for s_, t_ in h_guard:
                        ph.wait("sync", s_, t_)
                else:
                    for s_, t_ in (wait_ln or []):
                        ph.wait("sync", s_, t_)
                ht = None
                for k in range(NH):
                    ht = ph.do("sync", lambda e, k=k, tb=tb: e.dma_start(
                        h[:, k * TB:(k + 1) * TB],
                        h_src[k * 128:(k + 1) * 128, tb * TB:(tb + 1) * TB]),
                        dhs, 16)
                up_ev_prev = [(dhs, ht)] + (h_guard if tb > 0 else [])
                h_guard = []
            for ui, up in enumerate(ups):
                def mk_evac(up=up, ui=ui, tb=tb):
                    def evac(o, c, psum, mmw):
                        ph.wait("scalar", mmw[0], mmw[1])
                        kw = {}
                        if up.get("scale_ap") is not None:
                            kw["scale"] = up["scale_ap"][:, o:o + 1]
                        if up.get("bias_ap") is not None:
                            kw["bias"] = up["bias_ap"][:, o:o + 1]
                        fn = up.get("act") or AF.Identity
                        if up.get("combine") == "mult":
                            par2 = c % 2
                            if tmp_guard[par2] is not None:
                                ph.wait("scalar", sv, tmp_guard[par2])
                            et = ph.do("scalar", lambda e: e.activation(
                                tmp[:, par2 * 512:(par2 + 1) * 512], psum[:],
                                fn, **kw), ss, 1)
                            ph.wait("vector", ss, et)
                            vt = ph.do("vector", lambda e: e.tensor_tensor(
                                h[:, o * TB + c * 512:o * TB + (c + 1) * 512],
                                h[:, o * TB + c * 512:o * TB + (c + 1) * 512],
                                tmp[:, par2 * 512:(par2 + 1) * 512],
                                ALU.mult), sv, 1)
                            tmp_guard[par2] = vt
                            return (sv, vt)
                        else:
                            et = ph.do("scalar", lambda e: e.activation(
                                h[:, o * TB + c * 512:o * TB + (c + 1) * 512],
                                psum[:], fn, **kw), ss, 1)
                            return (ss, et)
                    return evac
                wr = wait_ln if (tb == 0 and ui == 0) else None
                wr = (wr or []) + (h_guard if ui == 0 else list(up_ev_prev))
                ev = emit_mm_big(
                    kb, ph, f"u{ui}", 
                    lambda k, c, tb=tb: xn[:, k * T + tb * TB + c * 512:
                                           k * T + tb * TB + (c + 1) * 512],
                    DT, up["w"], M_hid, TBc, mk_evac(),
                    og_tiles=2, wmode=up.get("wmode", "f32"), wait_rhs=wr)
                up_ev_prev = ev
            h_guard = []
            dn_ev = []

            def devac(o, c, psum, mmw, tb=tb, tpar=tpar):
                ph.wait("scalar", mmw[0], mmw[1])
                gi2 = (o * TBc + c) % 2
                kw = {}
                if down.get("bias_ap") is not None:
                    kw["bias"] = down["bias_ap"][:, o:o + 1]
                if down.get("scale") is not None:
                    kw["scale"] = float(down["scale"])
                if tf_guard[gi2] is not None:
                    ph.wait("scalar", sv, tf_guard[gi2])
                et = ph.do("scalar", lambda e: e.activation(
                    tf[:, gi2 * 512:(gi2 + 1) * 512], psum[:], AF.Identity,
                    **kw), ss, 1)
                if c == 0:
                    xiq = "gpsimd" if xin_bf else "sync"
                    if xo_dma[o % 2] is not None:
                        ph.wait(xiq, dxo[o % 2], xo_dma[o % 2])
                    xi_t[(tb, o)] = ph.do(
                        xiq, lambda e: e.dma_start(
                            xi[:, (o % 2) * TB:(o % 2 + 1) * TB],
                            xin[o * 128:(o + 1) * 128, tb * TB:(tb + 1) * TB]),
                        dxi[o % 2], 16)
                    if extra_add is not None:
                        xi2_t[(tb, o)] = ph.do(
                            "sync", lambda e: e.dma_start(
                                xi2[:, (o % 2) * TB:(o % 2 + 1) * TB],
                                extra_add[o * 128:(o + 1) * 128,
                                          tb * TB:(tb + 1) * TB]),
                            dxj[o % 2], 16)
                ph.wait("vector", ss, et)
                ph.wait("vector", dxi[o % 2], xi_t[(tb, o)])
                if extra_add is not None:
                    ph.wait("vector", dxj[o % 2], xi2_t[(tb, o)])
                if xo_dma[o % 2] is not None:
                    ph.wait("vector", dxo[o % 2], xo_dma[o % 2])
                vt = ph.do("vector", lambda e: e.tensor_tensor(
                    xo[:, (o % 2) * TB + c * 512:(o % 2) * TB + (c + 1) * 512],
                    tf[:, gi2 * 512:(gi2 + 1) * 512],
                    xi[:, (o % 2) * TB + c * 512:(o % 2) * TB + (c + 1) * 512],
                    ALU.add), sv, 1)
                if extra_add is not None:
                    ph.wait("vector", sv, vt)
                    vt = ph.do("vector", lambda e: e.tensor_tensor(
                        xo[:, (o % 2) * TB + c * 512:
                           (o % 2) * TB + (c + 1) * 512],
                        xo[:, (o % 2) * TB + c * 512:
                           (o % 2) * TB + (c + 1) * 512],
                        xi2[:, (o % 2) * TB + c * 512:
                            (o % 2) * TB + (c + 1) * 512],
                        ALU.add), sv, 1)
                tf_guard[gi2] = vt
                if c == TBc - 1:
                    ph.wait("sync", sv, vt)
                    xo_dma[o % 2] = ph.do("sync", lambda e: e.dma_start(
                        xout[o * 128:(o + 1) * 128, tb * TB:(tb + 1) * TB],
                        xo[:, (o % 2) * TB:(o % 2 + 1) * TB]), dxo[o % 2], 16)
                    dn_ev.append((dxo[o % 2], xo_dma[o % 2]))
                return (sv, vt)

            hg = emit_mm_big(
                kb, ph, "dn",
                lambda k, c: h[:, k * TB + c * 512:k * TB + (c + 1) * 512],
                NH, down["w"], D, TBc, devac, og_tiles=2,
                wmode=down.get("wmode", "f32"),
                wait_rhs=up_ev_prev)
            h_guard = hg
            fin = dn_ev
        ph.emit()
    return fin


L0F_CONSTS = ["ident", "ones"]


def decl_l0_ffn(nc, T, px=""):
    w = {}
    w["wg"] = nc.dram_tensor(px + "wg", [D, 3072], I32,
                             kind="ExternalInput").ap()
    w["wu"] = nc.dram_tensor(px + "wu", [D, 3072], I32,
                             kind="ExternalInput").ap()
    w["wd"] = nc.dram_tensor(px + "wd", [3072, D], I32,
                             kind="ExternalInput").ap()
    return w


def sub_l0_ffn(kb, cst, caps, w, scr, xin, xout):
    T = kb.T
    ph = Ph(kb)
    sv = kb.sem("v")
    with contextlib.ExitStack() as st0:
        xn = kb.sb("xn", [128, DT * T], BF16, st=st0)
        ln_ticks = emit_ln(kb, ph, st0, xin, xn, cst,
                           extra_scale=1.0 / 127.0)
        fin = emit_ffn(
            kb, cst, ph, xn,
            ups=[dict(w=w["wg"], wmode="i32", act=AF.Silu, combine="first"),
                 dict(w=w["wu"], wmode="i32", act=None, combine="mult")],
            M_hid=3072,
            down=dict(w=w["wd"], wmode="i32", scale=1.0 / 127.0, og_tiles=2),
            xin=xin, xout=xout,
            wait_ln=[(sv, ln_ticks[-1])])
    return fin


def build_l0_ffn(T):
    return standalone(decl_l0_ffn, sub_l0_ffn, L0F_CONSTS, T, [])


def build_l1_attn(T):
    nc = bass.Bass("TRN2", target_bir_lowering=False)
    xin = nc.dram_tensor("xin", [D, T], F32, kind="ExternalInput").ap()
    xout = nc.dram_tensor("xout", [D, T], F32, kind="ExternalOutput").ap()
    w = {}
    for n in ("wq", "wk", "wv", "wo"):
        w[n] = nc.dram_tensor(n, [D, D], F32, kind="ExternalInput").ap()
    w["w1"] = nc.dram_tensor("w1", [D, 4096], F32, kind="ExternalInput").ap()
    w["w2"] = nc.dram_tensor("w2", [4096, D], F32, kind="ExternalInput").ap()
    bt = {}
    for n, m in (("bq", 8), ("bk", 8), ("bv", 8), ("bo", 8), ("b1", 32),
                 ("b2", 8), ("g1", 8), ("be1", 8), ("g2", 8), ("be2", 8)):
        bt[n] = nc.dram_tensor(n + "_t", [128, m], F32,
                               kind="ExternalInput").ap()
    cnames = ["ident", "ones", "cosI", "sinI", "permI"]
    caps = declare_constants(nc, T, cnames)
    nc.const_names = cnames
    scrS = nc.dram_tensor("scrS", [D, T], BF16, kind="Internal").ap()
    scrS2 = nc.dram_tensor("scrS2", [D, T], F32, kind="Internal").ap()
    scrQ = nc.dram_tensor("scrQ", [D, T], BF16, kind="Internal").ap()
    scrK = nc.dram_tensor("scrK", [D, T], BF16, kind="Internal").ap()
    scrV = nc.dram_tensor("scrV", [D, T], BF16, kind="Internal").ap()
    scrO = nc.dram_tensor("scrO", [D, T], BF16, kind="Internal").ap()
    with contextlib.ExitStack() as st:
        kb = KB(nc, st, T)
        cst = Cst(kb, caps, resident=["ident", "ones"])
        srow = kb.sb("bstats", [1, 640], F32)
        bias_tiles = kb.sb("battn", [128, 16], F32)
        bsb = {}
        ph = Ph(kb)
        ds = kb.sem("dc")
        sv = kb.sem("v")
        cld = cst.load(kb, ph)
        t_b = None
        bsb_big = kb.sb("bvecs", [128, 8 * 9 + 32], F32)
        offs = {}
        off = 0
        for n, m in (("bq", 8), ("bk", 8), ("bv", 8), ("bo", 8), ("b1", 32),
                     ("b2", 8), ("g1", 8), ("be1", 8), ("g2", 8)):
            offs[n] = off
            t_b = ph.do("sync", lambda e, n=n, off=off, m=m: e.dma_start(
                bsb_big[:, off:off + m], bt[n][:]), ds, 16)
            off += m
        offs["be2"] = off
        t_b = ph.do("sync", lambda e, off=off: e.dma_start(
            bsb_big[:, off:off + 8], bt["be2"][:]), ds, 16)
        off += 8
        cld.append((ds, t_b))
        cldm = {}
        for s_, t_ in cld:
            cldm[s_] = max(cldm.get(s_, 0), t_)
        for eng in ("tensor", "scalar", "vector", "gpsimd"):
            for s_, t_ in cldm.items():
                ph.wait(eng, s_, t_)

        def bs(n, m):
            return bsb_big[:, offs[n]:offs[n] + m]

        dslot = kb.sem("dslot")
        with contextlib.ExitStack() as s1:
            xn0 = kb.sb("xn0", [128, DT * T], BF16, st=s1)
            lt = emit_ln(kb, ph, s1, xin, xn0, cst)
            sl_t = None
            for j in range(DT):
                ph.wait("sync", sv, lt[j])
                sl_t = ph.do("sync", lambda e, j=j: e.dma_start(
                    scrS[j * 128:(j + 1) * 128, :], xn0[:, j * T:(j + 1) * T]),
                    dslot, 16)
            ph.emit()
        with contextlib.ExitStack() as s2:
            xn1 = kb.sb("xn1", [128, DT * T], BF16, st=s2)
            lt1 = emit_ln(kb, ph, s2, scrS, xn1, cst,
                          gains=(bs("g1", 8), bs("be1", 8)),
                          wait_first=[(dslot, sl_t)])
            qkv_ticks = emit_qkv(kb, cst, ph, s2, xn1, [
                dict(name="q", w=w["wq"], bias=bs("bq", 8), post="ropeI",
                     dst=scrQ, bound=(srow, 0)),
                dict(name="k", w=w["wk"], bias=bs("bk", 8), post="ropeI",
                     dst=scrK, bound=(srow, 16 * kb.TC)),
                dict(name="v", w=w["wv"], bias=bs("bv", 8), post=None,
                     dst=scrV),
            ], wait_ln=lt1)
        bias_rdy = emit_bounds_final(kb, cst, ph, srow, bias_tiles, SCALE_ATTN)
        wfm = {}
        for s_, t_ in qkv_ticks.values():
            wfm[s_] = max(wfm.get(s_, 0), t_)
        (o_sem, o_tick), _ = emit_attn_core(
            kb, cst, ph, scrQ, scrK, scrV, scrO, bias_tiles, bias_rdy,
            SCALE_ATTN, wait_first=list(wfm.items()))
        fin5 = emit_outproj_residual(kb, cst, ph, scrO, w["wo"], bs("bo", 8),
                                     scrS, scrS2, wait_first=[(o_sem, o_tick)],
                                     xin_bf=True)
        with contextlib.ExitStack() as s6:
            xn2 = kb.sb("xn2", [128, DT * T], BF16, st=s6)
            lt2 = emit_ln(kb, ph, s6, scrS2, xn2, cst,
                          gains=(bs("g2", 8), bs("be2", 8)),
                          wait_first=fin5)
            fin = emit_ffn(
                kb, cst, ph, xn2,
                ups=[dict(w=w["w1"], wmode="f32", act=AF.Relu,
                          bias_ap=bs("b1", 32), combine="first")],
                M_hid=4096,
                down=dict(w=w["w2"], wmode="f32", bias_ap=bs("b2", 8),
                          og_tiles=1),
                xin=scrS2, xout=xout, extra_add=xin,
                wait_ln=[(sv, lt2[-1])])
        phz = Ph(kb)
        for s_, t_ in fin:
            phz.wait("sync", s_, t_)
        phz.do("sync", lambda e: e.nop())
        phz.emit()
    return nc


L2F_CONSTS = ["ident", "ones", "sel8", "sel32"]


def decl_l2_ffn(nc, T, px=""):
    w = {}
    w["fc1q"] = nc.dram_tensor(px + "fc1q", [D, 4096], I32,
                               kind="ExternalInput").ap()
    w["fc2q"] = nc.dram_tensor(px + "fc2q", [4096, D], I32,
                               kind="ExternalInput").ap()
    w["fc1s"] = nc.dram_tensor(px + "fc1s", [8, 4096], F32,
                               kind="ExternalInput").ap()
    w["fc2s"] = nc.dram_tensor(px + "fc2s", [32, D], F32,
                               kind="ExternalInput").ap()
    for n, m in (("b1", 32), ("b2h", 8), ("lg", 8), ("lb", 8)):
        w[n + "_t"] = nc.dram_tensor(px + n + "_t", [128, m], F32,
                                     kind="ExternalInput").ap()
    return w


def sub_l2_ffn(kb, cst, caps, w, scr, xin, xout):
    T = kb.T
    ph = Ph(kb)
    ds = kb.sem("dc")
    sv = kb.sem("v")
    scrS, scrW1, scrW2 = scr["s"], scr["w1q"], scr["w2q"]
    with contextlib.ExitStack() as st0:
        bsb = kb.sb("bvecs", [128, 32 + 8 * 3], F32, st=st0)
        offs = {"b1": 0, "b2h": 32, "lg": 40, "lb": 48}
        t_b = None
        for n, m in (("b1", 32), ("b2h", 8), ("lg", 8), ("lb", 8)):
            t_b = ph.do("sync", lambda e, n=n, m=m: e.dma_start(
                bsb[:, offs[n]:offs[n] + m], w[n + "_t"][:]), ds, 16)
        for eng in ("tensor", "scalar", "vector", "gpsimd"):
            ph.wait(eng, ds, t_b)

        def bs(n, m):
            return bsb[:, offs[n]:offs[n] + m]

        t_p = ph.do("vector", lambda e: e.tensor_single_scalar(
            bs("b2h", 8), bs("b2h", 8), 0.5, ALU.mult), sv, 1)
        ph.wait("scalar", sv, t_p)
        sel8sb = load_sel(kb, cst, ph, "sel8", 8, st0)
        sel32sb = load_sel(kb, cst, ph, "sel32", 32, st0)
        dq1 = emit_dequant_u8(kb, cst, ph, "f1", w["fc1q"], w["fc1s"], scrW1,
                              D, 4096, sel8sb, 8)
        dq2 = emit_dequant_u8(kb, cst, ph, "f2", w["fc2q"], w["fc2s"], scrW2,
                              4096, D, sel32sb, 32)
        dslot = kb.sem("dslot")
        dqm = {}
        for s_, t_ in dq1 + dq2:
            dqm[s_] = max(dqm.get(s_, 0), t_)
        dqw = list(dqm.items())
        with contextlib.ExitStack() as s1:
            xn0 = kb.sb("xn0", [128, DT * T], BF16, st=s1)
            lt = emit_ln(kb, ph, s1, xin, xn0, cst, wait_first=dqw)
            sl_t = None
            for j in range(DT):
                ph.wait("sync", sv, lt[j])
                sl_t = ph.do("sync", lambda e, j=j: e.dma_start(
                    scrS[j * 128:(j + 1) * 128, :], xn0[:, j * T:(j + 1) * T]),
                    dslot, 16)
            ph.emit()
        with contextlib.ExitStack() as s2:
            xn = kb.sb("xn", [128, DT * T], BF16, st=s2)
            lt1 = emit_ln(kb, ph, s2, scrS, xn, cst,
                          gains=(bs("lg", 8), bs("lb", 8)),
                          wait_first=[(dslot, sl_t)])
            fin = emit_ffn(
                kb, cst, ph, xn,
                ups=[dict(w=scrW1, wmode="bf16", act=AF.Silu,
                          bias_ap=bs("b1", 32), combine="first")],
                M_hid=4096,
                down=dict(w=scrW2, wmode="bf16", bias_ap=bs("b2h", 8),
                          scale=0.5, og_tiles=1),
                xin=scrS, xout=xout, extra_add=xin, xin_bf=True,
                wait_ln=[(sv, lt1[-1])] + dqw)
    return fin


def build_l2_ffn(T):
    return standalone(decl_l2_ffn, sub_l2_ffn, L2F_CONSTS, T,
                      ["s", "w1q", "w2q"])


L1F_CONSTS = ["ident", "ones"]


def decl_l1_ffn(nc, T, px=""):
    w = {}
    w["w1"] = nc.dram_tensor(px + "w1", [D, 4096], F32,
                             kind="ExternalInput").ap()
    w["w2"] = nc.dram_tensor(px + "w2", [D, 4096], F32,
                             kind="ExternalInput").ap()
    w["h1"] = nc.dram_tensor(px + "h1", [4096, 4096], F32,
                             kind="ExternalInput").ap()
    w["h2"] = nc.dram_tensor(px + "h2", [4096, 4096], F32,
                             kind="ExternalInput").ap()
    w["w3"] = nc.dram_tensor(px + "w3", [4096, D], F32,
                             kind="ExternalInput").ap()
    for n, m in (("s1", 32), ("s2", 32), ("s3", 32), ("s4", 32), ("s5", 8)):
        w[n + "_t"] = nc.dram_tensor(px + n + "_t", [128, m], F32,
                                     kind="ExternalInput").ap()
    return w


def sub_l1_ffn(kb, cst, caps, w, scr, xin, xout):
    T = kb.T
    nc = kb.nc
    ph = Ph(kb)
    ds = kb.sem("dc")
    sv = kb.sem("v")
    ss = kb.sem("s")
    stt = kb.sem("t")
    TB = min(1024, T)
    NTB = max(1, T // TB)
    TBc = TB // 512
    scrA = scr["a"]
    w1, w2, h1, h2, w3 = w["w1"], w["w2"], w["h1"], w["h2"], w["w3"]
    with contextlib.ExitStack() as st0:
        bsb = kb.sb("bvecs", [128, 32 * 4 + 8], F32, st=st0)
        offs = {"s1": 0, "s2": 32, "s3": 64, "s4": 96, "s5": 128}
        t_b = None
        for n, m in (("s1", 32), ("s2", 32), ("s3", 32), ("s4", 32), ("s5", 8)):
            t_b = ph.do("sync", lambda e, n=n, m=m: e.dma_start(
                bsb[:, offs[n]:offs[n] + m], w[n + "_t"][:]), ds, 16)
        for eng in ENGS:
            ph.wait(eng, ds, t_b)

        def bs(n, m):
            return bsb[:, offs[n]:offs[n] + m]

        da = [kb.sem("dqo0"), kb.sem("dqo1")]
        with contextlib.ExitStack() as sA:
            xn = kb.sb("xn", [128, DT * T], BF16, st=sA)
            lt = emit_ln(kb, ph, sA, xin, xn, cst)
            a_buf = kb.sb("abuf", [128, 2 * 32 * TB], BF16, st=sA)
            tmp = kb.sb("atmp", [128, 2 * 512], BF16, st=sA)
            tmp_guard = {0: None, 1: None}
            adma = {0: None, 1: None}
            for tb in range(NTB):
                par = tb % 2
                ab = a_buf[:, par * 32 * TB:(par + 1) * 32 * TB]
                if adma[par] is not None:
                    ph.wait("scalar", da[par], adma[par])
                    ph.wait("vector", da[par], adma[par])

                def ev1(o, c, psum, mmw, ab=ab):
                    ph.wait("scalar", mmw[0], mmw[1])
                    et = ph.do("scalar", lambda e: e.activation(
                        ab[:, o * TB + c * 512:o * TB + (c + 1) * 512],
                        psum[:], AF.Silu, scale=bs("s1", 32)[:, o:o + 1]),
                        ss, 1)
                    return (ss, et)

                def ev2(o, c, psum, mmw, ab=ab):
                    ph.wait("scalar", mmw[0], mmw[1])
                    p2 = c % 2
                    if tmp_guard[p2] is not None:
                        ph.wait("scalar", sv, tmp_guard[p2])
                    et = ph.do("scalar", lambda e: e.activation(
                        tmp[:, p2 * 512:(p2 + 1) * 512], psum[:], AF.Identity,
                        scale=bs("s2", 32)[:, o:o + 1]), ss, 1)
                    ph.wait("vector", ss, et)
                    vt = ph.do("vector", lambda e: e.tensor_tensor(
                        ab[:, o * TB + c * 512:o * TB + (c + 1) * 512],
                        ab[:, o * TB + c * 512:o * TB + (c + 1) * 512],
                        tmp[:, p2 * 512:(p2 + 1) * 512], ALU.mult), sv, 1)
                    tmp_guard[p2] = vt
                    return (sv, vt)

                wr = [(sv, lt[-1])] if tb == 0 else list(prev_e2)
                e1 = emit_mm_big(kb, ph, "a1",
                                 lambda k, c, tb=tb: xn[:, k * T + tb * TB +
                                                        c * 512:k * T + tb * TB
                                                        + (c + 1) * 512],
                                 DT, w1, 4096, TBc, ev1, og_tiles=2,
                                 wmode="f32", wait_rhs=wr)
                e2 = emit_mm_big(kb, ph, "a2",
                                 lambda k, c, tb=tb: xn[:, k * T + tb * TB +
                                                        c * 512:k * T + tb * TB
                                                        + (c + 1) * 512],
                                 DT, w2, 4096, TBc, ev2, og_tiles=2,
                                 wmode="f32", wait_rhs=e1)
                prev_e2 = e2
                ph.wait("sync", sv, kb.sems["v"].n)
                adma[par] = ph.do("sync", lambda e, tb=tb, ab=ab: e.dma_start(
                    scrA[tb].rearrange("(k p) t -> p k t", p=128),
                    ab.rearrange("p (k t) -> p k t", k=32)), da[par], 16)
            ph.emit()
        dal = [kb.sem("daq"), kb.sem("dak")]
        dxi = [kb.sem("dxi0"), kb.sem("dxi1")]
        dxo = [kb.sem("dxo0"), kb.sem("dxo1")]
        with contextlib.ExitStack() as sB:
            al = kb.sb("albuf", [128, 32 * TB], BF16, st=sB)
            gb = kb.sb("gbuf", [128, 32 * TB], BF16, st=sB)
            tf = kb.sb("ltf", [128, 2 * 512], F32, st=sB)
            xi = kb.sb("lxi", [128, 2 * TB], F32, st=sB)
            xo = kb.sb("lxo", [128, 2 * TB], F32, st=sB)
            tf_guard = {0: None, 1: None}
            xo_dma = {0: None, 1: None}
            xi_t = {}
            al_guard = []
            for tb in range(NTB):
                par = tb % 2
                for s_, t_ in al_guard:
                    ph.wait("sync", s_, t_)
                ph.wait("sync", da[par], adma[par])
                alt = ph.do("sync", lambda e, tb=tb: e.dma_start(
                    al.rearrange("p (k t) -> p k t", k=32),
                    scrA[tb].rearrange("(k p) t -> p k t", p=128)),
                    dal[tb % 2], 16)

                def evg(dst, sc_name):
                    def ev(o, c, psum, mmw):
                        ph.wait("scalar", mmw[0], mmw[1])
                        et = ph.do("scalar", lambda e: e.activation(
                            dst[:, o * TB + c * 512:o * TB + (c + 1) * 512],
                            psum[:], AF.Gelu,
                            scale=bs(sc_name, 32)[:, o:o + 1]), ss, 1)
                        return (ss, et)
                    return ev

                eg1 = emit_mm_big(kb, ph, "g1",
                                  lambda k, c: al[:, k * TB + c * 512:
                                                  k * TB + (c + 1) * 512],
                                  32, h1, 4096, TBc, evg(gb, "s3"),
                                  og_tiles=2, wmode="f32",
                                  wait_rhs=[(dal[tb % 2], alt)] + al_guard)
                eg2 = emit_mm_big(kb, ph, "g2",
                                  lambda k, c: gb[:, k * TB + c * 512:
                                                  k * TB + (c + 1) * 512],
                                  32, h2, 4096, TBc, evg(al, "s4"),
                                  og_tiles=2, wmode="f32", wait_rhs=eg1)

                def evd(o, c, psum, mmw, tb=tb):
                    ph.wait("scalar", mmw[0], mmw[1])
                    gi2 = (o * TBc + c) % 2
                    if tf_guard[gi2] is not None:
                        ph.wait("scalar", sv, tf_guard[gi2])
                    et = ph.do("scalar", lambda e: e.activation(
                        tf[:, gi2 * 512:(gi2 + 1) * 512], psum[:], AF.Identity,
                        scale=bs("s5", 8)[:, o:o + 1]), ss, 1)
                    if c == 0:
                        if xo_dma[o % 2] is not None:
                            ph.wait("sync", dxo[o % 2], xo_dma[o % 2])
                        xi_t[(tb, o)] = ph.do("sync", lambda e: e.dma_start(
                            xi[:, (o % 2) * TB:(o % 2 + 1) * TB],
                            xin[o * 128:(o + 1) * 128, tb * TB:(tb + 1) * TB]),
                            dxi[o % 2], 16)
                    ph.wait("vector", ss, et)
                    ph.wait("vector", dxi[o % 2], xi_t[(tb, o)])
                    if xo_dma[o % 2] is not None:
                        ph.wait("vector", dxo[o % 2], xo_dma[o % 2])
                    vt = ph.do("vector", lambda e: e.tensor_tensor(
                        xo[:, (o % 2) * TB + c * 512:
                           (o % 2) * TB + (c + 1) * 512],
                        tf[:, gi2 * 512:(gi2 + 1) * 512],
                        xi[:, (o % 2) * TB + c * 512:
                           (o % 2) * TB + (c + 1) * 512], ALU.add), sv, 1)
                    tf_guard[gi2] = vt
                    if c == TBc - 1:
                        ph.wait("sync", sv, vt)
                        xo_dma[o % 2] = ph.do("sync", lambda e: e.dma_start(
                            xout[o * 128:(o + 1) * 128, tb * TB:(tb + 1) * TB],
                            xo[:, (o % 2) * TB:(o % 2 + 1) * TB]),
                            dxo[o % 2], 16)
                    return (sv, vt)

                al_guard = emit_mm_big(kb, ph, "dn",
                                       lambda k, c: al[:, k * TB + c * 512:
                                                       k * TB + (c + 1) * 512],
                                       32, w3, D, TBc, evd, og_tiles=1,
                                       wmode="f32", wait_rhs=eg2)
            ph.emit()
    return [(v_, t_) for v_, t_ in
            ((dxo[0], xo_dma[0]), (dxo[1], xo_dma[1])) if t_ is not None]


def build_l1_ffn(T):
    return standalone(decl_l1_ffn, sub_l1_ffn, L1F_CONSTS, T, ["a"])


L3A_CONSTS = ["ident", "ones", "cosI", "sinI", "permI"]


def decl_l3_attn(nc, T, px=""):
    w = {}
    w["wqkv"] = nc.dram_tensor(px + "wqkv", [D, 3072], F32,
                               kind="ExternalInput").ap()
    w["bqkv_t"] = nc.dram_tensor(px + "bqkv_t", [128, 24], F32,
                                 kind="ExternalInput").ap()
    w["wlep_t"] = nc.dram_tensor(px + "wlep_t", [128, 24], F32,
                                 kind="ExternalInput").ap()
    w["blep_t"] = nc.dram_tensor(px + "blep_t", [128, 8], F32,
                                 kind="ExternalInput").ap()
    return w


def sub_l3_attn(kb, cst, caps, w, scr, xin, xout):
    T = kb.T
    ph = Ph(kb)
    ds = kb.sem("dc")
    sv = kb.sem("v")
    ss = kb.sem("s")
    stt = kb.sem("t")
    wqkv = w["wqkv"]
    scrQ, scrK, scrV, scrO = scr["q"], scr["k"], scr["v"], scr["o"]
    with contextlib.ExitStack() as st0:
        srow = kb.sb("bstats", [1, 640], F32, st=st0)
        bias_tiles = kb.sb("battn", [128, 16], F32, st=st0)
        bsb = kb.sb("bvecs", [128, 24 + 24 + 8], F32, st=st0)
        t_b = ph.do("sync", lambda e: e.dma_start(bsb[:, 0:24],
                                                  w["bqkv_t"][:]), ds, 16)
        t_b = ph.do("sync", lambda e: e.dma_start(bsb[:, 24:48],
                                                  w["wlep_t"][:]), ds, 16)
        t_b = ph.do("sync", lambda e: e.dma_start(bsb[:, 48:56],
                                                  w["blep_t"][:]), ds, 16)
        for eng in ENGS:
            ph.wait(eng, ds, t_b)
        with contextlib.ExitStack() as s1:
            xn = kb.sb("xn", [128, DT * T], BF16, st=s1)
            lt = emit_ln(kb, ph, s1, xin, xn, cst)
            qkv_ticks = emit_qkv(kb, cst, ph, s1, xn, [
                dict(name="q", w=wqkv[:, 0:1024], bias=bsb[:, 0:8],
                     post="elu1_ropeI", dst=scrQ, bound=(srow, 0)),
                dict(name="k", w=wqkv[:, 1024:2048], bias=bsb[:, 8:16],
                     post="elu1_ropeI", dst=scrK, bound=(srow, 16 * kb.TC)),
                dict(name="v", w=wqkv[:, 2048:3072], bias=bsb[:, 16:24],
                     post=None, dst=scrV),
            ], wait_ln=lt)
        bias_rdy = emit_bounds_final(kb, cst, ph, srow, bias_tiles, SCALE_ATTN)
        wfm = {}
        for s_, t_ in qkv_ticks.values():
            wfm[s_] = max(wfm.get(s_, 0), t_)
        (o_sem, o_tick), _ = emit_attn_core(
            kb, cst, ph, scrQ, scrK, scrV, scrO, bias_tiles, bias_rdy,
            SCALE_ATTN, wait_first=list(wfm.items()))
        dov = [kb.sem("daq"), kb.sem("dak")]
        doo = [kb.sem("dav"), kb.sem("dot2")]
        dxi = [kb.sem("dxi0"), kb.sem("dxi1")]
        dxo = [kb.sem("dxo0"), kb.sem("dxo1")]
        with contextlib.ExitStack() as s2:
            vt_ = kb.sb("l3vt", [128, 2 * T], BF16, st=s2)
            ot_ = kb.sb("l3ot", [128, 2 * T], BF16, st=s2)
            xi_ = kb.sb("l3xi", [128, 2 * T], F32, st=s2)
            lp_ = kb.sb("l3lp", [128, 2 * T], F32, st=s2)
            ph.wait("sync", o_sem, o_tick)
            lastv = {}
            xo_dma = {0: None, 1: None}
            for o in range(DT):
                par = o % 2
                if xo_dma[par] is not None:
                    ph.wait("sync", dxo[par], xo_dma[par])
                lv = ph.do("sync", lambda e, o=o, par=par: e.dma_start(
                    vt_[:, par * T:(par + 1) * T],
                    scrV[o * 128:(o + 1) * 128, :]), dov[par], 16)
                lo = ph.do("sync", lambda e, o=o, par=par: e.dma_start(
                    ot_[:, par * T:(par + 1) * T],
                    scrO[o * 128:(o + 1) * 128, :]), doo[par], 16)
                lx = ph.do("sync", lambda e, o=o, par=par: e.dma_start(
                    xi_[:, par * T:(par + 1) * T],
                    xin[o * 128:(o + 1) * 128, :]), dxi[par], 16)
                ph.wait("vector", dov[par], lv)
                if xo_dma[par] is not None:
                    ph.wait("vector", dxo[par], xo_dma[par])
                vsl = vt_[:, par * T:(par + 1) * T]
                lsl = lp_[:, par * T:(par + 1) * T]
                t1 = ph.do("vector", lambda e, o=o, vsl=vsl, lsl=lsl:
                           e.tensor_single_scalar(
                    lsl, vsl, bsb[:, 24 + o * 3 + 1:24 + o * 3 + 2], ALU.mult),
                    sv, 1)
                ph.wait("vector", sv, t1)
                t2 = ph.do("vector", lambda e, o=o, vsl=vsl, lsl=lsl:
                           e.scalar_tensor_tensor(
                    lsl[:, 1:T], vsl[:, 0:T - 1],
                    bsb[:, 24 + o * 3 + 0:24 + o * 3 + 1], lsl[:, 1:T],
                    ALU.mult, ALU.add), sv, 1)
                ph.wait("vector", sv, t2)
                t3 = ph.do("vector", lambda e, o=o, vsl=vsl, lsl=lsl:
                           e.scalar_tensor_tensor(
                    lsl[:, 0:T - 1], vsl[:, 1:T],
                    bsb[:, 24 + o * 3 + 2:24 + o * 3 + 3], lsl[:, 0:T - 1],
                    ALU.mult, ALU.add), sv, 1)
                ph.wait("vector", sv, t3)
                t4 = ph.do("vector", lambda e, o=o, lsl=lsl:
                           e.tensor_single_scalar(
                    lsl, lsl, bsb[:, 48 + o:48 + o + 1], ALU.add), sv, 1)
                ph.wait("vector", sv, t4)
                ph.wait("vector", dxi[par], lx)
                t5 = ph.do("vector", lambda e, par=par, lsl=lsl:
                           e.tensor_tensor(
                    lsl, lsl, xi_[:, par * T:(par + 1) * T], ALU.add), sv, 1)
                ph.wait("vector", sv, t5)
                ph.wait("vector", doo[par], lo)
                t6 = ph.do("vector", lambda e, par=par, lsl=lsl:
                           e.tensor_tensor(
                    lsl, lsl, ot_[:, par * T:(par + 1) * T], ALU.add), sv, 1)
                lastv[o] = t6
                ph.wait("sync", sv, t6)
                xo_dma[par] = ph.do("sync", lambda e, o=o, lsl=lsl:
                                    e.dma_start(
                    xout[o * 128:(o + 1) * 128, :], lsl), dxo[par], 16)
            ph.emit()
    return [(s_, t_) for s_, t_ in
            ((dxo[0], xo_dma[0]), (dxo[1], xo_dma[1])) if t_ is not None]


def build_l3_attn(T):
    return standalone(decl_l3_attn, sub_l3_attn, L3A_CONSTS, T,
                      ["q", "k", "v", "o"])


L3FF_CONSTS = ["ident", "ones", "sel9"]


def decl_l3_ffn(nc, T, px=""):
    w = {}
    w["w1p"] = nc.dram_tensor(px + "w1p", [9, D, 256], F32,
                              kind="ExternalInput").ap()
    w["w2p"] = nc.dram_tensor(px + "w2p", [9, 256, D], F32,
                              kind="ExternalInput").ap()
    w["wo1"] = nc.dram_tensor(px + "wo1", [D, 256], F32,
                              kind="ExternalInput").ap()
    w["wo2"] = nc.dram_tensor(px + "wo2", [256, 9], F32,
                              kind="ExternalInput").ap()
    w["b2p"] = nc.dram_tensor(px + "b2p", [9, D], F32,
                              kind="ExternalInput").ap()
    w["mixk_t"] = nc.dram_tensor(px + "mixk_t", [128, 24], F32,
                                 kind="ExternalInput").ap()
    w["b1p_t"] = nc.dram_tensor(px + "b1p_t", [128, 18], F32,
                                kind="ExternalInput").ap()
    w["bo1_t"] = nc.dram_tensor(px + "bo1_t", [128, 2], F32,
                                kind="ExternalInput").ap()
    w["bo2_r"] = nc.dram_tensor(px + "bo2_r", [1, 9], F32,
                                kind="ExternalInput").ap()
    w["gam_t"] = nc.dram_tensor(px + "gam_t", [128, 8], F32,
                                kind="ExternalInput").ap()
    w["bet_t"] = nc.dram_tensor(px + "bet_t", [128, 8], F32,
                                kind="ExternalInput").ap()
    return w


def sub_l3_ffn(kb, cst, caps, w, scr, xin, xout):
    T = kb.T
    ph = Ph(kb)
    ds = kb.sem("dc")
    dw = kb.sem("dwx")
    sv = kb.sem("v")
    ss = kb.sem("s")
    stt = kb.sem("t")
    scrS = scr["s"]
    w1p, w2p, wo1, wo2, b2p = (w["w1p"], w["w2p"], w["wo1"], w["wo2"],
                               w["b2p"])
    with contextlib.ExitStack() as st0:
        bsb = kb.sb("bvecs", [128, 24 + 18 + 2 + 8 + 8], F32, st=st0)
        OFF = {"mix": 0, "b1": 24, "bo1": 42, "gam": 44, "bet": 52}
        t_b = None
        for name, m in (("mixk_t", 24), ("b1p_t", 18), ("bo1_t", 2),
                        ("gam_t", 8), ("bet_t", 8)):
            key = {"mixk_t": "mix", "b1p_t": "b1", "bo1_t": "bo1",
                   "gam_t": "gam", "bet_t": "bet"}[name]
            t_b = ph.do("sync", lambda e, name=name, key=key, m=m:
                        e.dma_start(bsb[:, OFF[key]:OFF[key] + m],
                                    w[name][:]), ds, 16)
        bo2sb = kb.sb("bo2", [1, 9], BF16, st=st0)
        wo2sb = kb.sb("wo2", [128, 2 * 9], BF16, st=st0)
        b2sb = kb.sb("b2sb", [9, D], BF16, st=st0)
        sel9sb = load_sel(kb, cst, ph, "sel9", 9, st0)
        dgw = kb.sem("dwx")
        t_g0 = ph.do("gpsimd", lambda e: e.dma_start(
            wo2sb[:].rearrange("p (k m) -> p k m", k=2),
            wo2.rearrange("(k p) m -> p k m", p=128)), dgw, 16)
        t_g = ph.do("gpsimd", lambda e: e.dma_start(bo2sb[:], w["bo2_r"][:]),
                    dgw, 16)
        t_g = ph.do("gpsimd", lambda e: e.dma_start(b2sb[:], b2p[:]), dgw, 16)
        for eng in ENGS:
            ph.wait(eng, ds, t_b)
            ph.wait(eng, dgw, t_g)
        TBc = min(2, kb.TC)
        TB = TBc * 512
        NTB = T // TB
        dslot = kb.sem("dslot")
        s14 = contextlib.ExitStack()
        selT = kb.sb("selT", [9, T], BF16, st=s14, side="right")
        th = kb.sb("thbuf", [128, 2 * T], BF16, st=s14, side="right")
        with contextlib.ExitStack() as s1:
            xn = kb.sb("xn", [128, DT * T], BF16, st=s1)
            lt = emit_ln(kb, ph, s1, xin, xn, cst)
            xmix = kb.sb("xmix", [128, DT * T], BF16, st=s14, side="right")
            sl_t = None
            mix_t = {}
            for j in range(DT):
                ph.wait("sync", sv, lt[j])
                sl_t = ph.do("sync", lambda e, j=j: e.dma_start(
                    scrS[j * 128:(j + 1) * 128, :], xn[:, j * T:(j + 1) * T]),
                    dslot, 16)
                xs = xn[:, j * T:(j + 1) * T]
                ms = xmix[:, j * T:(j + 1) * T]
                t1 = ph.do("vector", lambda e, j=j, xs=xs, ms=ms:
                           e.tensor_single_scalar(
                    ms, xs, bsb[:, OFF["mix"] + j * 3 + 1:
                                OFF["mix"] + j * 3 + 2], ALU.mult), sv, 1)
                ph.wait("vector", sv, t1)
                t2 = ph.do("vector", lambda e, j=j, xs=xs, ms=ms:
                           e.scalar_tensor_tensor(
                    ms[:, 1:T], xs[:, 0:T - 1],
                    bsb[:, OFF["mix"] + j * 3 + 0:OFF["mix"] + j * 3 + 1],
                    ms[:, 1:T], ALU.mult, ALU.add), sv, 1)
                ph.wait("vector", sv, t2)
                t3 = ph.do("vector", lambda e, j=j, xs=xs, ms=ms:
                           e.scalar_tensor_tensor(
                    ms[:, 0:T - 1], xs[:, 1:T],
                    bsb[:, OFF["mix"] + j * 3 + 2:OFF["mix"] + j * 3 + 3],
                    ms[:, 0:T - 1], ALU.mult, ALU.add), sv, 1)
                mix_t[j] = t3
            ph.emit()
        mix_done = [(sv, mix_t[DT - 1]), (dslot, sl_t)]
        ev_prev = mix_done

        def evt(o, c, psum, mmw, tb=None):
            ph.wait("scalar", mmw[0], mmw[1])
            et = ph.do("scalar", lambda e: e.activation(
                th[:, o * T + tb * TB + c * 512:o * T + tb * TB + (c + 1) * 512],
                psum[:], AF.Tanh, bias=bsb[:, OFF["bo1"] + o:OFF["bo1"] + o + 1]),
                ss, 1)
            return (ss, et)

        for tb in range(NTB):
            ev_prev = emit_mm_big(
                kb, ph, "th",
                lambda k, c, tb=tb: xmix[:, k * T + tb * TB + c * 512:
                                         k * T + tb * TB + (c + 1) * 512],
                DT, wo1, 256, TBc,
                lambda o, c, psum, mmw, tb=tb: evt(o, c, psum, mmw, tb=tb),
                og_tiles=2, wmode="f32", wait_rhs=ev_prev)
        with contextlib.ExitStack() as s3:
            scb = kb.sb("scb", [128, 2 * 16], F32, st=s3)
            mxb = kb.sb("mxb", [128, 2], F32, st=s3)
            selb = kb.sb("selb", [128, 2 * 16], BF16, st=s3)
            psc = [kb.ps("psc", [128, 9], F32, st=s3) for _ in range(2)]
            pst_ = [kb.ps("pselT", [9, 128], BF16, st=s3) for _ in range(2)]
            for s_, t_ in ev_prev:
                ph.wait("tensor", s_, t_)
            sel_ev = {}
            NB = T // 128
            for b in range(NB):
                par = b % 2
                if b >= 2:
                    ph.wait("tensor", sv, sel_ev[b - 2])
                for k in range(2):
                    mm = ph.do("tensor", lambda e, b=b, k=k, par=par: e.matmul(
                        psc[par][:], th[:, k * T + b * 128:k * T + (b + 1) * 128],
                        wo2sb[:, k * 9:(k + 1) * 9],
                        start=(k == 0), stop=False), None)
                mm = ph.do("tensor", lambda e, b=b, par=par: e.matmul(
                    psc[par][:], cst.ones[0:1, :], bo2sb[:],
                    start=False, stop=True), stt, 1)
                ph.wait("vector", stt, mm)
                c1 = ph.do("vector", lambda e, par=par: e.tensor_copy(
                    scb[:, par * 16:par * 16 + 9], psc[par][:]), sv, 1)
                ph.wait("vector", sv, c1)
                c2 = ph.do("vector", lambda e, par=par: e.tensor_reduce(
                    mxb[:, par:par + 1], scb[:, par * 16:par * 16 + 9], AX,
                    ALU.max), sv, 1)
                ph.wait("vector", sv, c2)
                c3 = ph.do("vector", lambda e, par=par: e.tensor_tensor(
                    selb[:, par * 16:par * 16 + 9],
                    scb[:, par * 16:par * 16 + 9],
                    mxb[:, par:par + 1].broadcast_to([128, 9]), ALU.is_ge),
                    sv, 1)
                ph.wait("tensor", sv, c3)
                tr = ph.do("tensor", lambda e, par=par: e.transpose(
                    pst_[par][:], selb[:, par * 16:par * 16 + 9],
                    cst.ident[:]), stt, 1)
                ph.wait("vector", stt, tr)
                sel_ev[b] = ph.do("vector", lambda e, b=b, par=par:
                                  e.tensor_copy(
                    selT[:, b * 128:(b + 1) * 128], pst_[par][:]), sv, 1)
            ph.emit()
            sel_done = sel_ev[NB - 1]
        coll = kb.sb("coll", [128, DT * T], BF16, st=st0)
        with contextlib.ExitStack() as s4:
            hp = kb.sb("hp", [128, 2 * T], BF16, st=s4)
            sbc = kb.sb("sbc", [128, T], BF16, st=s4)
            psb = [kb.ps("psb", [128, 512], F32, st=s4) for _ in range(2)]
            ppat = [kb.ps("ppat", [128, 512], F32, st=s4) for _ in range(2)]
            psb2 = [kb.ps("psb2", [128, 512], F32, st=s4)]
            psb2_gate = {0: None}
            hp_guard = []
            sbc_guard = None
            coll_ev = {}
            for p in range(9):
                bct = {}
                for c in range(kb.TC):
                    pb2 = c % 2
                    if c >= 2:
                        ph.wait("tensor", sv, bct[c - 2])
                    elif sbc_guard is not None:
                        ph.wait("tensor", sv, sbc_guard)
                    if p == 0 and c == 0:
                        ph.wait("tensor", sv, sel_done)
                    bm = ph.do("tensor", lambda e, p=p, c=c, pb2=pb2: e.matmul(
                        psb[pb2][:], sel9sb[:, p * 128:(p + 1) * 128],
                        selT[:, c * 512:(c + 1) * 512], start=True, stop=True),
                        stt, 1)
                    ph.wait("vector", stt, bm)
                    bct[c] = ph.do("vector", lambda e, c=c, pb2=pb2:
                                   e.tensor_copy(
                        sbc[:, c * 512:(c + 1) * 512], psb[pb2][:]), sv, 1)
                sbc_done = bct[kb.TC - 1]

                def evh(o, c2, psum, mmw, p=p, tb=None):
                    ph.wait("scalar", mmw[0], mmw[1])
                    et = ph.do("scalar", lambda e: e.activation(
                        hp[:, o * T + tb * TB + c2 * 512:
                           o * T + tb * TB + (c2 + 1) * 512], psum[:],
                        AF.Identity,
                        bias=bsb[:, OFF["b1"] + p * 2 + o:
                                 OFF["b1"] + p * 2 + o + 1]), ss, 1)
                    return (ss, et)

                ev_prev2 = hp_guard or [(sv, sbc_done)]
                for tb in range(NTB):
                    ev_prev2 = emit_mm_big(
                        kb, ph, "hp",
                        lambda k, c2, tb=tb: xmix[:, k * T + tb * TB + c2 * 512:
                                                  k * T + tb * TB +
                                                  (c2 + 1) * 512],
                        DT, w1p[p], 256, TBc,
                        lambda o, c2, psum, mmw, p=p, tb=tb: evh(
                            o, c2, psum, mmw, p=p, tb=tb),
                        og_tiles=1, wmode="f32", wait_rhs=ev_prev2)
                mk = {}
                for o in range(2):
                    for s_, t_ in ev_prev2:
                        ph.wait("vector", s_, t_)
                    ph.wait("vector", sv, sbc_done)
                    mk[o] = ph.do("vector", lambda e, o=o: e.tensor_tensor(
                        hp[:, o * T:(o + 1) * T], hp[:, o * T:(o + 1) * T],
                        sbc[:], ALU.mult), sv, 1)
                wsb2 = kb.sb("w2sb", [128, 2 * D], BF16, st=s4) \
                    if p == 0 else wsb2
                if p > 0:
                    s_, t_ = coll_ev[(p - 1, DT * kb.TC - 1)]
                    ph.wait("gpsimd", s_, t_)
                dwt = ph.do("gpsimd", lambda e, p=p: e.dma_start(
                    wsb2[:].rearrange("p (k m) -> p k m", k=2),
                    w2p[p].rearrange("(k pp) m -> pp k m", pp=128)), dw, 16)
                ph.wait("tensor", dw, dwt)
                gi = 0
                for o in range(DT):
                    for c in range(kb.TC):
                        bi = gi % 2
                        if gi >= 2:
                            s_, t_ = coll_ev[(p, gi - 2)]
                            ph.wait("tensor", s_, t_)
                        elif p > 0:
                            s_, t_ = coll_ev[(p - 1, gi + DT * kb.TC - 2)]
                            ph.wait("tensor", s_, t_)
                        if gi == 0:
                            ph.wait("tensor", sv, mk[1])
                        for k in range(2):
                            mm = ph.do("tensor", lambda e, k=k, o=o, c=c,
                                       bi=bi: e.matmul(
                                ppat[bi][:],
                                wsb2[:, k * D + o * 128:k * D + (o + 1) * 128],
                                hp[:, k * T + c * 512:k * T + (c + 1) * 512],
                                start=(k == 0), stop=(k == 1)),
                                stt if k == 1 else None)
                        if p == 0:
                            b2i = 0
                            if psb2_gate[b2i] is not None:
                                ph.wait("tensor", sv, psb2_gate[b2i])
                            mmb = ph.do("tensor", lambda e, o=o, c=c, b2i=b2i:
                                        e.matmul(
                                psb2[b2i][:],
                                b2sb[:, o * 128:(o + 1) * 128],
                                selT[:, c * 512:(c + 1) * 512],
                                start=True, stop=True), stt, 1)
                            ph.wait("scalar", stt, mm)
                            ev0 = ph.do("scalar", lambda e, o=o, c=c, bi=bi:
                                        e.activation(
                                coll[:, o * T + c * 512:o * T + (c + 1) * 512],
                                ppat[bi][:], AF.Identity), ss, 1)
                            ph.wait("vector", ss, ev0)
                            ph.wait("vector", stt, mmb)
                            ev = ph.do("vector", lambda e, o=o, c=c, b2i=b2i:
                                       e.tensor_tensor(
                                coll[:, o * T + c * 512:o * T + (c + 1) * 512],
                                coll[:, o * T + c * 512:o * T + (c + 1) * 512],
                                psb2[b2i][:], ALU.add), sv, 1)
                            psb2_gate[b2i] = ev
                            coll_ev[(p, gi)] = (sv, ev)
                        else:
                            ph.wait("vector", stt, mm)
                            ev = ph.do("vector", lambda e, o=o, c=c, bi=bi:
                                       e.tensor_tensor(
                                coll[:, o * T + c * 512:o * T + (c + 1) * 512],
                                coll[:, o * T + c * 512:o * T + (c + 1) * 512],
                                ppat[bi][:], ALU.add), sv, 1)
                            coll_ev[(p, gi)] = (sv, ev)
                        gi += 1
                hp_guard = [coll_ev[(p, DT * kb.TC - 1)]]
                sbc_guard = mk[1]
            ph.emit()
        s14.close()
        dsl2 = [kb.sem("daq"), kb.sem("dak")]
        with contextlib.ExitStack() as s5:
            slb = kb.sb("slb", [128, T], BF16, st=s5)
            ladd = {}
            for j in range(DT):
                par = j % 2
                if j >= 1:
                    ph.wait("sync", sv, ladd[j - 1])
                lt_ = ph.do("sync", lambda e, j=j: e.dma_start(
                    slb[:], scrS[j * 128:(j + 1) * 128, :]), dsl2[par], 16)
                ph.wait("vector", dsl2[par], lt_)
                s_, t_ = coll_ev[(8, j * kb.TC + kb.TC - 1)]
                ph.wait("vector", s_, t_)
                ladd[j] = ph.do("vector", lambda e, j=j: e.tensor_tensor(
                    coll[:, j * T:(j + 1) * T], coll[:, j * T:(j + 1) * T],
                    slb[:], ALU.add), sv, 1)
            ph.emit()
            xnf = coll
            lnt = emit_ln(kb, ph, s5, None, xnf, cst,
                          gains=(bsb[:, OFF["gam"]:OFF["gam"] + 8],
                                 bsb[:, OFF["bet"]:OFF["bet"] + 8]),
                          wait_first=[(sv, ladd[DT - 1])],
                          xsrc_inplace=True)
            dxi = [kb.sem("dxi0"), kb.sem("dxi1")]
            dxo = [kb.sem("dxo0"), kb.sem("dxo1")]
            xi_ = kb.sb("l3fxi", [128, T], F32, st=s5)
            xo_ = kb.sb("l3fxo", [128, T], F32, st=s5)
            xo_dma = {0: None, 1: None}
            for j in range(DT):
                par = j % 2
                for p2 in (0, 1):
                    if xo_dma[p2] is not None:
                        ph.wait("sync", dxo[p2], xo_dma[p2])
                lx = ph.do("sync", lambda e, j=j: e.dma_start(
                    xi_[:], xin[j * 128:(j + 1) * 128, :]), dxi[par], 16)
                ph.wait("vector", dxi[par], lx)
                ph.wait("vector", sv, lnt[j])
                for p2 in (0, 1):
                    if xo_dma[p2] is not None:
                        ph.wait("vector", dxo[p2], xo_dma[p2])
                t6 = ph.do("vector", lambda e, j=j: e.tensor_tensor(
                    xo_[:], xi_[:], xnf[:, j * T:(j + 1) * T], ALU.add), sv, 1)
                ph.wait("sync", sv, t6)
                xo_dma[par] = ph.do("sync", lambda e, j=j: e.dma_start(
                    xout[j * 128:(j + 1) * 128, :], xo_[:]), dxo[par], 16)
            ph.emit()
    return [(s_, t_) for s_, t_ in
            ((dxo[0], xo_dma[0]), (dxo[1], xo_dma[1])) if t_ is not None]


def build_l3_ffn(T):
    return standalone(decl_l3_ffn, sub_l3_ffn, L3FF_CONSTS, T, ["s"])


L2A_CONSTS = ["ident", "ones", "sel16"]


def decl_l2_attn(nc, T, px=""):
    w = {}
    w["win"] = nc.dram_tensor(px + "win", [D, 4096], F32,
                              kind="ExternalInput").ap()
    w["wx"] = nc.dram_tensor(px + "wx", [2048, 96], F32,
                             kind="ExternalInput").ap()
    w["wdt"] = nc.dram_tensor(px + "wdt", [64, 2048], F32,
                              kind="ExternalInput").ap()
    w["wout2"] = nc.dram_tensor(px + "wout2", [2048, D], F32,
                                kind="ExternalInput").ap()
    for n, m in (("cw", 64), ("cb", 16), ("bdt", 16), ("alog", 256),
                 ("dp", 16)):
        w[n + "_t"] = nc.dram_tensor(px + n + "_t", [128, m], F32,
                                     kind="ExternalInput").ap()
    return w


def sub_l2_attn(kb, cst, caps, w, scr, xin, xout):
    T = kb.T
    ph = Ph(kb)
    ds = kb.sem("dc")
    sv = kb.sem("v")
    ss = kb.sem("s")
    stt = kb.sem("t")
    NJ = 16
    win, wx, wdt, wout = w["win"], w["wx"], w["wdt"], w["wout2"]
    scrXZ, scrXC, scrDT, scrYZ = (scr["xz"], scr["xc"], scr["dts"],
                                  scr["yz"])
    with contextlib.ExitStack() as st0:
        s45 = st0.enter_context(contextlib.ExitStack())
        bsb = kb.sb("bvecs", [128, 64 + 16 * 3 + 256], F32, st=st0)
        OFF = {"cw": 0, "cb": 64, "bdt": 80, "dp": 96, "alog": 112}
        t_b = None
        for n, m in (("cw", 64), ("cb", 16), ("bdt", 16), ("dp", 16),
                     ("alog", 256)):
            t_b = ph.do("sync", lambda e, n=n, m=m: e.dma_start(
                bsb[:, OFF[n]:OFF[n] + m], w[n + "_t"][:]), ds, 16)
        for eng in ENGS:
            ph.wait(eng, ds, t_b)
        ta = ph.do("scalar", lambda e: e.activation(
            bsb[:, OFF["alog"]:OFF["alog"] + 256],
            bsb[:, OFF["alog"]:OFF["alog"] + 256], AF.Exp), ss, 1)
        ph.wait("vector", ss, ta)
        ta2 = ph.do("vector", lambda e: e.tensor_single_scalar(
            bsb[:, OFF["alog"]:OFF["alog"] + 256],
            bsb[:, OFF["alog"]:OFF["alog"] + 256], -1.0, ALU.mult), sv, 1)
        ph.wait("scalar", sv, ta2)
        A_sb = bsb[:, OFF["alog"]:OFF["alog"] + 256]
        TBc = min(2, kb.TC)
        TB = TBc * 512
        NTB = T // TB
        dxz = [kb.sem("dxz0"), kb.sem("dxz1")]
        scope = kb.nc.named_scope
        with kb.nc.named_scope("p12"), contextlib.ExitStack() as s1:
            xn = kb.sb("xn", [128, DT * T], BF16, st=s1)
            lt = emit_ln(kb, ph, s1, xin, xn, cst)
            stg = kb.sb("xzstg", [128, 2 * TB], BF16, st=s1)
            stg_gate = {0: None, 1: None}
            gctr = [0]
            ev_prev = [(sv, lt[-1])]
            for tb in range(NTB):
                def evxz(o, c, psum, mmw, tb=tb):
                    ph.wait("scalar", mmw[0], mmw[1])
                    par = gctr[0] % 2
                    if c == 0 and stg_gate[par] is not None:
                        ph.wait("scalar", dxz[par], stg_gate[par])
                    et = ph.do("scalar", lambda e, c=c, par=par: e.activation(
                        stg[:, par * TB + c * 512:par * TB + (c + 1) * 512],
                        psum[:], AF.Identity), ss, 1)
                    if c == TBc - 1:
                        ph.wait("sync", ss, et)
                        dt_ = ph.do("sync", lambda e, o=o, tb=tb, par=par:
                                    e.dma_start(
                            scrXZ[o * 128:(o + 1) * 128, tb * TB:(tb + 1) * TB],
                            stg[:, par * TB:(par + 1) * TB]), dxz[par], 16)
                        stg_gate[par] = dt_
                        gctr[0] += 1
                    return (ss, et)
                ev_prev = emit_mm_big(
                    kb, ph, "xz",
                    lambda k, c, tb=tb: xn[:, k * T + tb * TB + c * 512:
                                           k * T + tb * TB + (c + 1) * 512],
                    DT, win, 4096, TBc, evxz, og_tiles=2, wmode="f32",
                    wait_rhs=ev_prev)
            ph.emit()
        xz_done = [(dxz[0], stg_gate[0]), (dxz[1], stg_gate[1])]
        dxc = [kb.sem("dxc0"), kb.sem("dxc1")]
        dxco = [kb.sem("dxco0"), kb.sem("dxco1")]
        with kb.nc.named_scope("conv"), contextlib.ExitStack() as s3:
            xcb = kb.sb("xcin", [128, 2 * T], BF16, st=s3)
            acc = kb.sb("xcacc", [128, T], F32, st=s3)
            ob = kb.sb("xcout", [128, 2 * T], BF16, st=s3)
            for eng in ("sync",):
                for s_, t_ in xz_done:
                    ph.wait(eng, s_, t_)
            out_dma = {0: None, 1: None}
            acc_gate = None
            for j in range(NJ):
                par = j % 2
                if out_dma[par] is not None:
                    ph.wait("sync", dxco[par], out_dma[par])
                lt_ = ph.do("sync", lambda e, j=j, par=par: e.dma_start(
                    xcb[:, par * T:(par + 1) * T],
                    scrXZ[j * 128:(j + 1) * 128, :]), dxc[par], 16)
                ph.wait("vector", dxc[par], lt_)
                if acc_gate is not None:
                    ph.wait("vector", ss, acc_gate)
                xs = xcb[:, par * T:(par + 1) * T]
                t1 = ph.do("vector", lambda e, j=j, xs=xs: e.tensor_single_scalar(
                    acc[:], xs, bsb[:, OFF["cw"] + j * 4 + 3:
                                    OFF["cw"] + j * 4 + 4], ALU.mult), sv, 1)
                for kk in range(3):
                    sh = 3 - kk
                    ph.wait("vector", sv, kb.sems["v"].n)
                    t1 = ph.do("vector", lambda e, j=j, kk=kk, sh=sh, xs=xs:
                               e.scalar_tensor_tensor(
                        acc[:, sh:T], xs[:, 0:T - sh],
                        bsb[:, OFF["cw"] + j * 4 + kk:OFF["cw"] + j * 4 + kk + 1],
                        acc[:, sh:T], ALU.mult, ALU.add), sv, 1)
                ph.wait("scalar", sv, t1)
                if out_dma[par] is not None:
                    ph.wait("scalar", dxco[par], out_dma[par])
                t2 = ph.do("scalar", lambda e, j=j, par=par: e.activation(
                    ob[:, par * T:(par + 1) * T], acc[:], AF.Silu,
                    bias=bsb[:, OFF["cb"] + j:OFF["cb"] + j + 1]), ss, 1)
                acc_gate = t2
                ph.wait("sync", ss, t2)
                out_dma[par] = ph.do("sync", lambda e, j=j, par=par:
                                     e.dma_start(
                    scrXC[j * 128:(j + 1) * 128, :],
                    ob[:, par * T:(par + 1) * T]), dxco[par], 16)
            ph.emit()
        xc_done = [(dxco[0], out_dma[0]), (dxco[1], out_dma[1])]
        dwx = kb.sem("dwx")
        dxc2 = [kb.sem("dxc20"), kb.sem("dxc21")]
        dtlow = kb.sb("dtlow", [64, T], BF16, st=s45, side="right")
        scrBC = scr["bc"]
        dbcs = kb.sem("dbcs")
        with kb.nc.named_scope("xp"), contextlib.ExitStack() as s4:
            wxsb = kb.sb("wxsb", [128, NJ * 96], BF16, st=s4)
            Bsb = kb.sb("Bsb", [16, T], BF16, st=s4)
            Csb = kb.sb("Csb", [16, T], BF16, st=s4)
            xck = kb.sb("xck", [128, 2 * NJ * TB], BF16, st=s4)
            pA = kb.ps("pA", [80, 512], F32, st=s4)
            pC = kb.ps("pC", [16, 512], F32, st=s4)
            for s_, t_ in xc_done:
                ph.wait("sync", s_, t_)
                ph.wait("gpsimd", s_, t_)
            wt = ph.do("gpsimd", lambda e: e.dma_start(
                wxsb[:].rearrange("p (k m) -> p k m", k=NJ),
                wx.rearrange("(k p) m -> p k m", p=128)), dwx, 16)
            ph.wait("tensor", dwx, wt)
            ld_gate = {0: None, 1: None}
            evg = []
            for tb2 in range(NTB):
                par = tb2 % 2
                if ld_gate[par] is not None:
                    ph.wait("sync", stt, ld_gate[par])
                lt_ = None
                for k in range(NJ):
                    lt_ = ph.do("sync", lambda e, k=k, tb2=tb2, par=par:
                                e.dma_start(
                        xck[:, par * NJ * TB + k * TB:par * NJ * TB +
                            (k + 1) * TB],
                        scrXC[k * 128:(k + 1) * 128,
                              tb2 * TB:(tb2 + 1) * TB]), dxc2[par], 16)
                ph.wait("tensor", dxc2[par], lt_)
                for c in range(TBc):
                    for s_, t_ in evg:
                        ph.wait("tensor", s_, t_)
                    evg = []
                    for k in range(NJ):
                        mA = ph.do("tensor", lambda e, k=k, c=c, par=par:
                                   e.matmul(
                            pA[:], wxsb[:, k * 96:k * 96 + 80],
                            xck[:, par * NJ * TB + k * TB + c * 512:
                                par * NJ * TB + k * TB + (c + 1) * 512],
                            start=(k == 0), stop=(k == NJ - 1)),
                            stt if k == NJ - 1 else None)
                    for k in range(NJ):
                        mC = ph.do("tensor", lambda e, k=k, c=c, par=par:
                                   e.matmul(
                            pC[:], wxsb[:, k * 96 + 80:k * 96 + 96],
                            xck[:, par * NJ * TB + k * TB + c * 512:
                                par * NJ * TB + k * TB + (c + 1) * 512],
                            start=(k == 0), stop=(k == NJ - 1)),
                            stt if k == NJ - 1 else None)
                    cc = tb2 * TB + c * 512
                    ph.wait("vector", stt, mC)
                    e1 = ph.do("vector", lambda e, cc=cc: e.tensor_copy(
                        dtlow[:, cc:cc + 512], pA[0:64, :]), sv, 1)
                    e2 = ph.do("vector", lambda e, cc=cc: e.tensor_copy(
                        Bsb[:, cc:cc + 512], pA[64:80, :]), sv, 1)
                    e3 = ph.do("vector", lambda e, cc=cc: e.tensor_copy(
                        Csb[:, cc:cc + 512], pC[:]), sv, 1)
                    evg = [(sv, e3)]
                ld_gate[par] = kb.sems["t"].n
            ph.wait("sync", sv, kb.sems["v"].n)
            ph.do("sync", lambda e: e.dma_start(scrBC[0:16, :], Bsb[:]),
                  dbcs, 16)
            bcs_t = ph.do("sync", lambda e: e.dma_start(scrBC[16:32, :],
                                                        Csb[:]), dbcs, 16)
            ph.emit()
        dwdt = kb.sem("dwx")
        ddt = [kb.sem("ddt0"), kb.sem("ddt1")]
        with kb.nc.named_scope("dt"), contextlib.ExitStack() as s5:
            wdsb = kb.sb("wdsb", [64, 2048], BF16, st=s5)
            sp0 = kb.sb("sp0", [128, T], F32, st=s5)
            sp1 = kb.sb("sp1", [128, T], F32, st=s5)
            dtb = kb.sb("dtstg", [128, 2 * T], BF16, st=s5)
            pD = [kb.ps("pD", [128, 512], F32, st=s5) for _ in range(2)]
            ph.wait("gpsimd", dbcs, bcs_t)
            wt2 = ph.do("gpsimd", lambda e: e.dma_start(wdsb[:], wdt[:]),
                        kb.sem("dwx"), 16)
            ph.wait("tensor", dwdt, wt2)
            ph.wait("tensor", sv, kb.sems["v"].n)
            dt_dma = {0: None, 1: None}
            ev_gate = {}
            sp_gate = None
            for o in range(NJ):
                par = o % 2
                for c in range(kb.TC):
                    bi = (o * kb.TC + c) % 2
                    if (o * kb.TC + c) >= 2:
                        s_, t_ = ev_gate[(o * kb.TC + c) - 2]
                        ph.wait("tensor", s_, t_)
                    mm = ph.do("tensor", lambda e, o=o, c=c, bi=bi: e.matmul(
                        pD[bi][:], wdsb[:, o * 128:(o + 1) * 128],
                        dtlow[:, c * 512:(c + 1) * 512],
                        start=True, stop=True, tile_position=(0, 0)), stt, 1)
                    ph.wait("scalar", stt, mm)
                    if sp_gate is not None and c == 0:
                        ph.wait("scalar", sv, sp_gate)
                    ev = ph.do("scalar", lambda e, o=o, c=c, bi=bi:
                               e.activation(
                        sp0[:, c * 512:(c + 1) * 512], pD[bi][:], AF.Identity,
                        bias=bsb[:, OFF["bdt"] + o:OFF["bdt"] + o + 1]), ss, 1)
                    ev_gate[o * kb.TC + c] = (ss, ev)
                ph.wait("scalar", ss, ev)
                t1 = ph.do("scalar", lambda e: e.activation(
                    sp1[:], sp0[:], AF.Abs), ss, 1)
                ph.wait("scalar", ss, t1)
                t2 = ph.do("scalar", lambda e: e.activation(
                    sp1[:], sp1[:], AF.Exp, scale=-1.0), ss, 1)
                ph.wait("scalar", ss, t2)
                t3 = ph.do("scalar", lambda e: e.activation(
                    sp1[:], sp1[:], AF.Ln, bias=1.0), ss, 1)
                ph.wait("scalar", ss, t3)
                t4 = ph.do("scalar", lambda e: e.activation(
                    sp0[:], sp0[:], AF.Relu), ss, 1)
                ph.wait("vector", ss, t4)
                if dt_dma[par] is not None:
                    ph.wait("vector", ddt[par], dt_dma[par])
                t5 = ph.do("vector", lambda e, par=par: e.tensor_tensor(
                    dtb[:, par * T:(par + 1) * T], sp0[:], sp1[:], ALU.add),
                    sv, 1)
                sp_gate = t5
                ph.wait("sync", sv, t5)
                dt_dma[par] = ph.do("sync", lambda e, o=o, par=par:
                                    e.dma_start(
                    scrDT[o * 128:(o + 1) * 128, :],
                    dtb[:, par * T:(par + 1) * T]), ddt[par], 16)
            ph.emit()
        s45.close()
        dt_done = [(ddt[0], dt_dma[0]), (ddt[1], dt_dma[1])]
        JG = 1
        djt = [kb.sem("djt0"), kb.sem("djt1")]
        djx = [kb.sem("djx0"), kb.sem("djx1")]
        djz = [kb.sem("djz0"), kb.sem("djz1")]
        dyz = [kb.sem("dyz0"), kb.sem("dyz1")]
        sp = kb.sem("p")
        with kb.nc.named_scope("scan"), contextlib.ExitStack() as s6:
            dtj = kb.sb("dtj", [128, T], BF16, st=s6)
            dtx = kb.sb("dtx", [128, T], BF16, st=s6)
            yacA = kb.sb("yacA", [128, T], F32, st=s6)
            yacB = kb.sb("yacB", [128, T], F32, st=s6)
            xtmp = kb.sb("xtmp", [128, T], BF16, st=s6)
            bcB = [kb.sb("bcB", [128, T], BF16, st=s6) for _ in range(2)]
            bcC = [kb.sb("bcC", [128, T], BF16, st=s6) for _ in range(2)]
            dAb = [kb.sb("dAb", [128, T], BF16, st=s6) for _ in range(2)]
            ub = [kb.sb("ub", [128, T], BF16, st=s6) for _ in range(2)]
            hb = [kb.sb("hb", [128, T], BF16, st=s6) for _ in range(2)]
            tm2 = [kb.sb("tm2", [128, T], BF16, st=s6) for _ in range(2)]
            zb = kb.sb("zb", [128, T], BF16, st=s6)
            yzb = kb.sb("yzb", [128, 2 * T], BF16, st=s6)
            dbc = [kb.sem("dbc0"), kb.sem("dbc1")]
            for s_, t_ in dt_done:
                ph.wait("sync", s_, t_)
            ph.wait("sync", dbcs, bcs_t)
            yz_dma = {0: None, 1: None}
            scan_t = {}
            multp_t = {}
            addA_t = None
            addB_t = None
            bc_t = {}
            bc_cons = {}
            prev_dtx = None
            prev_yz = {}
            for j in range(NJ):
                lt1 = ph.do("sync", lambda e, j=j: e.dma_start(
                    dtj[:], scrDT[j * 128:(j + 1) * 128, :]), djt[j % 2], 16)
                lt2 = ph.do("sync", lambda e, j=j: e.dma_start(
                    xtmp[:], scrXC[j * 128:(j + 1) * 128, :]), djx[j % 2], 16)
                ph.wait("vector", djt[j % 2], lt1)
                ph.wait("vector", djx[j % 2], lt2)
                if prev_dtx is not None:
                    ph.wait("vector", sv, prev_dtx)
                tdx = ph.do("vector", lambda e: e.tensor_tensor(
                    dtx[:], dtj[:], xtmp[:], ALU.mult), sv, 1)
                prev_dtx = tdx
                ph.wait("sync", sv, tdx)
                ph.wait("scalar", djt[j % 2], lt1)
                for n in range(16):
                    it = j * 16 + n
                    par = it % 2
                    if bc_cons.get(par) is not None:
                        ph.wait("sync", sp, bc_cons[par][0])
                        ph.wait("sync", sv, bc_cons[par][1])
                    ph.do("sync", lambda e, n=n, par=par: e.dma_start(
                        bcB[par][:],
                        scrBC[n:n + 1, :].partition_broadcast(128)),
                        dbc[par], 16)
                    bc_t[par] = ph.do("sync", lambda e, n=n, par=par:
                                      e.dma_start(
                        bcC[par][:],
                        scrBC[16 + n:17 + n, :].partition_broadcast(128)),
                        dbc[par], 16)
                    if scan_t.get(it - 2) is not None:
                        ph.wait("scalar", sv, scan_t[it - 2])
                    tda = ph.do("scalar", lambda e, j=j, n=n, par=par:
                                e.activation(
                        dAb[par][:], dtj[:], AF.Exp,
                        scale=A_sb[:, j * 16 + n:j * 16 + n + 1]), ss, 1)
                    if scan_t.get(it - 2) is not None:
                        ph.wait("vector", sv, scan_t[it - 2])
                    ph.wait("vector", dbc[par], bc_t[par])
                    tu = ph.do("vector", lambda e, par=par: e.tensor_tensor(
                        ub[par][:], dtx[:], bcB[par][:], ALU.mult), sv, 1)
                    ph.wait("vector", ss, tda)
                    ph.wait("vector", sv, tu)
                    if multp_t.get(it - 2) is not None:
                        ph.wait("vector", sp, multp_t[it - 2])
                    tsc = ph.do("vector", lambda e, par=par:
                                e.tensor_tensor_scan(
                        hb[par][:], dAb[par][:], ub[par][:], 0.0,
                        ALU.mult, ALU.add), sv, 1)
                    scan_t[it] = tsc
                    ph.wait("gpsimd", sv, tsc)
                    ph.wait("gpsimd", dbc[par], bc_t[par])
                    if it % 2 == 0 and addA_t is not None:
                        ph.wait("gpsimd", sv, addA_t)
                    elif it % 2 == 1 and addB_t is not None:
                        ph.wait("gpsimd", sp, addB_t)
                    tmp_ = ph.do("gpsimd", lambda e, par=par: e.tensor_tensor(
                        tm2[par][:], hb[par][:], bcC[par][:], ALU.mult), sp, 1)
                    multp_t[it] = tmp_
                    bc_cons[par] = (tmp_, tu)
                    if n % 2 == 0:
                        pendA = (par, tmp_, n)
                    else:
                        ph.wait("gpsimd", sp, tmp_)
                        if n == 1:
                            tb_ = ph.do("gpsimd", lambda e, par=par:
                                        e.tensor_copy(yacB[:], tm2[par][:]),
                                        sp, 1)
                        else:
                            tb_ = ph.do("gpsimd", lambda e, par=par:
                                        e.tensor_tensor(
                                yacB[:], yacB[:], tm2[par][:], ALU.add), sp, 1)
                        addB_t = tb_
                        parp, tmpp, np_ = pendA
                        ph.wait("vector", sp, tmpp)
                        if np_ == 0:
                            ta_ = ph.do("vector", lambda e, parp=parp:
                                        e.tensor_copy(yacA[:], tm2[parp][:]),
                                        sv, 1)
                        else:
                            ta_ = ph.do("vector", lambda e, parp=parp:
                                        e.tensor_tensor(
                                yacA[:], yacA[:], tm2[parp][:], ALU.add),
                                sv, 1)
                        addA_t = ta_
                ph.wait("vector", sp, addB_t)
                ph.wait("vector", sv, addA_t)
                tm_ = ph.do("vector", lambda e: e.tensor_tensor(
                    yacA[:], yacA[:], yacB[:], ALU.add), sv, 1)
                ph.wait("vector", sv, tm_)
                tD = ph.do("vector", lambda e, j=j: e.scalar_tensor_tensor(
                    yacA[:], xtmp[:], bsb[:, OFF["dp"] + j:OFF["dp"] + j + 1],
                    yacA[:], ALU.mult, ALU.add), sv, 1)
                lt4 = ph.do("sync", lambda e, j=j: e.dma_start(
                    zb[:], scrXZ[2048 + j * 128:2048 + (j + 1) * 128, :]),
                    djz[j % 2], 16)
                ph.wait("scalar", djz[j % 2], lt4)
                if prev_yz.get(j % 2) is not None:
                    ph.wait("scalar", dyz[j % 2], prev_yz[j % 2])
                tz = ph.do("scalar", lambda e: e.activation(
                    zb[:], zb[:], AF.Silu), ss, 1)
                ph.wait("vector", ss, tz)
                ph.wait("vector", sv, tD)
                if prev_yz.get(j % 2) is not None:
                    ph.wait("vector", dyz[j % 2], prev_yz[j % 2])
                tyz = ph.do("vector", lambda e, j=j: e.tensor_tensor(
                    yzb[:, (j % 2) * T:(j % 2 + 1) * T], yacA[:], zb[:],
                    ALU.mult), sv, 1)
                ph.wait("sync", sv, tyz)
                yz_dma[j % 2] = ph.do("sync", lambda e, j=j: e.dma_start(
                    scrYZ[j * 128:(j + 1) * 128, :],
                    yzb[:, (j % 2) * T:(j % 2 + 1) * T]), dyz[j % 2], 16)
                prev_yz[j % 2] = yz_dma[j % 2]
                ph.wait("sync", sv, tyz)
                ph.wait("scalar", sv, tyz)
                ph.wait("vector", sv, tyz)
                ph.wait("gpsimd", sv, tyz)
            ph.emit()
        yz_done = [(dyz[0], yz_dma[0]), (dyz[1], yz_dma[1])]
        kb.nc.push_named_scope("out") if False else None
        fin = emit_ffn(kb, cst, ph, None, ups=[], M_hid=2048,
                       down=dict(w=wout, wmode="f32", og_tiles=1),
                       xin=xin, xout=xout, wait_ln=yz_done, h_src=scrYZ,
                       TBc=2)
    return fin


def build_l2_attn(T):
    return standalone(decl_l2_attn, sub_l2_attn, L2A_CONSTS, T,
                      ["xz", "xc", "dts", "yz", "bc"])


def _bias_tiles(b):
    return np.ascontiguousarray(np.asarray(b, np.float32).reshape(-1, 128).T)


def _hT(w):
    return np.ascontiguousarray(np.asarray(w).T)


FUSED = True


def kernel(**inputs):
    z = inputs
    T = 4096
    NCORE = 8
    consts = host_constants(T)

    x = np.asarray(z["L0_attn_x"], np.float32)
    xT = [np.ascontiguousarray(x[b].T) for b in range(NCORE)]

    wlep = np.asarray(z["L3_attn_w_lepe"], np.float32).reshape(
        8, 128, 3).transpose(1, 0, 2).reshape(128, 24)
    mixt = np.asarray(z["L3_ffn_mix_kernel"], np.float32).T.reshape(
        8, 128, 3).transpose(1, 0, 2).reshape(128, 24)
    b1t = np.asarray(z["L3_ffn_b1"], np.float32).reshape(
        9, 2, 128).transpose(2, 0, 1).reshape(128, 18)
    cwt = np.asarray(z["L2_attn_conv_w"], np.float32).reshape(
        16, 128, 4).transpose(1, 0, 2).reshape(128, 64)
    alt = np.asarray(z["L2_attn_A_log"], np.float32).reshape(
        16, 128, 16).transpose(1, 0, 2).reshape(128, 256)

    launches = [
        (build_l0_attn, {
            "wq": z["L0_attn_Wq"], "wkv": z["L0_attn_Wkv"],
            "wout": z["L0_attn_Wout"], "bout_t": _bias_tiles(z["L0_attn_bout"]),
        }),
        (build_l0_ffn, {
            "wg": _hT(z["L0_ffn_w_gate"]), "wu": _hT(z["L0_ffn_w_up"]),
            "wd": _hT(z["L0_ffn_w_down"]),
        }),
        (build_l1_attn, {
            "wq": z["L1_attn_Wq"], "wk": z["L1_attn_Wk"],
            "wv": z["L1_attn_Wv"], "wo": z["L1_attn_Wo"],
            "w1": z["L1_attn_W1"], "w2": z["L1_attn_W2"],
            "bq_t": _bias_tiles(z["L1_attn_bq"]),
            "bk_t": _bias_tiles(z["L1_attn_bk"]),
            "bv_t": _bias_tiles(z["L1_attn_bv"]),
            "bo_t": _bias_tiles(z["L1_attn_bo"]),
            "b1_t": _bias_tiles(z["L1_attn_b1"]),
            "b2_t": _bias_tiles(z["L1_attn_b2"]),
            "g1_t": _bias_tiles(z["L1_attn_g1"]),
            "be1_t": _bias_tiles(z["L1_attn_beta1"]),
            "g2_t": _bias_tiles(z["L1_attn_g2"]),
            "be2_t": _bias_tiles(z["L1_attn_beta2"]),
        }),
        (build_l1_ffn, {
            "w1": _hT(z["L1_ffn_w1_w"]), "w2": _hT(z["L1_ffn_w2_w"]),
            "h1": _hT(z["L1_ffn_h1_w"]), "h2": _hT(z["L1_ffn_h2_w"]),
            "w3": _hT(z["L1_ffn_w3_w"]),
            "s1_t": _bias_tiles(z["L1_ffn_w1_s"]),
            "s2_t": _bias_tiles(z["L1_ffn_w2_s"]),
            "s3_t": _bias_tiles(z["L1_ffn_h1_s"]),
            "s4_t": _bias_tiles(z["L1_ffn_h2_s"]),
            "s5_t": _bias_tiles(z["L1_ffn_w3_s"]),
        }),
        (build_l2_attn, {
            "win": _hT(z["L2_attn_W_in"]), "wx": _hT(z["L2_attn_W_x"]),
            "wdt": _hT(z["L2_attn_W_dt"]), "wout2": _hT(z["L2_attn_W_out"]),
            "cw_t": np.ascontiguousarray(cwt),
            "cb_t": _bias_tiles(z["L2_attn_conv_b"]),
            "bdt_t": _bias_tiles(z["L2_attn_b_dt"]),
            "alog_t": np.ascontiguousarray(alt),
            "dp_t": _bias_tiles(z["L2_attn_Dp"]),
        }),
        (build_l2_ffn, {
            "fc1q": np.asarray(z["L2_ffn_fc1_q"], np.int32),
            "fc2q": np.asarray(z["L2_ffn_fc2_q"], np.int32),
            "fc1s": np.asarray(z["L2_ffn_fc1_scales"], np.float32),
            "fc2s": np.asarray(z["L2_ffn_fc2_scales"], np.float32),
            "b1_t": _bias_tiles(z["L2_ffn_fc1_bias"]),
            "b2h_t": _bias_tiles(z["L2_ffn_fc2_bias"]),
            "lg_t": _bias_tiles(z["L2_ffn_ln_gamma"]),
            "lb_t": _bias_tiles(z["L2_ffn_ln_beta"]),
        }),
        (build_l3_attn, {
            "wqkv": _hT(z["L3_attn_w_qkv"]),
            "bqkv_t": _bias_tiles(z["L3_attn_b_qkv"]),
            "wlep_t": np.ascontiguousarray(wlep),
            "blep_t": _bias_tiles(z["L3_attn_b_lepe"]),
        }),
        (build_l3_ffn, {
            "w1p": np.asarray(z["L3_ffn_W1"], np.float32),
            "w2p": np.asarray(z["L3_ffn_W2"], np.float32),
            "wo1": np.asarray(z["L3_ffn_Wo1"], np.float32),
            "wo2": np.asarray(z["L3_ffn_Wo2"], np.float32),
            "b2p": np.asarray(z["L3_ffn_b2"], np.float32),
            "mixk_t": np.ascontiguousarray(mixt),
            "b1p_t": np.ascontiguousarray(b1t),
            "bo1_t": _bias_tiles(z["L3_ffn_bo1"]),
            "bo2_r": np.asarray(z["L3_ffn_bo2"], np.float32)[None, :],
            "gam_t": _bias_tiles(z["L3_ffn_gamma"]),
            "bet_t": _bias_tiles(z["L3_ffn_beta"]),
        }),
    ]

    if FUSED:
        nc = build_fused(T)
        cmap = {"c_" + k: consts[k] for k in nc.const_names}
        wall = {}
        for i, (_, _, px) in enumerate(SUBS):
            for k, v in launches[i][1].items():
                wall[px + "_" + k] = v
        in_maps = []
        for b in range(NCORE):
            m = {"xin": xT[b]}
            m.update(wall)
            m.update(cmap)
            in_maps.append(m)
        res = run_bass_kernel_spmd(nc, in_maps, core_ids=list(range(NCORE)))
        xT = [np.asarray(res.results[b]["xout"]) for b in range(NCORE)]
    else:
        for build, wmap in launches:
            nc = build(T)
            cmap = {"c_" + k: consts[k] for k in nc.const_names}
            in_maps = []
            for b in range(NCORE):
                m = {"xin": xT[b]}
                m.update(wmap)
                m.update(cmap)
                in_maps.append(m)
            res = run_bass_kernel_spmd(nc, in_maps, core_ids=list(range(NCORE)))
            xT = [np.asarray(res.results[b]["xout"]) for b in range(NCORE)]

    out = np.stack([xT[b].T for b in range(NCORE)], axis=0)
    return np.ascontiguousarray(out.astype(np.float32))


ALL_CONSTS = ["ident", "ones", "cos0", "sin0", "cosI", "sinI", "perm0",
              "permI", "sel8", "sel9", "sel16", "sel32"]
SUBS = [
    (decl_l0_attn, sub_l0_attn, "a0"),
    (decl_l0_ffn, sub_l0_ffn, "f0"),
    (decl_l1_attn, sub_l1_attn, "a1"),
    (decl_l1_ffn, sub_l1_ffn, "f1"),
    (decl_l2_attn, sub_l2_attn, "a2"),
    (decl_l2_ffn, sub_l2_ffn, "f2"),
    (decl_l3_attn, sub_l3_attn, "a3"),
    (decl_l3_ffn, sub_l3_ffn, "f3"),
]
ALL_SCR = ["q", "k", "v", "o", "s", "s2", "a", "xz", "xc", "dts", "yz", "bc",
           "w1q", "w2q"]


def build_fused(T):
    nc = bass.Bass("TRN2", target_bir_lowering=False)
    xin0 = nc.dram_tensor("xin", [D, T], F32, kind="ExternalInput").ap()
    xoutF = nc.dram_tensor("xout", [D, T], F32, kind="ExternalOutput").ap()
    xA = nc.dram_tensor("xA", [D, T], F32, kind="Internal").ap()
    xB = nc.dram_tensor("xB", [D, T], F32, kind="Internal").ap()
    caps = declare_constants(nc, T, ALL_CONSTS)
    nc.const_names = ALL_CONSTS
    scr = make_scratch(nc, T, ALL_SCR)
    ws = [decl(nc, T, px + "_") for decl, _, px in SUBS]
    with contextlib.ExitStack() as st:
        kb = KB(nc, st, T)
        cst = Cst(kb, caps, resident=["ident", "ones"])
        ph = Ph(kb)
        cld = cst.load(kb, ph)
        cldm = {}
        for s_, t_ in cld:
            cldm[s_] = max(cldm.get(s_, 0), t_)
        for eng in ENGS:
            for s_, t_ in cldm.items():
                ph.wait(eng, s_, t_)
        ph.emit()
        bar = kb.sem("bar")
        bufs = [xA, xB]
        cur = xin0
        for i, (decl, sub, px) in enumerate(SUBS):
            xo = xoutF if i == len(SUBS) - 1 else bufs[i % 2]
            with nc.named_scope(px):
                fin = sub(kb, cst, caps, ws[i], scr, cur, xo)
            phb = Ph(kb)
            for name, sm in list(kb.sems.items()):
                if sm.is_dma and sm.n > 0:
                    phb.wait("sync", sm, sm.n)
            for eng in ENGS:
                phb.do(eng, lambda e: e.nop(), bar, 1)
            target = bar.n
            for eng in ENGS:
                phb.wait(eng, bar, target)
            phb.emit()
            cur = xo
    return nc


# version 06 (final)
# speedup vs baseline: 1.2259x; 1.2259x over previous
import contextlib
import math
import numpy as np
import ml_dtypes
import concourse.bass as bass
import concourse.mybir as mybir
from concourse.bass_utils import run_bass_kernel_spmd

F32 = mybir.dt.float32
BF16 = mybir.dt.bfloat16
I32 = mybir.dt.int32
AF = mybir.ActivationFunctionType
ALU = mybir.AluOpType
AX = mybir.AxisListType.X
BF = ml_dtypes.bfloat16

D = 1024
DT = D // 128
H = 16
DH = 64
NHP = H // 2
EPS = 1e-5

ENGS = ("sync", "gpsimd", "tensor", "scalar", "vector")


class Sem:
    def __init__(self, kb, name):
        self.h = kb.st.enter_context(kb.nc.semaphore(name))
        self.n = 0
        self.is_dma = False

    def reserve(self, k=1):
        if k >= 16:
            self.is_dma = True
        self.n += k
        return self.n


class KB:
    def __init__(self, nc, st, T):
        self.nc = nc
        self.st = st
        self.T = T
        self.TC = T // 512
        self.sems = {}
        self.uid = 0

    def sem(self, name):
        if name not in self.sems:
            self.sems[name] = Sem(self, name)
        return self.sems[name]

    def name(self, base):
        self.uid += 1
        return f"{base}_{self.uid}"

    def sb(self, base, shape, dt, st=None, side=None):
        if side is not None:
            return (st or self.st).enter_context(
                self.nc.sbuf_tensor(self.name(base), shape, dt, side=side))
        return (st or self.st).enter_context(
            self.nc.sbuf_tensor(self.name(base), shape, dt))

    def ps(self, base, shape, dt=F32, st=None):
        return (st or self.st).enter_context(
            self.nc.psum_tensor(self.name(base), shape, dt))


class Ph:

    def __init__(self, kb):
        self.kb = kb
        self.ops = {k: [] for k in ENGS}

    def on(self, eng, fn):
        self.ops[eng].append(fn)

    def wait(self, eng, sem, val):
        self.on(eng, lambda e: e.wait_ge(sem.h, val))

    def do(self, eng, fn, sem=None, k=1):
        if sem is None:
            self.on(eng, fn)
            return None
        t = sem.reserve(k)
        self.on(eng, lambda e: fn(e).then_inc(sem.h, k))
        return t

    def emit(self):
        with self.kb.nc.Block() as blk:
            for eng in ENGS:
                fns = self.ops[eng]
                if fns:
                    getattr(blk, eng)(lambda e, fns=fns: [fn(e) for fn in fns])
        self.ops = {k: [] for k in ENGS}


def barrier(kb, ph):
    b = kb.sem("bar")
    for eng in ENGS:
        ph.do(eng, lambda e: e.nop(), b, 1)
    return b.n


def wait_bar(kb, ph, val):
    b = kb.sem("bar")
    for eng in ENGS:
        ph.wait(eng, b, val)


def host_constants(T):
    c = {}
    c["ident"] = np.eye(128, dtype=BF)
    ones = np.ones((128, 128), dtype=BF)
    c["ones"] = ones
    inv = 10000.0 ** (-np.arange(0, DH, 2, dtype=np.float64) / DH)
    t = np.arange(T, dtype=np.float64)
    f = np.outer(t, inv)
    cos0 = np.cos(np.concatenate([f, f], 1)).T
    sin0 = np.sin(np.concatenate([f, f], 1)).T
    c["cos0"] = np.ascontiguousarray(np.tile(cos0, (2, 1)), dtype=None).astype(BF)
    c["sin0"] = np.ascontiguousarray(np.tile(sin0, (2, 1))).astype(BF)
    cosI = np.repeat(np.cos(f), 2, axis=1).T
    sinI = np.repeat(np.sin(f), 2, axis=1).T
    c["cosI"] = np.ascontiguousarray(np.tile(cosI, (2, 1))).astype(BF)
    c["sinI"] = np.ascontiguousarray(np.tile(sinI, (2, 1))).astype(BF)
    M0 = np.zeros((64, 64), dtype=np.float32)
    for i in range(32):
        M0[i + 32, i] = -1.0
        M0[i, i + 32] = 1.0
    MI = np.zeros((64, 64), dtype=np.float32)
    for i in range(0, 64, 2):
        MI[i + 1, i] = -1.0
        MI[i, i + 1] = 1.0
    perm0 = np.zeros((128, 128), dtype=np.float32)
    perm0[0:64, 0:64] = M0
    perm0[64:128, 64:128] = M0
    permI = np.zeros((128, 128), dtype=np.float32)
    permI[0:64, 0:64] = MI
    permI[64:128, 64:128] = MI
    c["perm0"] = perm0.astype(BF)
    c["permI"] = permI.astype(BF)
    for P in (8, 9, 16, 32):
        s = np.zeros((P, P * 128), dtype=np.float32)
        for n in range(P):
            s[n, n * 128:(n + 1) * 128] = 1.0
        c[f"sel{P}"] = s.astype(BF)
    return c


CONST_SHAPES = {
    "ident": [128, 128], "ones": [128, 128],
    "cos0": ("T",), "sin0": ("T",), "cosI": ("T",), "sinI": ("T",),
    "perm0": [128, 128], "permI": [128, 128],
    "sel8": [8, 8 * 128], "sel9": [9, 9 * 128],
    "sel16": [16, 16 * 128], "sel32": [32, 32 * 128],
}


def declare_constants(nc, T, names):
    aps = {}
    for k in names:
        shp = CONST_SHAPES[k]
        if shp == ("T",):
            shp = [128, T]
        aps[k] = nc.dram_tensor("c_" + k, shp, BF16, kind="ExternalInput").ap()
    return aps


class Cst:

    def __init__(self, kb, caps, resident=None):
        T = kb.T
        self.names = list(caps.keys()) if resident is None else list(resident)
        for k in self.names:
            shp = CONST_SHAPES[k]
            if shp == ("T",):
                shp = [128, T]
            setattr(self, k, kb.sb(k, shp, BF16))
        self.cvals = kb.sb("cvals", [128, 8], F32)
        self.cmap = {}
        self.caps = caps

    def cval(self, v, parts=128):
        i = self.cmap[float(v)]
        return self.cvals[0:parts, i:i + 1]

    def load(self, kb, ph, vals=(EPS, -128.0, 1.0 / 127.0, 0.5)):
        ds = kb.sem("dc")
        sv = kb.sem("v")
        last = None
        for n in self.names:
            last = ph.do("sync", lambda e, n=n: e.dma_start(
                getattr(self, n)[:], self.caps[n][:]), ds, 16)
        tms = None
        for i, v in enumerate(vals):
            self.cmap[float(v)] = i
            tms = ph.do("vector", lambda e, i=i, v=v: e.memset(
                self.cvals[:, i:i + 1], float(v)), sv, 1)
        return [(ds, last), (sv, tms)]


def load_sel(kb, cst, ph, name, P, st):
    t = kb.sb(name, [P, P * 128], BF16, st=st)
    dsl = kb.sem("dsel")
    ph.wait("sync", dsl, dsl.n)
    tk = ph.do("sync", lambda e: e.dma_start(t[:], cst.caps[name][:]), dsl, 16)
    for eng in ("tensor", "vector", "scalar"):
        ph.wait(eng, dsl, tk)
    return t


def dma_in(ph, kb, dst, src, cast=False):
    q = "gpsimd" if cast else "sync"
    s = kb.sem("dgp" if cast else "dsync")
    return ph.do(q, lambda e: e.dma_start(dst, src), s, 16), s


def dma_out(ph, kb, dst, src):
    s = kb.sem("dout")
    return ph.do("sync", lambda e: e.dma_start(dst, src), s, 16), s


def emit_ln(kb, ph, sub, xin_ap, xn, cst, gains=None, slot_out=None,
            xsrc_sb=None, extra_scale=None, wait_first=None,
            xsrc_inplace=False):
    T = kb.T
    nc = kb.nc
    dg = kb.sem("dlnx")
    sv = kb.sem("v")
    ss = kb.sem("s")
    stt = kb.sem("t")

    lst = contextlib.ExitStack()
    xb = xn
    xsq = kb.sb("ln_xsq", [128, 2 * T], BF16, st=lst)
    rows = kb.sb("ln_rows", [1, 3 * T], F32, st=lst)
    rows_bf = kb.sb("ln_rows_bf", [1, 2 * T], BF16, st=lst)
    mbc = kb.sb("ln_mbc", [128, T], BF16, st=lst)
    rbc = kb.sb("ln_rbc", [128, T], BF16, st=lst)
    with contextlib.ExitStack() as pst:
        pstat = kb.ps("ln_pstat", [1, T], F32, st=pst)
        if wait_first:
            for s_, t_ in wait_first:
                ph.wait("gpsimd", s_, t_)
        ldt = []
        if xsrc_inplace:
            if wait_first:
                for s_, t_ in wait_first:
                    ph.wait("tensor", s_, t_)
                    ph.wait("scalar", s_, t_)
                    ph.wait("vector", s_, t_)
            ldt.append((sv, kb.sems["v"].n))
        else:
            for j in range(DT):
                if xsrc_sb is None:
                    t0 = ph.do("gpsimd", lambda e, j=j: e.dma_start(
                        xb[:, j * T:(j + 1) * T], xin_ap[j * 128:(j + 1) * 128, :]), dg, 16)
                    ldt.append((dg, t0))
                else:
                    t0 = ph.do("vector", lambda e, j=j: e.tensor_copy(
                        xb[:, j * T:(j + 1) * T], xsrc_sb[j]), sv, 1)
                    ldt.append((sv, t0))
            ph.wait("tensor", ldt[-1][0], ldt[-1][1])
            ph.wait("scalar", ldt[-1][0], ldt[-1][1])
        for j in range(DT):
            for c in range(kb.TC):
                ph.do("tensor", lambda e, j=j, c=c: e.matmul(
                    pstat[:, c * 512:(c + 1) * 512], cst.ones[:, 0:1],
                    xb[:, j * T + c * 512: j * T + (c + 1) * 512],
                    start=(j == 0), stop=(j == DT - 1)),
                    stt if (j == DT - 1 and c == kb.TC - 1) else None)
        tsum = kb.sems["t"].n
        ph.wait("vector", stt, tsum)
        t_evs = ph.do("vector", lambda e: e.tensor_copy(rows[:, 0:T], pstat[:]), sv, 1)
        sq_ticks = []
        for j in range(DT):
            if j >= 2:
                ph.wait("scalar", stt, sq_ticks[j - 2])
            ph.do("scalar", lambda e, j=j: e.activation(
                xsq[:, (j % 2) * T:(j % 2 + 1) * T], xb[:, j * T:(j + 1) * T],
                AF.Square), ss, 1)
            sqs = kb.sems["s"].n
            ph.wait("tensor", ss, sqs)
            if j == 0:
                ph.wait("tensor", sv, t_evs)
            for c in range(kb.TC):
                tick = ph.do("tensor", lambda e, j=j, c=c: e.matmul(
                    pstat[:, c * 512:(c + 1) * 512], cst.ones[:, 0:1],
                    xsq[:, (j % 2) * T + c * 512:(j % 2) * T + (c + 1) * 512],
                    start=(j == 0), stop=(j == DT - 1)),
                    stt, 1)
            sq_ticks.append(tick)
        ph.wait("vector", stt, sq_ticks[-1])
        t_evs2 = ph.do("vector", lambda e: e.tensor_copy(rows[:, T:2 * T], pstat[:]), sv, 1)
        ph.wait("vector", sv, t_evs2)
        t1 = ph.do("vector", lambda e: e.tensor_single_scalar(
            rows[:, 0:T], rows[:, 0:T], 1.0 / D, ALU.mult), sv, 1)
        ph.wait("scalar", sv, t1)
        t2 = ph.do("scalar", lambda e: e.activation(
            rows[:, 2 * T:3 * T], rows[:, 0:T], AF.Square), ss, 1)
        ph.wait("vector", ss, t2)
        t3 = ph.do("vector", lambda e: e.scalar_tensor_tensor(
            rows[:, T:2 * T], rows[:, T:2 * T], 1.0 / D, rows[:, 2 * T:3 * T],
            ALU.mult, ALU.subtract), sv, 1)
        ph.wait("scalar", sv, t3)
        t4 = ph.do("scalar", lambda e: e.activation(
            rows[:, 2 * T:3 * T], rows[:, T:2 * T], AF.Ln,
            bias=cst.cval(EPS, 1)), ss, 1)
        ph.wait("scalar", ss, t4)
        t5 = ph.do("scalar", lambda e: e.activation(
            rows[:, T:2 * T], rows[:, 2 * T:3 * T], AF.Exp, scale=-0.5), ss, 1)
        if extra_scale is not None:
            ph.wait("vector", ss, t5)
            t5v = ph.do("vector", lambda e: e.tensor_single_scalar(
                rows[:, T:2 * T], rows[:, T:2 * T], float(extra_scale),
                ALU.mult), sv, 1)
            ph.wait("vector", sv, t5v)
        else:
            ph.wait("vector", ss, t5)
        t6 = ph.do("vector", lambda e: e.tensor_copy(
            rows_bf[:, 0:T], rows[:, 0:T]), sv, 1)
        t7 = ph.do("vector", lambda e: e.tensor_copy(
            rows_bf[:, T:2 * T], rows[:, T:2 * T]), sv, 1)
        ph.wait("tensor", sv, t7)
        ph.emit()
    with contextlib.ExitStack() as pst:
        pbc = [kb.ps("ln_pbc", [128, 512], F32, st=pst) for _ in range(4)]
        ev_t = {}
        for which, dst in ((0, mbc), (1, rbc)):
            for c in range(kb.TC):
                bi = (which * kb.TC + c) % 4
                if (which * kb.TC + c) >= 4:
                    ph.wait("tensor", sv, ev_t[which * kb.TC + c - 4])
                mt = ph.do("tensor", lambda e, which=which, c=c, bi=bi: e.matmul(
                    pbc[bi][:], cst.ones[0:1, :],
                    rows_bf[:, which * T + c * 512: which * T + (c + 1) * 512],
                    start=True, stop=True), stt, 1)
                ph.wait("vector", stt, mt)
                ev_t[which * kb.TC + c] = ph.do(
                    "vector", lambda e, dst=dst, c=c, bi=bi: e.tensor_copy(
                        dst[:, c * 512:(c + 1) * 512], pbc[bi][:]), sv, 1)
        last_apply = []
        ph.wait("vector", stt, sq_ticks[-1])
        for j in range(DT):
            ph.wait("vector", sv, ev_t[2 * kb.TC - 1])
            ta = ph.do("vector", lambda e, j=j: e.tensor_tensor(
                xn[:, j * T:(j + 1) * T], xb[:, j * T:(j + 1) * T], mbc[:],
                ALU.subtract), sv, 1)
            ph.wait("vector", sv, ta)
            tb = ph.do("vector", lambda e, j=j: e.tensor_tensor(
                xn[:, j * T:(j + 1) * T], xn[:, j * T:(j + 1) * T], rbc[:],
                ALU.mult), sv, 1)
            if gains is not None:
                g_sb, b_sb = gains
                ph.wait("vector", sv, tb)
                tb = ph.do("vector", lambda e, j=j: e.tensor_scalar(
                    xn[:, j * T:(j + 1) * T], xn[:, j * T:(j + 1) * T],
                    g_sb[:, j:j + 1], b_sb[:, j:j + 1], ALU.mult, ALU.add), sv, 1)
            last_apply.append(tb)
        ph.emit()
    lst.close()
    return last_apply


def emit_proj(kb, ph, sub, xn, w_dram, M_out, evac, n_k=DT, xn_stride=None,
              wait_before=None, kslices=None):
    T = kb.T
    dg = kb.sem("dgp")
    stt = kb.sem("t")
    wsb = kb.sb("proj_w", [128, n_k * M_out], BF16, st=sub)
    wticks = []
    for k in range(n_k):
        t0 = ph.do("gpsimd", lambda e, k=k: e.dma_start(
            wsb[:, k * M_out:(k + 1) * M_out], w_dram[k * 128:(k + 1) * 128, :]),
            dg, 16)
        wticks.append(t0)
    bank_gate = {}
    nb = 4
    with contextlib.ExitStack() as pst:
        pp = [kb.ps("proj_pp", [128, 512], F32, st=pst) for _ in range(nb)]
        gi = 0
        for o in range(M_out // 128):
            for c in range(kb.TC):
                bi = gi % nb
                if gi >= nb:
                    s_, t_ = bank_gate[gi - nb]
                    ph.wait("tensor", s_, t_)
                if gi == 0:
                    ph.wait("tensor", dg, wticks[-1])
                    if wait_before is not None:
                        for s_, t_ in wait_before:
                            ph.wait("tensor", s_, t_)
                for k in range(n_k):
                    if kslices is not None:
                        src, off = kslices[k]
                        rhs = lambda e, k=k, c=c, src=src, off=off: src[
                            :, off + c * 512: off + (c + 1) * 512]
                    else:
                        rhs = lambda e, k=k, c=c: xn[:, k * T + c * 512:
                                                     k * T + (c + 1) * 512]
                    tick = ph.do("tensor", lambda e, k=k, o=o, c=c, bi=bi,
                                 rhs=rhs: e.matmul(
                        pp[bi][:], wsb[:, k * M_out + o * 128:
                                       k * M_out + (o + 1) * 128],
                        rhs(e), start=(k == 0), stop=(k == n_k - 1)),
                        stt if k == n_k - 1 else None)
                bank_gate[gi] = evac(o, c, pp[bi], ph, (stt, tick))
                gi += 1
        ph.emit()



def emit_qkv(kb, cst, ph, sub, xn, projs, wait_ln):
    T = kb.T
    dg = kb.sem("dwq")
    dqo = [kb.sem("dqo0"), kb.sem("dqo1")]
    stt = kb.sem("t")
    ss = kb.sem("s")
    sv = kb.sem("v")
    M_out = D

    rope_any = any(pr.get("post") in ("rope0", "ropeI", "elu1_ropeI")
                   for pr in projs)
    with contextlib.ExitStack() as pst:
        if rope_any:
            cosT = kb.sb("qkv_cos", [128, T], BF16, st=pst)
            sinT = kb.sb("qkv_sin", [128, T], BF16, st=pst)
            permM = kb.sb("qkv_perm", [128, 128], BF16, st=pst)
            drt = kb.sem("drope")
            style = [pr.get("post") for pr in projs
                     if pr.get("post") in ("rope0", "ropeI", "elu1_ropeI")][0]
            cn = ("cos0", "sin0", "perm0") if style == "rope0" else \
                 ("cosI", "sinI", "permI")
            rt = None
            for dst_, nmm in ((cosT, cn[0]), (sinT, cn[1]), (permM, cn[2])):
                rt = ph.do("sync", lambda e, dst_=dst_, nmm=nmm: e.dma_start(
                    dst_[:], cst.caps[nmm][:]), drt, 16)
            ph.wait("tensor", drt, rt)
            ph.wait("vector", drt, rt)
        praw = kb.sb("qkv_praw", [128, 2 * T], BF16, st=pst)
        pout = kb.sb("qkv_pout", [128, 2 * T], BF16, st=pst)
        tmpa = kb.sb("qkv_tmpa", [128, 2 * T], BF16, st=pst)
        tmpb = kb.sb("qkv_tmpb", [128, 512], BF16, st=pst)
        bsq = kb.sb("qkv_bsq", [128, 512], BF16, st=pst)
        pp = [kb.ps("qkv_pp", [128, 512], F32, st=pst) for _ in range(4)]
        pperm = [kb.ps("qkv_pperm", [128, 512], F32, st=pst) for _ in range(2)]
        pbnd = [kb.ps("qkv_pbnd", [1, 512], F32, st=pst) for _ in range(2)]
        if wait_ln:
            ph.wait("gpsimd", kb.sems["v"], wait_ln[-1])
        wsbs = []
        wticks = []
        for pi, pr in enumerate(projs):
            w = kb.sb("qkv_wp", [128, DT * M_out], BF16, st=pst)
            wsbs.append(w)
            wt = None
            for k in range(DT):
                wt = ph.do("gpsimd", lambda e, k=k, pr=pr, w=w: e.dma_start(
                    w[:, k * M_out:(k + 1) * M_out],
                    pr["w"][k * 128:(k + 1) * 128, :]), dg, 16)
            wticks.append(wt)

        out_ticks = {}
        gidx_dma = {}
        gi = 0
        bank_gate = {}
        gt = 0
        prev_elu_t3 = None
        bnd_hist = []
        sq_prev = None
        tmpb_prev = None
        for pi, pr in enumerate(projs):
            wsb = wsbs[pi]
            for o in range(DT):
                par = gt % 2
                if gt >= 2:
                    prev = gidx_dma[gt - 2]
                    ph.wait("scalar", dqo[par], prev)
                    ph.wait("vector", dqo[par], prev)
                evac_ticks = []
                for c in range(kb.TC):
                    bi = gi % 4
                    if gi >= 4:
                        s_, t_ = bank_gate[gi - 4]
                        ph.wait("tensor", s_, t_)
                    if gi == 0:
                        for wl in wait_ln:
                            ph.wait("tensor", kb.sems["v"], wl)
                    if c == 0:
                        ph.wait("tensor", dg, wticks[-1])
                    for k in range(DT):
                        mt = ph.do("tensor", lambda e, k=k, o=o, c=c, bi=bi,
                                   wsb=wsb: e.matmul(
                            pp[bi][:],
                            wsb[:, k * M_out + o * 128:k * M_out + (o + 1) * 128],
                            xn[:, k * T + c * 512:k * T + (c + 1) * 512],
                            start=(k == 0), stop=(k == DT - 1)),
                            stt if k == DT - 1 else None)
                    ph.wait("scalar", stt, mt)
                    bias_ap = pr.get("bias")
                    if bias_ap is not None:
                        ev = ph.do("scalar", lambda e, c=c, par=par, bi=bi, o=o,
                                   bias_ap=bias_ap: e.activation(
                            praw[:, par * T + c * 512:par * T + (c + 1) * 512],
                            pp[bi][:], AF.Identity, bias=bias_ap[:, o:o + 1]), ss, 1)
                    else:
                        ev = ph.do("scalar", lambda e, c=c, par=par, bi=bi:
                                   e.activation(
                            praw[:, par * T + c * 512:par * T + (c + 1) * 512],
                            pp[bi][:], AF.Identity), ss, 1)
                    bank_gate[gi] = (ss, ev)
                    evac_ticks.append(ev)
                    gi += 1
                post = pr.get("post")
                src_buf, src_off = praw, par * T
                if post == "elu1_ropeI":
                    ph.wait("vector", ss, evac_ticks[-1])
                    if prev_elu_t3 is not None:
                        ph.wait("vector", sv, prev_elu_t3)
                    t1 = ph.do("vector", lambda e, par=par: e.tensor_single_scalar(
                        tmpa[:, 0:T], praw[:, par * T:par * T + T], 0.0,
                        ALU.min), sv, 1)
                    ph.wait("scalar", sv, t1)
                    t2 = ph.do("scalar", lambda e: e.activation(
                        tmpa[:, T:2 * T], tmpa[:, 0:T], AF.Exp), ss, 1)
                    ph.wait("scalar", ss, t2)
                    t2b = ph.do("scalar", lambda e, par=par: e.activation(
                        tmpa[:, 0:T], praw[:, par * T:par * T + T], AF.Relu),
                        ss, 1)
                    ph.wait("vector", ss, t2b)
                    t3 = ph.do("vector", lambda e, par=par: e.tensor_tensor(
                        praw[:, par * T:par * T + T], tmpa[:, 0:T],
                        tmpa[:, T:2 * T], ALU.add), sv, 1)
                    prev_elu_t3 = t3
                    ph.wait("tensor", sv, t3)
                    ph.wait("vector", sv, t3)
                if post in ("rope0", "ropeI", "elu1_ropeI"):
                    rope_ticks = {}
                    pmts = {}
                    for c in range(kb.TC):
                        pi2 = c % 2
                        ph.wait("tensor", ss, evac_ticks[c])
                        if c >= 2:
                            ph.wait("tensor", sv, rope_ticks[c - 2])
                        pmt = ph.do("tensor", lambda e, c=c, pi2=pi2, par=par,
                                    permM=permM: e.matmul(
                            pperm[pi2][:], permM[:],
                            praw[:, par * T + c * 512:par * T + (c + 1) * 512],
                            start=True, stop=True), stt, 1)
                        pmts[c] = pmt
                        ph.wait("vector", ss, evac_ticks[c])
                        t1 = ph.do("vector", lambda e, c=c, par=par, cosT=cosT:
                                   e.tensor_tensor(
                            tmpa[:, par * T + c * 512:par * T + (c + 1) * 512],
                            praw[:, par * T + c * 512:par * T + (c + 1) * 512],
                            cosT[:, c * 512:(c + 1) * 512], ALU.mult), sv, 1)
                        ph.wait("vector", stt, pmt)
                        if tmpb_prev is not None:
                            ph.wait("vector", sv, tmpb_prev)
                        t2 = ph.do("vector", lambda e, c=c, pi2=pi2, sinT=sinT:
                                   e.tensor_tensor(
                            tmpb[:], pperm[pi2][:],
                            sinT[:, c * 512:(c + 1) * 512], ALU.mult), sv, 1)
                        ph.wait("vector", sv, t2)
                        t3 = ph.do("vector", lambda e, c=c, par=par:
                                   e.tensor_tensor(
                            pout[:, par * T + c * 512:par * T + (c + 1) * 512],
                            tmpa[:, par * T + c * 512:par * T + (c + 1) * 512],
                            tmpb[:], ALU.add), sv, 1)
                        rope_ticks[c] = t3
                        tmpb_prev = t3
                    src_buf, src_off = pout, par * T
                    done_sem, done_tick = sv, rope_ticks[kb.TC - 1]
                elif post == "elu1":
                    ph.wait("vector", ss, evac_ticks[-1])
                    if prev_elu_t3 is not None:
                        ph.wait("vector", sv, prev_elu_t3)
                    t1 = ph.do("vector", lambda e, par=par: e.tensor_single_scalar(
                        tmpa[:, 0:T], praw[:, par * T:par * T + T], 0.0,
                        ALU.min), sv, 1)
                    ph.wait("scalar", sv, t1)
                    t2 = ph.do("scalar", lambda e: e.activation(
                        tmpa[:, T:2 * T], tmpa[:, 0:T], AF.Exp), ss, 1)
                    ph.wait("scalar", ss, t2)
                    t2b = ph.do("scalar", lambda e, par=par: e.activation(
                        tmpa[:, 0:T], praw[:, par * T:par * T + T], AF.Relu),
                        ss, 1)
                    ph.wait("vector", ss, t2b)
                    t3 = ph.do("vector", lambda e, par=par: e.tensor_tensor(
                        pout[:, par * T:par * T + T], tmpa[:, 0:T],
                        tmpa[:, T:2 * T], ALU.add), sv, 1)
                    prev_elu_t3 = t3
                    src_buf, src_off = pout, par * T
                    done_sem, done_tick = sv, t3
                else:
                    done_sem, done_tick = ss, evac_ticks[-1]
                if pr.get("bound") is not None:
                    srow, col0 = pr["bound"]
                    for c in range(kb.TC):
                        ph.wait("scalar", done_sem, done_tick)
                        if sq_prev is not None:
                            ph.wait("scalar", stt, sq_prev)
                        sq = ph.do("scalar", lambda e, c=c, src_buf=src_buf,
                                   src_off=src_off: e.activation(
                            bsq[:], src_buf[:, src_off + c * 512:
                                            src_off + (c + 1) * 512],
                            AF.Square), ss, 1)
                        for side in range(2):
                            bb = (c * 2 + side) % 2
                            ph.wait("tensor", ss, sq)
                            if len(bnd_hist) >= 2:
                                s_, t_ = bnd_hist[-2]
                                ph.wait("tensor", s_, t_)
                            tp = (0, 0) if side == 0 else (64, 0)
                            bm = ph.do("tensor", lambda e, side=side, bb=bb,
                                       tp=tp: e.matmul(
                                pbnd[bb][:],
                                cst.ones[side * 64:(side + 1) * 64, 0:1],
                                bsq[side * 64:(side + 1) * 64, :],
                                start=True, stop=True, tile_position=tp),
                                stt, 1)
                            sq_prev = bm
                            ph.wait("vector", stt, bm)
                            cell = col0 + (o * 2 + side) * kb.TC + c
                            bt = ph.do("vector", lambda e, bb=bb, cell=cell,
                                       srow=srow: e.tensor_reduce(
                                srow[:, cell:cell + 1], pbnd[bb][:], AX,
                                ALU.max), sv, 1)
                            bnd_hist.append((sv, bt))
                    done_sem, done_tick = sv, bt
                ph.wait("sync", done_sem, done_tick)
                ot = ph.do("sync", lambda e, o=o, pr=pr, src_buf=src_buf,
                           src_off=src_off: e.dma_start(
                    pr["dst"][o * 128:(o + 1) * 128, :],
                    src_buf[:, src_off:src_off + T]), dqo[par], 16)
                out_ticks[(pr["name"], o)] = (dqo[par], ot)
                gidx_dma[gt] = ot
                gt += 1
        ph.emit()
    return out_ticks


SCALE_ATTN = 0.125


def emit_bounds_final(kb, cst, ph, srow, bias_tiles, scale):
    TC = kb.TC
    sv = kb.sem("v")
    ss = kb.sem("s")
    stt = kb.sem("t")
    t = None
    for h in range(16):
        t = ph.do("vector", lambda e, h=h: e.tensor_reduce(
            srow[:, 256 + h:256 + h + 1], srow[:, h * TC:(h + 1) * TC], AX,
            ALU.max), sv, 1)
        t = ph.do("vector", lambda e, h=h: e.tensor_reduce(
            srow[:, 288 + h:288 + h + 1],
            srow[:, 16 * TC + h * TC:16 * TC + (h + 1) * TC], AX,
            ALU.max), sv, 1)
    ph.wait("vector", sv, t)
    t = ph.do("vector", lambda e: e.tensor_tensor(
        srow[:, 320:336], srow[:, 256:272], srow[:, 288:304], ALU.mult), sv, 1)
    ph.wait("scalar", sv, t)
    t2 = ph.do("scalar", lambda e: e.activation(
        srow[:, 336:352], srow[:, 320:336], AF.Sqrt), ss, 1)
    ph.wait("vector", ss, t2)
    t3 = ph.do("vector", lambda e: e.tensor_single_scalar(
        srow[:, 352:368], srow[:, 336:352], -float(scale), ALU.mult), sv, 1)
    ph.wait("vector", sv, t3)
    with contextlib.ExitStack() as pst:
        rbf = kb.sb("bnd_rbf", [1, 16], BF16, st=pst)
        pb = kb.ps("bnd_pb", [128, 16], F32, st=pst)
        t4 = ph.do("vector", lambda e: e.tensor_copy(rbf[:], srow[:, 352:368]),
                   sv, 1)
        ph.wait("tensor", sv, t4)
        t5 = ph.do("tensor", lambda e: e.matmul(
            pb[:], cst.ones[0:1, :], rbf[:], start=True, stop=True), stt, 1)
        ph.wait("vector", stt, t5)
        t6 = ph.do("vector", lambda e: e.tensor_copy(bias_tiles[:], pb[:]),
                   sv, 1)
        ph.emit()
    return t6


def emit_attn_core(kb, cst, ph, scrQ, scrK, scrV, scrO, bias_tiles, bias_rdy,
                   scale, wait_first):
    T = kb.T
    NKT = T // 128
    NG = NKT // 2
    dq = kb.sem("daq")
    dk = kb.sem("dak")
    dv = kb.sem("dav")
    dot = kb.sem("dot")
    stt = kb.sem("t")
    ss = kb.sem("s")
    sv = kb.sem("v")
    with contextlib.ExitStack() as pst:
        qt = kb.sb("at_qt", [128, T], BF16, st=pst)
        kt_ = kb.sb("at_kt", [128, T], BF16, st=pst)
        vt = kb.sb("at_vt", [128, T], BF16, st=pst) if scrV is not scrK else None
        VA = kb.sb("at_VA", [128, NKT * 65], BF16, st=pst)
        VB = kb.sb("at_VB", [128, NKT * 65], BF16, st=pst)
        P = kb.sb("at_P", [128, NKT * 1024], BF16, st=pst)
        bp = kb.sb("at_bp", [128, 8], F32, st=pst)
        OT = kb.sb("at_OT", [128, T], BF16, st=pst)
        rrec = kb.sb("at_rrec", [1, 2 * 512], F32, st=pst)
        rbf = kb.sb("at_rbf", [1, 2 * 512], BF16, st=pst)
        rnbc = kb.sb("at_rnbc", [64, 2 * 512], BF16, st=pst)
        otmp = kb.sb("at_otmp", [64, 2 * 512], BF16, st=pst)
        ps2 = [kb.ps("at_ps%d" % p_, [128, 1024], F32, st=pst)
               for p_ in range(2)]
        psOA = kb.ps("at_psOA", [65, 512], F32, st=pst)
        psOB = kb.ps("at_psOB", [65, 512], F32, st=pst)
        pnrm = kb.ps("at_pnrm", [64, 512], F32, st=pst)
        ptr = kb.ps("at_ptr", [128, 128], BF16, st=pst)

        for eng in ("sync", "tensor", "scalar", "vector"):
            for s_, t_ in wait_first:
                ph.wait(eng, s_, t_)
        ph.wait("scalar", kb.sems["v"], bias_rdy)
        ph.wait("vector", kb.sems["v"], bias_rdy)
        bt3 = bias_tiles.rearrange("p (a b) -> p a b", b=2)
        bp_t = ph.do("vector", lambda e: e.tensor_tensor(
            bp[:], bt3[:, :, 0], bt3[:, :, 1], ALU.min), kb.sems["v"], 1)
        ph.wait("scalar", kb.sems["v"], bp_t)

        prev_last_mm = None
        prev_ot_dma = None
        prev_vready = None
        prev_evt = {0: None, 1: None}
        prev_ot2 = {0: None, 1: None}
        last_ot_tick = None
        for hp in range(NHP):
            hA, hB = 2 * hp, 2 * hp + 1
            if prev_last_mm is not None:
                ph.wait("sync", stt, prev_last_mm)
                ph.wait("vector", stt, prev_last_mm)
                ph.wait("scalar", stt, prev_last_mm)
                ph.wait("tensor", sv, prev_vready)
                ph.wait("vector", dot, prev_ot_dma)
            lq = ph.do("sync", lambda e, hp=hp: e.dma_start(
                qt[:], scrQ[hp * 128:(hp + 1) * 128, :]), dq, 16)
            lk = ph.do("sync", lambda e, hp=hp: e.dma_start(
                kt_[:], scrK[hp * 128:(hp + 1) * 128, :]), dk, 16)
            if scrV is not scrK:
                lv = ph.do("sync", lambda e, hp=hp: e.dma_start(
                    vt[:], scrV[hp * 128:(hp + 1) * 128, :]), dv, 16)
                lvs = dv
                vt_use = vt
            else:
                lv = lk
                lvs = dk
                vt_use = kt_
            msA = ph.do("vector", lambda e: e.memset(VA[:], 1.0), sv, 1)
            msB = ph.do("vector", lambda e: e.memset(VB[:], 1.0), sv, 1)
            ph.wait("tensor", lvs, lv)
            ev_ticks = {}
            for kt in range(NKT):
                if kt >= 1:
                    ph.wait("tensor", sv, ev_ticks[kt - 1])
                trt = ph.do("tensor", lambda e, kt=kt, vt_use=vt_use:
                            e.transpose(
                    ptr[:], vt_use[:, kt * 128:(kt + 1) * 128], cst.ident[:]),
                    stt, 1)
                ph.wait("vector", stt, trt)
                if kt == 0:
                    ph.wait("vector", sv, msB)
                ph.do("vector", lambda e, kt=kt: e.tensor_copy(
                    VA[:, kt * 65:kt * 65 + 64], ptr[:, 0:64]), sv, 1)
                ev_ticks[kt] = ph.do("vector", lambda e, kt=kt: e.tensor_copy(
                    VB[:, kt * 65:kt * 65 + 64], ptr[:, 64:128]), sv, 1)
            vready = ev_ticks[NKT - 1]
            prev_vready = vready
            ph.wait("tensor", dq, lq)
            ph.wait("tensor", dk, lk)
            ph.wait("tensor", sv, vready)
            pv_last_qb = None
            ev_qb = None
            prev_exp = None
            prev_pv = None
            ot2_ticks = []
            for qb in range(kb.TC):
                exp_ticks = {}
                pv_ticks = {}

                def emit_pv(kt):
                    for side in range(2):
                        VX = VA if side == 0 else VB
                        psO = psOA if side == 0 else psOB
                        ph.wait("tensor", ss, exp_ticks[kt])
                        if ev_qb is not None and kt == 0:
                            for s_, t_ in ev_qb:
                                ph.wait("tensor", s_, t_)
                        pv_ticks[(side, kt)] = ph.do(
                            "tensor", lambda e, kt=kt, VX=VX, side=side,
                            psO=psO: e.matmul(
                                psO[:], VX[:, kt * 65:(kt + 1) * 65],
                                P[:, kt * 1024 + side * 512:
                                  kt * 1024 + side * 512 + 512],
                                start=(kt == 0), stop=(kt == NKT - 1),
                                skip_group_check=True),
                            stt, 1)

                for kt in range(NKT):
                    par = kt % 2
                    sc_t = {}
                    for side in range(2):
                        tp = (0, 0) if side == 0 else (64, 0)
                        p0 = side * 64
                        if kt >= 2:
                            ph.wait("tensor", ss, exp_ticks[kt - 2])
                        elif prev_exp is not None:
                            ph.wait("tensor", ss, prev_exp[NKT - 2 + kt])
                        sc_t[side] = ph.do(
                            "tensor", lambda e, kt=kt, par=par, tp=tp,
                            p0=p0, side=side, qb=qb: e.matmul(
                                ps2[par][:, side * 512:side * 512 + 512],
                                kt_[p0:p0 + 64, kt * 128:(kt + 1) * 128],
                                qt[p0:p0 + 64, qb * 512:(qb + 1) * 512],
                                start=True, stop=True, tile_position=tp),
                            stt, 1)
                    if kt >= 1:
                        emit_pv(kt - 1)
                    ph.wait("scalar", stt, sc_t[1])
                    if prev_pv is not None:
                        ph.wait("scalar", stt, prev_pv[(1, kt)])
                    exp_ticks[kt] = ph.do(
                        "scalar", lambda e, kt=kt, par=par, hp=hp:
                        e.activation(
                            P[:, kt * 1024:(kt + 1) * 1024],
                            ps2[par][:], AF.Exp, scale=float(scale),
                            bias=bp[:, hp:hp + 1]), ss, 1)
                emit_pv(NKT - 1)
                prev_exp = exp_ticks
                prev_pv = pv_ticks
                pv_last_qb = max(pv_ticks[(0, NKT - 1)], pv_ticks[(1, NKT - 1)])
                ev_qb = []
                for side in range(2):
                    psO = psOA if side == 0 else psOB
                    p0 = side * 64
                    so = side * 512
                    ph.wait("vector", stt, pv_ticks[(side, NKT - 1)])
                    ph.wait("scalar", stt, pv_ticks[(side, NKT - 1)])
                    rt = ph.do("scalar", lambda e, psO=psO, so=so:
                               e.activation(rrec[:, so:so + 512],
                                            psO[64:65, :], AF.Ln), ss, 1)
                    ph.wait("scalar", ss, rt)
                    ct = ph.do("scalar", lambda e, so=so: e.activation(
                        rbf[:, so:so + 512], rrec[:, so:so + 512],
                        AF.Exp, scale=-1.0), ss, 1)
                    ph.wait("tensor", ss, ct)
                    if prev_evt[side] is not None:
                        ph.wait("tensor", sv, prev_evt[side])
                    if prev_evt[1 - side] is not None:
                        ph.wait("tensor", sv, prev_evt[1 - side])
                    bct = ph.do("tensor", lambda e, so=so: e.matmul(
                        pnrm[:], cst.ones[0:1, 0:64], rbf[:, so:so + 512],
                        start=True, stop=True), stt, 1)
                    ph.wait("vector", stt, bct)
                    if prev_ot2[side] is not None:
                        ph.wait("vector", sv, prev_ot2[side])
                    evt = ph.do("vector", lambda e, so=so: e.tensor_copy(
                        rnbc[:, so:so + 512], pnrm[:]), sv, 1)
                    prev_evt[side] = evt
                    o1 = ph.do("vector", lambda e, psO=psO, so=so:
                               e.tensor_copy(otmp[:, so:so + 512],
                                             psO[0:64, :]), sv, 1)
                    ph.wait("vector", sv, o1)
                    o2 = ph.do("vector", lambda e, p0=p0, so=so, qb=qb:
                               e.tensor_tensor(
                        OT[p0:p0 + 64, qb * 512:(qb + 1) * 512],
                        otmp[:, so:so + 512], rnbc[:, so:so + 512],
                        ALU.mult), sv, 1)
                    prev_ot2[side] = o2
                    ot2_ticks.append(o2)
                    ev_qb.append((sv, o1))
                    ev_qb.append((ss, rt))
            ph.wait("sync", sv, ot2_ticks[-1])
            last_ot_tick = ph.do("sync", lambda e, hp=hp: e.dma_start(
                scrO[hp * 128:(hp + 1) * 128, :], OT[:]), dot, 16)
            prev_ot_dma = last_ot_tick
            prev_last_mm = pv_last_qb
        ph.emit()
    return (dot, last_ot_tick), (stt, prev_last_mm)


def emit_outproj_residual(kb, cst, ph, srcO, w_dram, bias_sb, xin, xout,
                          wait_first, scale=None, extra_add=None,
                          xin_bf=False):
    T = kb.T
    dso = kb.sem("dopo")
    dg = kb.sem("dopw")
    if xin_bf:
        dxi = [kb.sem("gxi0"), kb.sem("gxi1")]
    else:
        dxi = [kb.sem("dxi0"), kb.sem("dxi1")]
    dxi2 = [kb.sem("dxj0"), kb.sem("dxj1")]
    dxo = [kb.sem("dxo0"), kb.sem("dxo1")]
    stt = kb.sem("t")
    ss = kb.sem("s")
    sv = kb.sem("v")
    M_out = D
    with contextlib.ExitStack() as pst:
        ot = kb.sb("op_ot", [128, DT * T], BF16, st=pst)
        wsb = kb.sb("op_w", [128, DT * M_out], BF16, st=pst)
        xi = kb.sb("op_xi", [128, 2 * T], F32, st=pst)
        xi2 = kb.sb("op_xi2", [128, 2 * T], F32, st=pst) if extra_add else None
        xo = kb.sb("op_xo", [128, 2 * T], F32, st=pst)
        tf = kb.sb("op_tf", [128, 2 * 512], F32, st=pst)
        pp = [kb.ps("op_pp", [128, 512], F32, st=pst) for _ in range(4)]
        for eng in ("sync", "gpsimd", "tensor", "scalar", "vector"):
            for s_, t_ in wait_first:
                ph.wait(eng, s_, t_)
        lo = []
        for k in range(DT):
            lo.append(ph.do("sync", lambda e, k=k: e.dma_start(
                ot[:, k * T:(k + 1) * T], srcO[k * 128:(k + 1) * 128, :]),
                dso, 16))
        wt = None
        for k in range(DT):
            wt = ph.do("gpsimd", lambda e, k=k: e.dma_start(
                wsb[:, k * M_out:(k + 1) * M_out],
                w_dram[k * 128:(k + 1) * 128, :]), dg, 16)
        xi_t = {}
        xi2_t = {}
        xo_dma = {}
        tt_last = {}
        gi = 0
        bank_gate = {}
        tf_gate = {}
        for o in range(DT):
            par = o % 2
            if o >= 2:
                ph.wait("sync", sv, tt_last[o - 2])
                if xin_bf:
                    ph.wait("gpsimd", sv, tt_last[o - 2])
            xiq = "gpsimd" if xin_bf else "sync"
            xi_t[o] = ph.do(xiq, lambda e, o=o, par=par: e.dma_start(
                xi[:, par * T:(par + 1) * T], xin[o * 128:(o + 1) * 128, :]),
                dxi[par], 16)
            if extra_add is not None:
                xi2_t[o] = ph.do("sync", lambda e, o=o, par=par: e.dma_start(
                    xi2[:, par * T:(par + 1) * T],
                    extra_add[o * 128:(o + 1) * 128, :]), dxi2[par], 16)
            for c in range(kb.TC):
                bi = gi % 4
                if gi >= 4:
                    s_, t_ = bank_gate[gi - 4]
                    ph.wait("tensor", s_, t_)
                if gi == 0:
                    ph.wait("tensor", dg, wt)
                    ph.wait("tensor", dso, lo[-1])
                for k in range(DT):
                    mt = ph.do("tensor", lambda e, k=k, o=o, c=c, bi=bi:
                               e.matmul(
                        pp[bi][:],
                        wsb[:, k * M_out + o * 128:k * M_out + (o + 1) * 128],
                        ot[:, k * T + c * 512:k * T + (c + 1) * 512],
                        start=(k == 0), stop=(k == DT - 1)),
                        stt if k == DT - 1 else None)
                ph.wait("scalar", stt, mt)
                if gi >= 2:
                    ph.wait("scalar", sv, tf_gate[gi - 2])
                kw = {}
                if bias_sb is not None:
                    kw["bias"] = bias_sb[:, o:o + 1]
                if scale is not None:
                    kw["scale"] = float(scale)
                ev = ph.do("scalar", lambda e, bi=bi, gi=gi, kw=kw: e.activation(
                    tf[:, (gi % 2) * 512:(gi % 2 + 1) * 512], pp[bi][:],
                    AF.Identity, **kw), ss, 1)
                bank_gate[gi] = (ss, ev)
                ph.wait("vector", ss, ev)
                ph.wait("vector", dxi[par], xi_t[o])
                if extra_add is not None:
                    ph.wait("vector", dxi2[par], xi2_t[o])
                if o >= 2 and c == 0:
                    ph.wait("vector", dxo[par], xo_dma[o - 2])
                if extra_add is None:
                    tt = ph.do("vector", lambda e, c=c, par=par, gi=gi: e.tensor_tensor(
                        xo[:, par * T + c * 512:par * T + (c + 1) * 512],
                        tf[:, (gi % 2) * 512:(gi % 2 + 1) * 512],
                        xi[:, par * T + c * 512:par * T + (c + 1) * 512],
                        ALU.add), sv, 1)
                else:
                    tt = ph.do("vector", lambda e, c=c, par=par, gi=gi: e.tensor_tensor(
                        xo[:, par * T + c * 512:par * T + (c + 1) * 512],
                        tf[:, (gi % 2) * 512:(gi % 2 + 1) * 512],
                        xi[:, par * T + c * 512:par * T + (c + 1) * 512],
                        ALU.add), sv, 1)
                    ph.wait("vector", sv, tt)
                    tt = ph.do("vector", lambda e, c=c, par=par: e.tensor_tensor(
                        xo[:, par * T + c * 512:par * T + (c + 1) * 512],
                        xo[:, par * T + c * 512:par * T + (c + 1) * 512],
                        xi2[:, par * T + c * 512:par * T + (c + 1) * 512],
                        ALU.add), sv, 1)
                tf_gate[gi] = tt
                gi += 1
            tt_last[o] = tt
            ph.wait("sync", sv, tt)
            xo_dma[o] = ph.do("sync", lambda e, o=o, par=par: e.dma_start(
                xout[o * 128:(o + 1) * 128, :], xo[:, par * T:(par + 1) * T]),
                dxo[par], 16)
        ph.emit()
    return [(dxo[0], xo_dma[DT - 2]), (dxo[1], xo_dma[DT - 1])]


L0A_CONSTS = ["ident", "ones", "cos0", "sin0", "perm0"]


def decl_l0_attn(nc, T, px=""):
    w = {}
    w["wq"] = nc.dram_tensor(px + "wq", [D, D], F32, kind="ExternalInput").ap()
    w["wkv"] = nc.dram_tensor(px + "wkv", [D, D], F32, kind="ExternalInput").ap()
    w["wout"] = nc.dram_tensor(px + "wout", [D, D], F32,
                               kind="ExternalInput").ap()
    w["bout_t"] = nc.dram_tensor(px + "bout_t", [128, DT], F32,
                                 kind="ExternalInput").ap()
    return w


def sub_l0_attn(kb, cst, caps, w, scr, xin, xout):
    T = kb.T
    ph = Ph(kb)
    ds = kb.sem("dc")
    scrQ, scrK, scrO = scr["q"], scr["k"], scr["o"]
    wq, wkv, wout = w["wq"], w["wkv"], w["wout"]
    with contextlib.ExitStack() as st0:
        srow = kb.sb("bstats", [1, 640], F32, st=st0)
        bias_tiles = kb.sb("battn", [128, 16], F32, st=st0)
        bout_sb = kb.sb("bout", [128, DT], F32, st=st0)
        t_b = ph.do("sync", lambda e: e.dma_start(bout_sb[:], w["bout_t"][:]),
                    ds, 16)
        for eng in ("tensor", "scalar", "vector", "gpsimd"):
            ph.wait(eng, ds, t_b)
        with kb.nc.named_scope("lnqkv"), contextlib.ExitStack() as s1:
            xn = kb.sb("xn", [128, DT * T], BF16, st=s1)
            ln_ticks = emit_ln(kb, ph, s1, xin, xn, cst)
            qkv_ticks = emit_qkv(kb, cst, ph, s1, xn, [
                dict(name="q", w=wq, post="rope0", dst=scrQ,
                     bound=(srow, 0)),
                dict(name="kv", w=wkv, post="rope0", dst=scrK,
                     bound=(srow, 16 * kb.TC)),
            ], wait_ln=ln_ticks)
        bias_rdy = emit_bounds_final(kb, cst, ph, srow, bias_tiles, SCALE_ATTN)
        wfm = {}
        for s_, t_ in qkv_ticks.values():
            wfm[s_] = max(wfm.get(s_, 0), t_)
        with kb.nc.named_scope("core"):
            (o_sem, o_tick), _ = emit_attn_core(
                kb, cst, ph, scrQ, scrK, scrK, scrO, bias_tiles, bias_rdy,
                SCALE_ATTN, wait_first=list(wfm.items()))
        fin = emit_outproj_residual(kb, cst, ph, scrO, wout, bout_sb, xin,
                                    xout, wait_first=[(o_sem, o_tick)])
    return fin


SCR_SHAPES = {
    "q": ([D, "T"], BF16), "k": ([D, "T"], BF16), "v": ([D, "T"], BF16),
    "o": ([D, "T"], BF16), "s": ([D, "T"], BF16), "s2": ([D, "T"], F32),
    "a": (["NTB", 4096, "TB"], BF16),
    "xz": ([4096, "T"], BF16), "xc": ([2048, "T"], BF16),
    "dts": ([2048, "T"], BF16), "yz": ([2048, "T"], BF16),
    "bc": ([32, "T"], BF16),
    "w1q": ([D, 4096], BF16), "w2q": ([4096, D], BF16),
}


def make_scratch(nc, T, names):
    scr = {}
    for n in names:
        shp, dt_ = SCR_SHAPES[n]
        shp = [T if s == "T" else
               (max(1, T // 1024) if s == "NTB" else
                (min(1024, T) if s == "TB" else s)) for s in shp]
        scr[n] = nc.dram_tensor("scr_" + n, shp, dt_, kind="Internal").ap()
    return scr


def standalone(decl, sub, cnames, T, scr_names):
    nc = bass.Bass("TRN2", target_bir_lowering=False)
    xin = nc.dram_tensor("xin", [D, T], F32, kind="ExternalInput").ap()
    xout = nc.dram_tensor("xout", [D, T], F32, kind="ExternalOutput").ap()
    w = decl(nc, T, "")
    caps = declare_constants(nc, T, cnames)
    nc.const_names = cnames
    scr = make_scratch(nc, T, scr_names)
    with contextlib.ExitStack() as st:
        kb = KB(nc, st, T)
        cst = Cst(kb, caps, resident=["ident", "ones"])
        ph = Ph(kb)
        cld = cst.load(kb, ph)
        cldm = {}
        for s_, t_ in cld:
            cldm[s_] = max(cldm.get(s_, 0), t_)
        for eng in ENGS:
            for s_, t_ in cldm.items():
                ph.wait(eng, s_, t_)
        ph.emit()
        fin = sub(kb, cst, caps, w, scr, xin, xout)
        phz = Ph(kb)
        for s_, t_ in fin:
            phz.wait("sync", s_, t_)
        phz.do("sync", lambda e: e.nop())
        phz.emit()
    return nc


def build_l0_attn(T):
    return standalone(decl_l0_attn, sub_l0_attn, L0A_CONSTS, T,
                      ["q", "k", "o"])


def emit_mm_big(kb, ph, name, rhs_slice, n_k, w_dram, M_out, n_c, evac,
                og_tiles=2, wmode="f32", wait_rhs=None, col0=0):
    stt = kb.sem("t")
    q = "gpsimd" if wmode in ("f32", "i32") else "sync"
    if q == "gpsimd":
        dwp = [kb.sem("dwm0"), kb.sem("dwm1")]
    else:
        dwp = [kb.sem("dwn0"), kb.sem("dwn1")]
    OG = og_tiles * 128
    n_og = M_out // OG
    nb = og_tiles * n_c
    assert nb <= 4
    nsets = 2
    with contextlib.ExitStack() as pst:
        wog = [kb.sb(f"w{name}", [128, n_k * OG], BF16, st=pst)
               for _ in range(2)]
        pp = [kb.ps(f"pp{name}", [128, 512], F32, st=pst)
              for _ in range(nb * nsets)]
        wr = w_dram.rearrange("(k p) m -> p k m", p=128)
        if wait_rhs:
            for s_, t_ in wait_rhs:
                ph.wait(q, s_, t_)
        mm_last = {}
        ev_hist = {}
        for og in range(n_og):
            par = og % 2
            pset = (og % nsets) * nb
            if og >= 2:
                ph.wait(q, stt, mm_last[og - 2])
            wt = ph.do(q, lambda e, og=og, par=par: e.dma_start(
                wog[par][:].rearrange("p (k m) -> p k m", k=n_k),
                wr[:, :, col0 + og * OG: col0 + (og + 1) * OG]),
                dwp[par], 16)
            ph.wait("tensor", dwp[par], wt)
            if og == 0 and wait_rhs:
                for s_, t_ in wait_rhs:
                    ph.wait("tensor", s_, t_)
            if og >= nsets:
                for s_, t_ in ev_hist[og - nsets]:
                    ph.wait("tensor", s_, t_)
            mmt = {}
            for k in range(n_k):
                for oi in range(og_tiles):
                    for c in range(n_c):
                        bi = pset + oi * n_c + c
                        mm = ph.do("tensor", lambda e, k=k, oi=oi, c=c, bi=bi,
                                   par=par: e.matmul(
                            pp[bi][:],
                            wog[par][:, k * OG + oi * 128:k * OG + (oi + 1) * 128],
                            rhs_slice(k, c),
                            start=(k == 0), stop=(k == n_k - 1)),
                            stt if k == n_k - 1 else None)
                        if k == n_k - 1:
                            mmt[(oi, c)] = mm
            mm_last[og] = mm
            evs = []
            for oi in range(og_tiles):
                for c in range(n_c):
                    bi = pset + oi * n_c + c
                    evs.append(evac(og * og_tiles + oi, c, pp[bi],
                                    (stt, mmt[(oi, c)])))
            ev_hist[og] = evs
        ph.emit()
    return ev_hist[n_og - 1]


def emit_dequant_u8(kb, cst, ph, name, q_dram, scales_dram, scrW, K_in, M,
                    selP, P):
    dqp = [kb.sem("dqq0"), kb.sem("dqq1")]
    dsc = kb.sem("dscq")
    dwo = [kb.sem("dwoq0"), kb.sem("dwoq1")]
    stt = kb.sem("t")
    ss = kb.sem("s")
    sv = kb.sem("v")
    NK = K_in // 128
    with contextlib.ExitStack() as pst:
        ssb = kb.sb(f"dq_s{name}", [P, M], BF16, st=pst)
        qb = [kb.sb(f"dq_q{name}", [128, M], BF16, st=pst) for _ in range(2)]
        sbc = kb.sb(f"dq_bc{name}", [128, M], BF16, st=pst)
        pb = [kb.ps(f"dq_pb{name}", [128, 512], F32, st=pst) for _ in range(2)]
        lt = ph.do("gpsimd", lambda e: e.dma_start(ssb[:], scales_dram[:]),
                   dsc, 16)
        ph.wait("tensor", dsc, lt)
        prev_tt = None
        prev_dma = {0: None, 1: None}
        for k in range(NK):
            par = k % 2
            if prev_dma[par] is not None:
                ph.wait("gpsimd", dwo[par], prev_dma[par])
            qt = ph.do("gpsimd", lambda e, k=k, par=par: e.dma_start(
                qb[par][:], q_dram[k * 128:(k + 1) * 128, :]), dqp[par], 16)
            ev_t = {}
            for c in range(M // 512):
                bi = c % 2
                if c >= 2:
                    ph.wait("tensor", sv, ev_t[c - 2])
                elif prev_tt is not None and c < 2:
                    ph.wait("tensor", sv, prev_tt)
                bm = ph.do("tensor", lambda e, k=k, c=c, bi=bi: e.matmul(
                    pb[bi][:], selP[:, k * 128:(k + 1) * 128],
                    ssb[:, c * 512:(c + 1) * 512], start=True, stop=True),
                    stt, 1)
                ph.wait("vector", stt, bm)
                ev_t[c] = ph.do("vector", lambda e, c=c, bi=bi: e.tensor_copy(
                    sbc[:, c * 512:(c + 1) * 512], pb[bi][:]), sv, 1)
            ph.wait("vector", dqp[par], qt)
            t1 = ph.do("vector", lambda e, par=par: e.tensor_single_scalar(
                qb[par][:], qb[par][:], -128.0, ALU.add), sv, 1)
            ph.wait("vector", sv, t1)
            tt = ph.do("vector", lambda e, par=par: e.tensor_tensor(
                qb[par][:], qb[par][:], sbc[:], ALU.mult), sv, 1)
            prev_tt = tt
            ph.wait("sync", sv, tt)
            prev_dma[par] = ph.do("sync", lambda e, k=k, par=par: e.dma_start(
                scrW[k * 128:(k + 1) * 128, :], qb[par][:]), dwo[par], 16)
        ph.emit()
    return [(dwo[0], prev_dma[0]), (dwo[1], prev_dma[1])]


def emit_ffn(kb, cst, ph, xn, ups, M_hid, down, xin, xout, wait_ln,
             extra_add=None, TBc=2, xin_bf=False, h_src=None):
    T = kb.T
    TBc = min(TBc, kb.TC)
    TB = TBc * 512
    NTB = T // TB
    NH = M_hid // 128
    stt = kb.sem("t")
    ss = kb.sem("s")
    sv = kb.sem("v")
    if xin_bf:
        dxi = [kb.sem("gxi0"), kb.sem("gxi1")]
    else:
        dxi = [kb.sem("fxi0"), kb.sem("fxi1")]
    dxj = [kb.sem("fxj0"), kb.sem("fxj1")]
    dxo = [kb.sem("fxo0"), kb.sem("fxo1")]
    with contextlib.ExitStack() as pst:
        h = kb.sb("ffn_h", [128, NH * TB], BF16, st=pst)
        tmp = kb.sb("ffn_tmp", [128, 2 * 512], BF16, st=pst)
        tf = kb.sb("ffn_tf", [128, 2 * 512], F32, st=pst)
        xi = kb.sb("ffn_xi", [128, 2 * TB], F32, st=pst)
        xi2 = kb.sb("ffn_xi2", [128, 2 * TB], F32, st=pst) if extra_add else None
        xo = kb.sb("ffn_xo", [128, 2 * TB], F32, st=pst)
        fin = []
        up_ev_prev = []
        xi_t = {}
        xi2_t = {}
        xo_dma = {0: None, 1: None}
        h_guard = []
        tmp_guard = {0: None, 1: None}
        tf_guard = {0: None, 1: None}
        xo_guard = {}
        dhs = kb.sem("fhs")
        for tb in range(NTB):
            tpar = tb % 2
            if h_src is not None:
                if tb > 0:
                    for s_, t_ in h_guard:
                        ph.wait("sync", s_, t_)
                else:
                    for s_, t_ in (wait_ln or []):
                        ph.wait("sync", s_, t_)
                ht = None
                for k in range(NH):
                    ht = ph.do("sync", lambda e, k=k, tb=tb: e.dma_start(
                        h[:, k * TB:(k + 1) * TB],
                        h_src[k * 128:(k + 1) * 128, tb * TB:(tb + 1) * TB]),
                        dhs, 16)
                up_ev_prev = [(dhs, ht)] + (h_guard if tb > 0 else [])
                h_guard = []
            for ui, up in enumerate(ups):
                def mk_evac(up=up, ui=ui, tb=tb):
                    def evac(o, c, psum, mmw):
                        ph.wait("scalar", mmw[0], mmw[1])
                        kw = {}
                        if up.get("scale_ap") is not None:
                            kw["scale"] = up["scale_ap"][:, o:o + 1]
                        if up.get("bias_ap") is not None:
                            kw["bias"] = up["bias_ap"][:, o:o + 1]
                        fn = up.get("act") or AF.Identity
                        if up.get("combine") == "mult":
                            par2 = c % 2
                            if tmp_guard[par2] is not None:
                                ph.wait("scalar", sv, tmp_guard[par2])
                            et = ph.do("scalar", lambda e: e.activation(
                                tmp[:, par2 * 512:(par2 + 1) * 512], psum[:],
                                fn, **kw), ss, 1)
                            ph.wait("vector", ss, et)
                            vt = ph.do("vector", lambda e: e.tensor_tensor(
                                h[:, o * TB + c * 512:o * TB + (c + 1) * 512],
                                h[:, o * TB + c * 512:o * TB + (c + 1) * 512],
                                tmp[:, par2 * 512:(par2 + 1) * 512],
                                ALU.mult), sv, 1)
                            tmp_guard[par2] = vt
                            return (sv, vt)
                        else:
                            et = ph.do("scalar", lambda e: e.activation(
                                h[:, o * TB + c * 512:o * TB + (c + 1) * 512],
                                psum[:], fn, **kw), ss, 1)
                            return (ss, et)
                    return evac
                wr = wait_ln if (tb == 0 and ui == 0) else None
                wr = (wr or []) + (h_guard if ui == 0 else list(up_ev_prev))
                ev = emit_mm_big(
                    kb, ph, f"u{ui}", 
                    lambda k, c, tb=tb: xn[:, k * T + tb * TB + c * 512:
                                           k * T + tb * TB + (c + 1) * 512],
                    DT, up["w"], M_hid, TBc, mk_evac(),
                    og_tiles=2, wmode=up.get("wmode", "f32"), wait_rhs=wr)
                up_ev_prev = ev
            h_guard = []
            dn_ev = []

            def devac(o, c, psum, mmw, tb=tb, tpar=tpar):
                ph.wait("scalar", mmw[0], mmw[1])
                gi2 = (o * TBc + c) % 2
                kw = {}
                if down.get("bias_ap") is not None:
                    kw["bias"] = down["bias_ap"][:, o:o + 1]
                if down.get("scale") is not None:
                    kw["scale"] = float(down["scale"])
                if tf_guard[gi2] is not None:
                    ph.wait("scalar", sv, tf_guard[gi2])
                et = ph.do("scalar", lambda e: e.activation(
                    tf[:, gi2 * 512:(gi2 + 1) * 512], psum[:], AF.Identity,
                    **kw), ss, 1)
                if c == 0:
                    xiq = "gpsimd" if xin_bf else "sync"
                    if xo_dma[o % 2] is not None:
                        ph.wait(xiq, dxo[o % 2], xo_dma[o % 2])
                    xi_t[(tb, o)] = ph.do(
                        xiq, lambda e: e.dma_start(
                            xi[:, (o % 2) * TB:(o % 2 + 1) * TB],
                            xin[o * 128:(o + 1) * 128, tb * TB:(tb + 1) * TB]),
                        dxi[o % 2], 16)
                    if extra_add is not None:
                        xi2_t[(tb, o)] = ph.do(
                            "sync", lambda e: e.dma_start(
                                xi2[:, (o % 2) * TB:(o % 2 + 1) * TB],
                                extra_add[o * 128:(o + 1) * 128,
                                          tb * TB:(tb + 1) * TB]),
                            dxj[o % 2], 16)
                ph.wait("vector", ss, et)
                ph.wait("vector", dxi[o % 2], xi_t[(tb, o)])
                if extra_add is not None:
                    ph.wait("vector", dxj[o % 2], xi2_t[(tb, o)])
                if xo_dma[o % 2] is not None:
                    ph.wait("vector", dxo[o % 2], xo_dma[o % 2])
                vt = ph.do("vector", lambda e: e.tensor_tensor(
                    xo[:, (o % 2) * TB + c * 512:(o % 2) * TB + (c + 1) * 512],
                    tf[:, gi2 * 512:(gi2 + 1) * 512],
                    xi[:, (o % 2) * TB + c * 512:(o % 2) * TB + (c + 1) * 512],
                    ALU.add), sv, 1)
                if extra_add is not None:
                    ph.wait("vector", sv, vt)
                    vt = ph.do("vector", lambda e: e.tensor_tensor(
                        xo[:, (o % 2) * TB + c * 512:
                           (o % 2) * TB + (c + 1) * 512],
                        xo[:, (o % 2) * TB + c * 512:
                           (o % 2) * TB + (c + 1) * 512],
                        xi2[:, (o % 2) * TB + c * 512:
                            (o % 2) * TB + (c + 1) * 512],
                        ALU.add), sv, 1)
                tf_guard[gi2] = vt
                if c == TBc - 1:
                    ph.wait("sync", sv, vt)
                    xo_dma[o % 2] = ph.do("sync", lambda e: e.dma_start(
                        xout[o * 128:(o + 1) * 128, tb * TB:(tb + 1) * TB],
                        xo[:, (o % 2) * TB:(o % 2 + 1) * TB]), dxo[o % 2], 16)
                    dn_ev.append((dxo[o % 2], xo_dma[o % 2]))
                return (sv, vt)

            hg = emit_mm_big(
                kb, ph, "dn",
                lambda k, c: h[:, k * TB + c * 512:k * TB + (c + 1) * 512],
                NH, down["w"], D, TBc, devac, og_tiles=2,
                wmode=down.get("wmode", "f32"),
                wait_rhs=up_ev_prev)
            h_guard = hg
            fin = dn_ev
        ph.emit()
    return fin


L0F_CONSTS = ["ident", "ones"]


def decl_l0_ffn(nc, T, px=""):
    w = {}
    w["wg"] = nc.dram_tensor(px + "wg", [D, 3072], I32,
                             kind="ExternalInput").ap()
    w["wu"] = nc.dram_tensor(px + "wu", [D, 3072], I32,
                             kind="ExternalInput").ap()
    w["wd"] = nc.dram_tensor(px + "wd", [3072, D], I32,
                             kind="ExternalInput").ap()
    return w


def sub_l0_ffn(kb, cst, caps, w, scr, xin, xout):
    T = kb.T
    ph = Ph(kb)
    sv = kb.sem("v")
    with contextlib.ExitStack() as st0:
        xn = kb.sb("xn", [128, DT * T], BF16, st=st0)
        ln_ticks = emit_ln(kb, ph, st0, xin, xn, cst,
                           extra_scale=1.0 / 127.0)
        fin = emit_ffn(
            kb, cst, ph, xn,
            ups=[dict(w=w["wg"], wmode="i32", act=AF.Silu, combine="first"),
                 dict(w=w["wu"], wmode="i32", act=None, combine="mult")],
            M_hid=3072,
            down=dict(w=w["wd"], wmode="i32", scale=1.0 / 127.0, og_tiles=2),
            xin=xin, xout=xout,
            wait_ln=[(sv, ln_ticks[-1])])
    return fin


def build_l0_ffn(T):
    return standalone(decl_l0_ffn, sub_l0_ffn, L0F_CONSTS, T, [])


L1A_CONSTS = ["ident", "ones", "cosI", "sinI", "permI"]


def decl_l1_attn(nc, T, px=""):
    w = {}
    for n in ("wq", "wk", "wv", "wo"):
        w[n] = nc.dram_tensor(px + n, [D, D], F32, kind="ExternalInput").ap()
    w["w1"] = nc.dram_tensor(px + "w1", [D, 4096], F32,
                             kind="ExternalInput").ap()
    w["w2"] = nc.dram_tensor(px + "w2", [4096, D], F32,
                             kind="ExternalInput").ap()
    for n, m in (("bq", 8), ("bk", 8), ("bv", 8), ("bo", 8), ("b1", 32),
                 ("b2", 8), ("g1", 8), ("be1", 8), ("g2", 8), ("be2", 8)):
        w[n + "_t"] = nc.dram_tensor(px + n + "_t", [128, m], F32,
                                     kind="ExternalInput").ap()
    return w


def sub_l1_attn(kb, cst, caps, w, scr, xin, xout):
    T = kb.T
    ph = Ph(kb)
    ds = kb.sem("dc")
    sv = kb.sem("v")
    scrS, scrS2 = scr["s"], scr["s2"]
    scrQ, scrK, scrV, scrO = scr["q"], scr["k"], scr["v"], scr["o"]
    with contextlib.ExitStack() as st0:
        srow = kb.sb("bstats", [1, 640], F32, st=st0)
        bias_tiles = kb.sb("battn", [128, 16], F32, st=st0)
        bsb_big = kb.sb("bvecs", [128, 8 * 9 + 32], F32, st=st0)
        offs = {}
        off = 0
        t_b = None
        for n, m in (("bq", 8), ("bk", 8), ("bv", 8), ("bo", 8), ("b1", 32),
                     ("b2", 8), ("g1", 8), ("be1", 8), ("g2", 8), ("be2", 8)):
            offs[n] = off
            t_b = ph.do("sync", lambda e, n=n, off=off, m=m: e.dma_start(
                bsb_big[:, off:off + m], w[n + "_t"][:]), ds, 16)
            off += m
        for eng in ("tensor", "scalar", "vector", "gpsimd"):
            ph.wait(eng, ds, t_b)

        def bs(n, m):
            return bsb_big[:, offs[n]:offs[n] + m]

        dslot = kb.sem("dslot")
        with contextlib.ExitStack() as s1:
            xn0 = kb.sb("xn0", [128, DT * T], BF16, st=s1)
            lt = emit_ln(kb, ph, s1, xin, xn0, cst)
            sl_t = None
            for j in range(DT):
                ph.wait("sync", sv, lt[j])
                sl_t = ph.do("sync", lambda e, j=j: e.dma_start(
                    scrS[j * 128:(j + 1) * 128, :], xn0[:, j * T:(j + 1) * T]),
                    dslot, 16)
            ph.emit()
        with contextlib.ExitStack() as s2:
            xn1 = kb.sb("xn1", [128, DT * T], BF16, st=s2)
            lt1 = emit_ln(kb, ph, s2, scrS, xn1, cst,
                          gains=(bs("g1", 8), bs("be1", 8)),
                          wait_first=[(dslot, sl_t)])
            qkv_ticks = emit_qkv(kb, cst, ph, s2, xn1, [
                dict(name="q", w=w["wq"], bias=bs("bq", 8), post="ropeI",
                     dst=scrQ, bound=(srow, 0)),
                dict(name="k", w=w["wk"], bias=bs("bk", 8), post="ropeI",
                     dst=scrK, bound=(srow, 16 * kb.TC)),
                dict(name="v", w=w["wv"], bias=bs("bv", 8), post=None,
                     dst=scrV),
            ], wait_ln=lt1)
        bias_rdy = emit_bounds_final(kb, cst, ph, srow, bias_tiles, SCALE_ATTN)
        wfm = {}
        for s_, t_ in qkv_ticks.values():
            wfm[s_] = max(wfm.get(s_, 0), t_)
        (o_sem, o_tick), _ = emit_attn_core(
            kb, cst, ph, scrQ, scrK, scrV, scrO, bias_tiles, bias_rdy,
            SCALE_ATTN, wait_first=list(wfm.items()))
        fin5 = emit_outproj_residual(kb, cst, ph, scrO, w["wo"], bs("bo", 8),
                                     scrS, scrS2, wait_first=[(o_sem, o_tick)],
                                     xin_bf=True)
        with contextlib.ExitStack() as s6:
            xn2 = kb.sb("xn2", [128, DT * T], BF16, st=s6)
            lt2 = emit_ln(kb, ph, s6, scrS2, xn2, cst,
                          gains=(bs("g2", 8), bs("be2", 8)),
                          wait_first=fin5)
            fin = emit_ffn(
                kb, cst, ph, xn2,
                ups=[dict(w=w["w1"], wmode="f32", act=AF.Relu,
                          bias_ap=bs("b1", 32), combine="first")],
                M_hid=4096,
                down=dict(w=w["w2"], wmode="f32", bias_ap=bs("b2", 8),
                          og_tiles=1),
                xin=scrS2, xout=xout, extra_add=xin,
                wait_ln=[(sv, lt2[-1])])
    return fin


def build_l1_attn(T):
    return standalone(decl_l1_attn, sub_l1_attn, L1A_CONSTS, T,
                      ["s", "s2", "q", "k", "v", "o"])


def emit_mm_big(kb, ph, name, rhs_slice, n_k, w_dram, M_out, n_c, evac,
                og_tiles=2, wmode="f32", wait_rhs=None, col0=0):
    stt = kb.sem("t")
    q = "gpsimd" if wmode in ("f32", "i32") else "sync"
    if q == "gpsimd":
        dwp = [kb.sem("dwm0"), kb.sem("dwm1")]
    else:
        dwp = [kb.sem("dwn0"), kb.sem("dwn1")]
    OG = og_tiles * 128
    n_og = M_out // OG
    nb = og_tiles * n_c
    assert nb <= 4
    with contextlib.ExitStack() as pst:
        wog = [kb.sb(f"w{name}", [128, n_k * OG], BF16, st=pst)
               for _ in range(2)]
        pp = [kb.ps(f"pp{name}", [128, 512], F32, st=pst) for _ in range(nb)]
        wr = w_dram.rearrange("(k p) m -> p k m", p=128)
        if wait_rhs:
            for s_, t_ in wait_rhs:
                ph.wait(q, s_, t_)
        mm_last = {}
        ev_pend = []
        for og in range(n_og):
            par = og % 2
            if og >= 2:
                ph.wait(q, stt, mm_last[og - 2])
            wt = ph.do(q, lambda e, og=og, par=par: e.dma_start(
                wog[par][:].rearrange("p (k m) -> p k m", k=n_k),
                wr[:, :, col0 + og * OG: col0 + (og + 1) * OG]),
                dwp[par], 16)
            ph.wait("tensor", dwp[par], wt)
            if og == 0 and wait_rhs:
                for s_, t_ in wait_rhs:
                    ph.wait("tensor", s_, t_)
            if og >= 1:
                for s_, t_ in ev_pend:
                    ph.wait("tensor", s_, t_)
                ev_pend = []
            mmt = {}
            for k in range(n_k):
                for oi in range(og_tiles):
                    for c in range(n_c):
                        bi = oi * n_c + c
                        mm = ph.do("tensor", lambda e, k=k, oi=oi, c=c, bi=bi,
                                   par=par: e.matmul(
                            pp[bi][:],
                            wog[par][:, k * OG + oi * 128:k * OG + (oi + 1) * 128],
                            rhs_slice(k, c),
                            start=(k == 0), stop=(k == n_k - 1)),
                            stt if k == n_k - 1 else None)
                        if k == n_k - 1:
                            mmt[(oi, c)] = mm
            mm_last[og] = mm
            for oi in range(og_tiles):
                for c in range(n_c):
                    bi = oi * n_c + c
                    ev_pend.append(evac(og * og_tiles + oi, c, pp[bi],
                                        (stt, mmt[(oi, c)])))
        ph.emit()
    return ev_pend


def emit_dequant_u8(kb, cst, ph, name, q_dram, scales_dram, scrW, K_in, M,
                    selP, P):
    dqp = [kb.sem("dqq0"), kb.sem("dqq1")]
    dsc = kb.sem("dscq")
    dwo = [kb.sem("dwoq0"), kb.sem("dwoq1")]
    stt = kb.sem("t")
    ss = kb.sem("s")
    sv = kb.sem("v")
    NK = K_in // 128
    with contextlib.ExitStack() as pst:
        ssb = kb.sb(f"dq_s{name}", [P, M], BF16, st=pst)
        qb = [kb.sb(f"dq_q{name}", [128, M], BF16, st=pst) for _ in range(2)]
        sbc = kb.sb(f"dq_bc{name}", [128, M], BF16, st=pst)
        pb = [kb.ps(f"dq_pb{name}", [128, 512], F32, st=pst) for _ in range(2)]
        lt = ph.do("gpsimd", lambda e: e.dma_start(ssb[:], scales_dram[:]),
                   dsc, 16)
        ph.wait("tensor", dsc, lt)
        prev_tt = None
        prev_dma = {0: None, 1: None}
        for k in range(NK):
            par = k % 2
            if prev_dma[par] is not None:
                ph.wait("gpsimd", dwo[par], prev_dma[par])
            qt = ph.do("gpsimd", lambda e, k=k, par=par: e.dma_start(
                qb[par][:], q_dram[k * 128:(k + 1) * 128, :]), dqp[par], 16)
            ev_t = {}
            for c in range(M // 512):
                bi = c % 2
                if c >= 2:
                    ph.wait("tensor", sv, ev_t[c - 2])
                elif prev_tt is not None and c < 2:
                    ph.wait("tensor", sv, prev_tt)
                bm = ph.do("tensor", lambda e, k=k, c=c, bi=bi: e.matmul(
                    pb[bi][:], selP[:, k * 128:(k + 1) * 128],
                    ssb[:, c * 512:(c + 1) * 512], start=True, stop=True),
                    stt, 1)
                ph.wait("vector", stt, bm)
                ev_t[c] = ph.do("vector", lambda e, c=c, bi=bi: e.tensor_copy(
                    sbc[:, c * 512:(c + 1) * 512], pb[bi][:]), sv, 1)
            ph.wait("vector", dqp[par], qt)
            t1 = ph.do("vector", lambda e, par=par: e.tensor_single_scalar(
                qb[par][:], qb[par][:], -128.0, ALU.add), sv, 1)
            ph.wait("vector", sv, t1)
            tt = ph.do("vector", lambda e, par=par: e.tensor_tensor(
                qb[par][:], qb[par][:], sbc[:], ALU.mult), sv, 1)
            prev_tt = tt
            ph.wait("sync", sv, tt)
            prev_dma[par] = ph.do("sync", lambda e, k=k, par=par: e.dma_start(
                scrW[k * 128:(k + 1) * 128, :], qb[par][:]), dwo[par], 16)
        ph.emit()
    return [(dwo[0], prev_dma[0]), (dwo[1], prev_dma[1])]


def emit_ffn(kb, cst, ph, xn, ups, M_hid, down, xin, xout, wait_ln,
             extra_add=None, TBc=2, xin_bf=False, h_src=None):
    T = kb.T
    TBc = min(TBc, kb.TC)
    TB = TBc * 512
    NTB = T // TB
    NH = M_hid // 128
    stt = kb.sem("t")
    ss = kb.sem("s")
    sv = kb.sem("v")
    if xin_bf:
        dxi = [kb.sem("gxi0"), kb.sem("gxi1")]
    else:
        dxi = [kb.sem("fxi0"), kb.sem("fxi1")]
    dxj = [kb.sem("fxj0"), kb.sem("fxj1")]
    dxo = [kb.sem("fxo0"), kb.sem("fxo1")]
    with contextlib.ExitStack() as pst:
        h = kb.sb("ffn_h", [128, NH * TB], BF16, st=pst)
        tmp = kb.sb("ffn_tmp", [128, 2 * 512], BF16, st=pst)
        tf = kb.sb("ffn_tf", [128, 2 * 512], F32, st=pst)
        xi = kb.sb("ffn_xi", [128, 2 * TB], F32, st=pst)
        xi2 = kb.sb("ffn_xi2", [128, 2 * TB], F32, st=pst) if extra_add else None
        xo = kb.sb("ffn_xo", [128, 2 * TB], F32, st=pst)
        fin = []
        up_ev_prev = []
        xi_t = {}
        xi2_t = {}
        xo_dma = {0: None, 1: None}
        h_guard = []
        tmp_guard = {0: None, 1: None}
        tf_guard = {0: None, 1: None}
        xo_guard = {}
        dhs = kb.sem("fhs")
        for tb in range(NTB):
            tpar = tb % 2
            if h_src is not None:
                if tb > 0:
                    for s_, t_ in h_guard:
                        ph.wait("sync", s_, t_)
                else:
                    for s_, t_ in (wait_ln or []):
                        ph.wait("sync", s_, t_)
                ht = None
                for k in range(NH):
                    ht = ph.do("sync", lambda e, k=k, tb=tb: e.dma_start(
                        h[:, k * TB:(k + 1) * TB],
                        h_src[k * 128:(k + 1) * 128, tb * TB:(tb + 1) * TB]),
                        dhs, 16)
                up_ev_prev = [(dhs, ht)] + (h_guard if tb > 0 else [])
                h_guard = []
            for ui, up in enumerate(ups):
                def mk_evac(up=up, ui=ui, tb=tb):
                    def evac(o, c, psum, mmw):
                        ph.wait("scalar", mmw[0], mmw[1])
                        kw = {}
                        if up.get("scale_ap") is not None:
                            kw["scale"] = up["scale_ap"][:, o:o + 1]
                        if up.get("bias_ap") is not None:
                            kw["bias"] = up["bias_ap"][:, o:o + 1]
                        fn = up.get("act") or AF.Identity
                        if up.get("combine") == "mult":
                            par2 = c % 2
                            if tmp_guard[par2] is not None:
                                ph.wait("scalar", sv, tmp_guard[par2])
                            et = ph.do("scalar", lambda e: e.activation(
                                tmp[:, par2 * 512:(par2 + 1) * 512], psum[:],
                                fn, **kw), ss, 1)
                            ph.wait("vector", ss, et)
                            vt = ph.do("vector", lambda e: e.tensor_tensor(
                                h[:, o * TB + c * 512:o * TB + (c + 1) * 512],
                                h[:, o * TB + c * 512:o * TB + (c + 1) * 512],
                                tmp[:, par2 * 512:(par2 + 1) * 512],
                                ALU.mult), sv, 1)
                            tmp_guard[par2] = vt
                            return (sv, vt)
                        else:
                            et = ph.do("scalar", lambda e: e.activation(
                                h[:, o * TB + c * 512:o * TB + (c + 1) * 512],
                                psum[:], fn, **kw), ss, 1)
                            return (ss, et)
                    return evac
                wr = wait_ln if (tb == 0 and ui == 0) else None
                wr = (wr or []) + (h_guard if ui == 0 else list(up_ev_prev))
                ev = emit_mm_big(
                    kb, ph, f"u{ui}", 
                    lambda k, c, tb=tb: xn[:, k * T + tb * TB + c * 512:
                                           k * T + tb * TB + (c + 1) * 512],
                    DT, up["w"], M_hid, TBc, mk_evac(),
                    og_tiles=2, wmode=up.get("wmode", "f32"), wait_rhs=wr)
                up_ev_prev = ev
            h_guard = []
            dn_ev = []

            def devac(o, c, psum, mmw, tb=tb, tpar=tpar):
                ph.wait("scalar", mmw[0], mmw[1])
                gi2 = (o * TBc + c) % 2
                kw = {}
                if down.get("bias_ap") is not None:
                    kw["bias"] = down["bias_ap"][:, o:o + 1]
                if down.get("scale") is not None:
                    kw["scale"] = float(down["scale"])
                if tf_guard[gi2] is not None:
                    ph.wait("scalar", sv, tf_guard[gi2])
                et = ph.do("scalar", lambda e: e.activation(
                    tf[:, gi2 * 512:(gi2 + 1) * 512], psum[:], AF.Identity,
                    **kw), ss, 1)
                if c == 0:
                    xiq = "gpsimd" if xin_bf else "sync"
                    if xo_dma[o % 2] is not None:
                        ph.wait(xiq, dxo[o % 2], xo_dma[o % 2])
                    xi_t[(tb, o)] = ph.do(
                        xiq, lambda e: e.dma_start(
                            xi[:, (o % 2) * TB:(o % 2 + 1) * TB],
                            xin[o * 128:(o + 1) * 128, tb * TB:(tb + 1) * TB]),
                        dxi[o % 2], 16)
                    if extra_add is not None:
                        xi2_t[(tb, o)] = ph.do(
                            "sync", lambda e: e.dma_start(
                                xi2[:, (o % 2) * TB:(o % 2 + 1) * TB],
                                extra_add[o * 128:(o + 1) * 128,
                                          tb * TB:(tb + 1) * TB]),
                            dxj[o % 2], 16)
                ph.wait("vector", ss, et)
                ph.wait("vector", dxi[o % 2], xi_t[(tb, o)])
                if extra_add is not None:
                    ph.wait("vector", dxj[o % 2], xi2_t[(tb, o)])
                if xo_dma[o % 2] is not None:
                    ph.wait("vector", dxo[o % 2], xo_dma[o % 2])
                vt = ph.do("vector", lambda e: e.tensor_tensor(
                    xo[:, (o % 2) * TB + c * 512:(o % 2) * TB + (c + 1) * 512],
                    tf[:, gi2 * 512:(gi2 + 1) * 512],
                    xi[:, (o % 2) * TB + c * 512:(o % 2) * TB + (c + 1) * 512],
                    ALU.add), sv, 1)
                if extra_add is not None:
                    ph.wait("vector", sv, vt)
                    vt = ph.do("vector", lambda e: e.tensor_tensor(
                        xo[:, (o % 2) * TB + c * 512:
                           (o % 2) * TB + (c + 1) * 512],
                        xo[:, (o % 2) * TB + c * 512:
                           (o % 2) * TB + (c + 1) * 512],
                        xi2[:, (o % 2) * TB + c * 512:
                            (o % 2) * TB + (c + 1) * 512],
                        ALU.add), sv, 1)
                tf_guard[gi2] = vt
                if c == TBc - 1:
                    ph.wait("sync", sv, vt)
                    xo_dma[o % 2] = ph.do("sync", lambda e: e.dma_start(
                        xout[o * 128:(o + 1) * 128, tb * TB:(tb + 1) * TB],
                        xo[:, (o % 2) * TB:(o % 2 + 1) * TB]), dxo[o % 2], 16)
                    dn_ev.append((dxo[o % 2], xo_dma[o % 2]))
                return (sv, vt)

            hg = emit_mm_big(
                kb, ph, "dn",
                lambda k, c: h[:, k * TB + c * 512:k * TB + (c + 1) * 512],
                NH, down["w"], D, TBc, devac, og_tiles=2,
                wmode=down.get("wmode", "f32"),
                wait_rhs=up_ev_prev)
            h_guard = hg
            fin = dn_ev
        ph.emit()
    return fin


L0F_CONSTS = ["ident", "ones"]


def decl_l0_ffn(nc, T, px=""):
    w = {}
    w["wg"] = nc.dram_tensor(px + "wg", [D, 3072], I32,
                             kind="ExternalInput").ap()
    w["wu"] = nc.dram_tensor(px + "wu", [D, 3072], I32,
                             kind="ExternalInput").ap()
    w["wd"] = nc.dram_tensor(px + "wd", [3072, D], I32,
                             kind="ExternalInput").ap()
    return w


def sub_l0_ffn(kb, cst, caps, w, scr, xin, xout):
    T = kb.T
    ph = Ph(kb)
    sv = kb.sem("v")
    with contextlib.ExitStack() as st0:
        xn = kb.sb("xn", [128, DT * T], BF16, st=st0)
        ln_ticks = emit_ln(kb, ph, st0, xin, xn, cst,
                           extra_scale=1.0 / 127.0)
        fin = emit_ffn(
            kb, cst, ph, xn,
            ups=[dict(w=w["wg"], wmode="i32", act=AF.Silu, combine="first"),
                 dict(w=w["wu"], wmode="i32", act=None, combine="mult")],
            M_hid=3072,
            down=dict(w=w["wd"], wmode="i32", scale=1.0 / 127.0, og_tiles=2),
            xin=xin, xout=xout,
            wait_ln=[(sv, ln_ticks[-1])])
    return fin


def build_l0_ffn(T):
    return standalone(decl_l0_ffn, sub_l0_ffn, L0F_CONSTS, T, [])


def build_l1_attn(T):
    nc = bass.Bass("TRN2", target_bir_lowering=False)
    xin = nc.dram_tensor("xin", [D, T], F32, kind="ExternalInput").ap()
    xout = nc.dram_tensor("xout", [D, T], F32, kind="ExternalOutput").ap()
    w = {}
    for n in ("wq", "wk", "wv", "wo"):
        w[n] = nc.dram_tensor(n, [D, D], F32, kind="ExternalInput").ap()
    w["w1"] = nc.dram_tensor("w1", [D, 4096], F32, kind="ExternalInput").ap()
    w["w2"] = nc.dram_tensor("w2", [4096, D], F32, kind="ExternalInput").ap()
    bt = {}
    for n, m in (("bq", 8), ("bk", 8), ("bv", 8), ("bo", 8), ("b1", 32),
                 ("b2", 8), ("g1", 8), ("be1", 8), ("g2", 8), ("be2", 8)):
        bt[n] = nc.dram_tensor(n + "_t", [128, m], F32,
                               kind="ExternalInput").ap()
    cnames = ["ident", "ones", "cosI", "sinI", "permI"]
    caps = declare_constants(nc, T, cnames)
    nc.const_names = cnames
    scrS = nc.dram_tensor("scrS", [D, T], BF16, kind="Internal").ap()
    scrS2 = nc.dram_tensor("scrS2", [D, T], F32, kind="Internal").ap()
    scrQ = nc.dram_tensor("scrQ", [D, T], BF16, kind="Internal").ap()
    scrK = nc.dram_tensor("scrK", [D, T], BF16, kind="Internal").ap()
    scrV = nc.dram_tensor("scrV", [D, T], BF16, kind="Internal").ap()
    scrO = nc.dram_tensor("scrO", [D, T], BF16, kind="Internal").ap()
    with contextlib.ExitStack() as st:
        kb = KB(nc, st, T)
        cst = Cst(kb, caps, resident=["ident", "ones"])
        srow = kb.sb("bstats", [1, 640], F32)
        bias_tiles = kb.sb("battn", [128, 16], F32)
        bsb = {}
        ph = Ph(kb)
        ds = kb.sem("dc")
        sv = kb.sem("v")
        cld = cst.load(kb, ph)
        t_b = None
        bsb_big = kb.sb("bvecs", [128, 8 * 9 + 32], F32)
        offs = {}
        off = 0
        for n, m in (("bq", 8), ("bk", 8), ("bv", 8), ("bo", 8), ("b1", 32),
                     ("b2", 8), ("g1", 8), ("be1", 8), ("g2", 8)):
            offs[n] = off
            t_b = ph.do("sync", lambda e, n=n, off=off, m=m: e.dma_start(
                bsb_big[:, off:off + m], bt[n][:]), ds, 16)
            off += m
        offs["be2"] = off
        t_b = ph.do("sync", lambda e, off=off: e.dma_start(
            bsb_big[:, off:off + 8], bt["be2"][:]), ds, 16)
        off += 8
        cld.append((ds, t_b))
        cldm = {}
        for s_, t_ in cld:
            cldm[s_] = max(cldm.get(s_, 0), t_)
        for eng in ("tensor", "scalar", "vector", "gpsimd"):
            for s_, t_ in cldm.items():
                ph.wait(eng, s_, t_)

        def bs(n, m):
            return bsb_big[:, offs[n]:offs[n] + m]

        dslot = kb.sem("dslot")
        with contextlib.ExitStack() as s1:
            xn0 = kb.sb("xn0", [128, DT * T], BF16, st=s1)
            lt = emit_ln(kb, ph, s1, xin, xn0, cst)
            sl_t = None
            for j in range(DT):
                ph.wait("sync", sv, lt[j])
                sl_t = ph.do("sync", lambda e, j=j: e.dma_start(
                    scrS[j * 128:(j + 1) * 128, :], xn0[:, j * T:(j + 1) * T]),
                    dslot, 16)
            ph.emit()
        with contextlib.ExitStack() as s2:
            xn1 = kb.sb("xn1", [128, DT * T], BF16, st=s2)
            lt1 = emit_ln(kb, ph, s2, scrS, xn1, cst,
                          gains=(bs("g1", 8), bs("be1", 8)),
                          wait_first=[(dslot, sl_t)])
            qkv_ticks = emit_qkv(kb, cst, ph, s2, xn1, [
                dict(name="q", w=w["wq"], bias=bs("bq", 8), post="ropeI",
                     dst=scrQ, bound=(srow, 0)),
                dict(name="k", w=w["wk"], bias=bs("bk", 8), post="ropeI",
                     dst=scrK, bound=(srow, 16 * kb.TC)),
                dict(name="v", w=w["wv"], bias=bs("bv", 8), post=None,
                     dst=scrV),
            ], wait_ln=lt1)
        bias_rdy = emit_bounds_final(kb, cst, ph, srow, bias_tiles, SCALE_ATTN)
        wfm = {}
        for s_, t_ in qkv_ticks.values():
            wfm[s_] = max(wfm.get(s_, 0), t_)
        (o_sem, o_tick), _ = emit_attn_core(
            kb, cst, ph, scrQ, scrK, scrV, scrO, bias_tiles, bias_rdy,
            SCALE_ATTN, wait_first=list(wfm.items()))
        fin5 = emit_outproj_residual(kb, cst, ph, scrO, w["wo"], bs("bo", 8),
                                     scrS, scrS2, wait_first=[(o_sem, o_tick)],
                                     xin_bf=True)
        with contextlib.ExitStack() as s6:
            xn2 = kb.sb("xn2", [128, DT * T], BF16, st=s6)
            lt2 = emit_ln(kb, ph, s6, scrS2, xn2, cst,
                          gains=(bs("g2", 8), bs("be2", 8)),
                          wait_first=fin5)
            fin = emit_ffn(
                kb, cst, ph, xn2,
                ups=[dict(w=w["w1"], wmode="f32", act=AF.Relu,
                          bias_ap=bs("b1", 32), combine="first")],
                M_hid=4096,
                down=dict(w=w["w2"], wmode="f32", bias_ap=bs("b2", 8),
                          og_tiles=1),
                xin=scrS2, xout=xout, extra_add=xin,
                wait_ln=[(sv, lt2[-1])])
        phz = Ph(kb)
        for s_, t_ in fin:
            phz.wait("sync", s_, t_)
        phz.do("sync", lambda e: e.nop())
        phz.emit()
    return nc


L2F_CONSTS = ["ident", "ones", "sel8", "sel32"]


def decl_l2_ffn(nc, T, px=""):
    w = {}
    w["fc1q"] = nc.dram_tensor(px + "fc1q", [D, 4096], I32,
                               kind="ExternalInput").ap()
    w["fc2q"] = nc.dram_tensor(px + "fc2q", [4096, D], I32,
                               kind="ExternalInput").ap()
    w["fc1s"] = nc.dram_tensor(px + "fc1s", [8, 4096], F32,
                               kind="ExternalInput").ap()
    w["fc2s"] = nc.dram_tensor(px + "fc2s", [32, D], F32,
                               kind="ExternalInput").ap()
    for n, m in (("b1", 32), ("b2h", 8), ("lg", 8), ("lb", 8)):
        w[n + "_t"] = nc.dram_tensor(px + n + "_t", [128, m], F32,
                                     kind="ExternalInput").ap()
    return w


def sub_l2_ffn(kb, cst, caps, w, scr, xin, xout):
    T = kb.T
    ph = Ph(kb)
    ds = kb.sem("dc")
    sv = kb.sem("v")
    scrS, scrW1, scrW2 = scr["s"], scr["w1q"], scr["w2q"]
    with contextlib.ExitStack() as st0:
        bsb = kb.sb("bvecs", [128, 32 + 8 * 3], F32, st=st0)
        offs = {"b1": 0, "b2h": 32, "lg": 40, "lb": 48}
        t_b = None
        for n, m in (("b1", 32), ("b2h", 8), ("lg", 8), ("lb", 8)):
            t_b = ph.do("sync", lambda e, n=n, m=m: e.dma_start(
                bsb[:, offs[n]:offs[n] + m], w[n + "_t"][:]), ds, 16)
        for eng in ("tensor", "scalar", "vector", "gpsimd"):
            ph.wait(eng, ds, t_b)

        def bs(n, m):
            return bsb[:, offs[n]:offs[n] + m]

        t_p = ph.do("vector", lambda e: e.tensor_single_scalar(
            bs("b2h", 8), bs("b2h", 8), 0.5, ALU.mult), sv, 1)
        ph.wait("scalar", sv, t_p)
        sel8sb = load_sel(kb, cst, ph, "sel8", 8, st0)
        sel32sb = load_sel(kb, cst, ph, "sel32", 32, st0)
        dq1 = emit_dequant_u8(kb, cst, ph, "f1", w["fc1q"], w["fc1s"], scrW1,
                              D, 4096, sel8sb, 8)
        dq2 = emit_dequant_u8(kb, cst, ph, "f2", w["fc2q"], w["fc2s"], scrW2,
                              4096, D, sel32sb, 32)
        dslot = kb.sem("dslot")
        dqm = {}
        for s_, t_ in dq1 + dq2:
            dqm[s_] = max(dqm.get(s_, 0), t_)
        dqw = list(dqm.items())
        with contextlib.ExitStack() as s1:
            xn0 = kb.sb("xn0", [128, DT * T], BF16, st=s1)
            lt = emit_ln(kb, ph, s1, xin, xn0, cst, wait_first=dqw)
            sl_t = None
            for j in range(DT):
                ph.wait("sync", sv, lt[j])
                sl_t = ph.do("sync", lambda e, j=j: e.dma_start(
                    scrS[j * 128:(j + 1) * 128, :], xn0[:, j * T:(j + 1) * T]),
                    dslot, 16)
            ph.emit()
        with contextlib.ExitStack() as s2:
            xn = kb.sb("xn", [128, DT * T], BF16, st=s2)
            lt1 = emit_ln(kb, ph, s2, scrS, xn, cst,
                          gains=(bs("lg", 8), bs("lb", 8)),
                          wait_first=[(dslot, sl_t)])
            fin = emit_ffn(
                kb, cst, ph, xn,
                ups=[dict(w=scrW1, wmode="bf16", act=AF.Silu,
                          bias_ap=bs("b1", 32), combine="first")],
                M_hid=4096,
                down=dict(w=scrW2, wmode="bf16", bias_ap=bs("b2h", 8),
                          scale=0.5, og_tiles=1),
                xin=scrS, xout=xout, extra_add=xin, xin_bf=True,
                wait_ln=[(sv, lt1[-1])] + dqw)
    return fin


def build_l2_ffn(T):
    return standalone(decl_l2_ffn, sub_l2_ffn, L2F_CONSTS, T,
                      ["s", "w1q", "w2q"])


L1F_CONSTS = ["ident", "ones"]


def decl_l1_ffn(nc, T, px=""):
    w = {}
    w["w1"] = nc.dram_tensor(px + "w1", [D, 4096], F32,
                             kind="ExternalInput").ap()
    w["w2"] = nc.dram_tensor(px + "w2", [D, 4096], F32,
                             kind="ExternalInput").ap()
    w["h1"] = nc.dram_tensor(px + "h1", [4096, 4096], F32,
                             kind="ExternalInput").ap()
    w["h2"] = nc.dram_tensor(px + "h2", [4096, 4096], F32,
                             kind="ExternalInput").ap()
    w["w3"] = nc.dram_tensor(px + "w3", [4096, D], F32,
                             kind="ExternalInput").ap()
    for n, m in (("s1", 32), ("s2", 32), ("s3", 32), ("s4", 32), ("s5", 8)):
        w[n + "_t"] = nc.dram_tensor(px + n + "_t", [128, m], F32,
                                     kind="ExternalInput").ap()
    return w


def sub_l1_ffn(kb, cst, caps, w, scr, xin, xout):
    T = kb.T
    nc = kb.nc
    ph = Ph(kb)
    ds = kb.sem("dc")
    sv = kb.sem("v")
    ss = kb.sem("s")
    stt = kb.sem("t")
    TB = min(1024, T)
    NTB = max(1, T // TB)
    TBc = TB // 512
    scrA = scr["a"]
    w1, w2, h1, h2, w3 = w["w1"], w["w2"], w["h1"], w["h2"], w["w3"]
    with contextlib.ExitStack() as st0:
        bsb = kb.sb("bvecs", [128, 32 * 4 + 8], F32, st=st0)
        offs = {"s1": 0, "s2": 32, "s3": 64, "s4": 96, "s5": 128}
        t_b = None
        for n, m in (("s1", 32), ("s2", 32), ("s3", 32), ("s4", 32), ("s5", 8)):
            t_b = ph.do("sync", lambda e, n=n, m=m: e.dma_start(
                bsb[:, offs[n]:offs[n] + m], w[n + "_t"][:]), ds, 16)
        for eng in ENGS:
            ph.wait(eng, ds, t_b)

        def bs(n, m):
            return bsb[:, offs[n]:offs[n] + m]

        da = [kb.sem("dqo0"), kb.sem("dqo1")]
        with contextlib.ExitStack() as sA:
            xn = kb.sb("xn", [128, DT * T], BF16, st=sA)
            lt = emit_ln(kb, ph, sA, xin, xn, cst)
            a_buf = kb.sb("abuf", [128, 2 * 32 * TB], BF16, st=sA)
            tmp = kb.sb("atmp", [128, 2 * 512], BF16, st=sA)
            tmp_guard = {0: None, 1: None}
            adma = {0: None, 1: None}
            for tb in range(NTB):
                par = tb % 2
                ab = a_buf[:, par * 32 * TB:(par + 1) * 32 * TB]
                if adma[par] is not None:
                    ph.wait("scalar", da[par], adma[par])
                    ph.wait("vector", da[par], adma[par])

                def ev1(o, c, psum, mmw, ab=ab):
                    ph.wait("scalar", mmw[0], mmw[1])
                    et = ph.do("scalar", lambda e: e.activation(
                        ab[:, o * TB + c * 512:o * TB + (c + 1) * 512],
                        psum[:], AF.Silu, scale=bs("s1", 32)[:, o:o + 1]),
                        ss, 1)
                    return (ss, et)

                def ev2(o, c, psum, mmw, ab=ab):
                    ph.wait("scalar", mmw[0], mmw[1])
                    p2 = c % 2
                    if tmp_guard[p2] is not None:
                        ph.wait("scalar", sv, tmp_guard[p2])
                    et = ph.do("scalar", lambda e: e.activation(
                        tmp[:, p2 * 512:(p2 + 1) * 512], psum[:], AF.Identity,
                        scale=bs("s2", 32)[:, o:o + 1]), ss, 1)
                    ph.wait("vector", ss, et)
                    vt = ph.do("vector", lambda e: e.tensor_tensor(
                        ab[:, o * TB + c * 512:o * TB + (c + 1) * 512],
                        ab[:, o * TB + c * 512:o * TB + (c + 1) * 512],
                        tmp[:, p2 * 512:(p2 + 1) * 512], ALU.mult), sv, 1)
                    tmp_guard[p2] = vt
                    return (sv, vt)

                wr = [(sv, lt[-1])] if tb == 0 else list(prev_e2)
                e1 = emit_mm_big(kb, ph, "a1",
                                 lambda k, c, tb=tb: xn[:, k * T + tb * TB +
                                                        c * 512:k * T + tb * TB
                                                        + (c + 1) * 512],
                                 DT, w1, 4096, TBc, ev1, og_tiles=2,
                                 wmode="f32", wait_rhs=wr)
                e2 = emit_mm_big(kb, ph, "a2",
                                 lambda k, c, tb=tb: xn[:, k * T + tb * TB +
                                                        c * 512:k * T + tb * TB
                                                        + (c + 1) * 512],
                                 DT, w2, 4096, TBc, ev2, og_tiles=2,
                                 wmode="f32", wait_rhs=e1)
                prev_e2 = e2
                ph.wait("sync", sv, kb.sems["v"].n)
                adma[par] = ph.do("sync", lambda e, tb=tb, ab=ab: e.dma_start(
                    scrA[tb].rearrange("(k p) t -> p k t", p=128),
                    ab.rearrange("p (k t) -> p k t", k=32)), da[par], 16)
            ph.emit()
        dal = [kb.sem("daq"), kb.sem("dak")]
        dxi = [kb.sem("dxi0"), kb.sem("dxi1")]
        dxo = [kb.sem("dxo0"), kb.sem("dxo1")]
        with contextlib.ExitStack() as sB:
            al = kb.sb("albuf", [128, 32 * TB], BF16, st=sB)
            gb = kb.sb("gbuf", [128, 32 * TB], BF16, st=sB)
            tf = kb.sb("ltf", [128, 2 * 512], F32, st=sB)
            xi = kb.sb("lxi", [128, 2 * TB], F32, st=sB)
            xo = kb.sb("lxo", [128, 2 * TB], F32, st=sB)
            tf_guard = {0: None, 1: None}
            xo_dma = {0: None, 1: None}
            xi_t = {}
            al_guard = []
            for tb in range(NTB):
                par = tb % 2
                for s_, t_ in al_guard:
                    ph.wait("sync", s_, t_)
                ph.wait("sync", da[par], adma[par])
                alt = ph.do("sync", lambda e, tb=tb: e.dma_start(
                    al.rearrange("p (k t) -> p k t", k=32),
                    scrA[tb].rearrange("(k p) t -> p k t", p=128)),
                    dal[tb % 2], 16)

                def evg(dst, sc_name):
                    def ev(o, c, psum, mmw):
                        ph.wait("scalar", mmw[0], mmw[1])
                        et = ph.do("scalar", lambda e: e.activation(
                            dst[:, o * TB + c * 512:o * TB + (c + 1) * 512],
                            psum[:], AF.Gelu,
                            scale=bs(sc_name, 32)[:, o:o + 1]), ss, 1)
                        return (ss, et)
                    return ev

                eg1 = emit_mm_big(kb, ph, "g1",
                                  lambda k, c: al[:, k * TB + c * 512:
                                                  k * TB + (c + 1) * 512],
                                  32, h1, 4096, TBc, evg(gb, "s3"),
                                  og_tiles=2, wmode="f32",
                                  wait_rhs=[(dal[tb % 2], alt)] + al_guard)
                eg2 = emit_mm_big(kb, ph, "g2",
                                  lambda k, c: gb[:, k * TB + c * 512:
                                                  k * TB + (c + 1) * 512],
                                  32, h2, 4096, TBc, evg(al, "s4"),
                                  og_tiles=2, wmode="f32", wait_rhs=eg1)

                def evd(o, c, psum, mmw, tb=tb):
                    ph.wait("scalar", mmw[0], mmw[1])
                    gi2 = (o * TBc + c) % 2
                    if tf_guard[gi2] is not None:
                        ph.wait("scalar", sv, tf_guard[gi2])
                    et = ph.do("scalar", lambda e: e.activation(
                        tf[:, gi2 * 512:(gi2 + 1) * 512], psum[:], AF.Identity,
                        scale=bs("s5", 8)[:, o:o + 1]), ss, 1)
                    if c == 0:
                        if xo_dma[o % 2] is not None:
                            ph.wait("sync", dxo[o % 2], xo_dma[o % 2])
                        xi_t[(tb, o)] = ph.do("sync", lambda e: e.dma_start(
                            xi[:, (o % 2) * TB:(o % 2 + 1) * TB],
                            xin[o * 128:(o + 1) * 128, tb * TB:(tb + 1) * TB]),
                            dxi[o % 2], 16)
                    ph.wait("vector", ss, et)
                    ph.wait("vector", dxi[o % 2], xi_t[(tb, o)])
                    if xo_dma[o % 2] is not None:
                        ph.wait("vector", dxo[o % 2], xo_dma[o % 2])
                    vt = ph.do("vector", lambda e: e.tensor_tensor(
                        xo[:, (o % 2) * TB + c * 512:
                           (o % 2) * TB + (c + 1) * 512],
                        tf[:, gi2 * 512:(gi2 + 1) * 512],
                        xi[:, (o % 2) * TB + c * 512:
                           (o % 2) * TB + (c + 1) * 512], ALU.add), sv, 1)
                    tf_guard[gi2] = vt
                    if c == TBc - 1:
                        ph.wait("sync", sv, vt)
                        xo_dma[o % 2] = ph.do("sync", lambda e: e.dma_start(
                            xout[o * 128:(o + 1) * 128, tb * TB:(tb + 1) * TB],
                            xo[:, (o % 2) * TB:(o % 2 + 1) * TB]),
                            dxo[o % 2], 16)
                    return (sv, vt)

                al_guard = emit_mm_big(kb, ph, "dn",
                                       lambda k, c: al[:, k * TB + c * 512:
                                                       k * TB + (c + 1) * 512],
                                       32, w3, D, TBc, evd, og_tiles=1,
                                       wmode="f32", wait_rhs=eg2)
            ph.emit()
    return [(v_, t_) for v_, t_ in
            ((dxo[0], xo_dma[0]), (dxo[1], xo_dma[1])) if t_ is not None]


def build_l1_ffn(T):
    return standalone(decl_l1_ffn, sub_l1_ffn, L1F_CONSTS, T, ["a"])


L3A_CONSTS = ["ident", "ones", "cosI", "sinI", "permI"]


def decl_l3_attn(nc, T, px=""):
    w = {}
    w["wqkv"] = nc.dram_tensor(px + "wqkv", [D, 3072], F32,
                               kind="ExternalInput").ap()
    w["bqkv_t"] = nc.dram_tensor(px + "bqkv_t", [128, 24], F32,
                                 kind="ExternalInput").ap()
    w["wlep_t"] = nc.dram_tensor(px + "wlep_t", [128, 24], F32,
                                 kind="ExternalInput").ap()
    w["blep_t"] = nc.dram_tensor(px + "blep_t", [128, 8], F32,
                                 kind="ExternalInput").ap()
    return w


def sub_l3_attn(kb, cst, caps, w, scr, xin, xout):
    T = kb.T
    ph = Ph(kb)
    ds = kb.sem("dc")
    sv = kb.sem("v")
    ss = kb.sem("s")
    stt = kb.sem("t")
    wqkv = w["wqkv"]
    scrQ, scrK, scrV, scrO = scr["q"], scr["k"], scr["v"], scr["o"]
    with contextlib.ExitStack() as st0:
        srow = kb.sb("bstats", [1, 640], F32, st=st0)
        bias_tiles = kb.sb("battn", [128, 16], F32, st=st0)
        bsb = kb.sb("bvecs", [128, 24 + 24 + 8], F32, st=st0)
        t_b = ph.do("sync", lambda e: e.dma_start(bsb[:, 0:24],
                                                  w["bqkv_t"][:]), ds, 16)
        t_b = ph.do("sync", lambda e: e.dma_start(bsb[:, 24:48],
                                                  w["wlep_t"][:]), ds, 16)
        t_b = ph.do("sync", lambda e: e.dma_start(bsb[:, 48:56],
                                                  w["blep_t"][:]), ds, 16)
        for eng in ENGS:
            ph.wait(eng, ds, t_b)
        with contextlib.ExitStack() as s1:
            xn = kb.sb("xn", [128, DT * T], BF16, st=s1)
            lt = emit_ln(kb, ph, s1, xin, xn, cst)
            qkv_ticks = emit_qkv(kb, cst, ph, s1, xn, [
                dict(name="q", w=wqkv[:, 0:1024], bias=bsb[:, 0:8],
                     post="elu1_ropeI", dst=scrQ, bound=(srow, 0)),
                dict(name="k", w=wqkv[:, 1024:2048], bias=bsb[:, 8:16],
                     post="elu1_ropeI", dst=scrK, bound=(srow, 16 * kb.TC)),
                dict(name="v", w=wqkv[:, 2048:3072], bias=bsb[:, 16:24],
                     post=None, dst=scrV),
            ], wait_ln=lt)
        bias_rdy = emit_bounds_final(kb, cst, ph, srow, bias_tiles, SCALE_ATTN)
        wfm = {}
        for s_, t_ in qkv_ticks.values():
            wfm[s_] = max(wfm.get(s_, 0), t_)
        (o_sem, o_tick), _ = emit_attn_core(
            kb, cst, ph, scrQ, scrK, scrV, scrO, bias_tiles, bias_rdy,
            SCALE_ATTN, wait_first=list(wfm.items()))
        dov = [kb.sem("daq"), kb.sem("dak")]
        doo = [kb.sem("dav"), kb.sem("dot2")]
        dxi = [kb.sem("dxi0"), kb.sem("dxi1")]
        dxo = [kb.sem("dxo0"), kb.sem("dxo1")]
        with contextlib.ExitStack() as s2:
            vt_ = kb.sb("l3vt", [128, 2 * T], BF16, st=s2)
            ot_ = kb.sb("l3ot", [128, 2 * T], BF16, st=s2)
            xi_ = kb.sb("l3xi", [128, 2 * T], F32, st=s2)
            lp_ = kb.sb("l3lp", [128, 2 * T], F32, st=s2)
            ph.wait("sync", o_sem, o_tick)
            lastv = {}
            xo_dma = {0: None, 1: None}
            for o in range(DT):
                par = o % 2
                if xo_dma[par] is not None:
                    ph.wait("sync", dxo[par], xo_dma[par])
                lv = ph.do("sync", lambda e, o=o, par=par: e.dma_start(
                    vt_[:, par * T:(par + 1) * T],
                    scrV[o * 128:(o + 1) * 128, :]), dov[par], 16)
                lo = ph.do("sync", lambda e, o=o, par=par: e.dma_start(
                    ot_[:, par * T:(par + 1) * T],
                    scrO[o * 128:(o + 1) * 128, :]), doo[par], 16)
                lx = ph.do("sync", lambda e, o=o, par=par: e.dma_start(
                    xi_[:, par * T:(par + 1) * T],
                    xin[o * 128:(o + 1) * 128, :]), dxi[par], 16)
                ph.wait("vector", dov[par], lv)
                if xo_dma[par] is not None:
                    ph.wait("vector", dxo[par], xo_dma[par])
                vsl = vt_[:, par * T:(par + 1) * T]
                lsl = lp_[:, par * T:(par + 1) * T]
                t1 = ph.do("vector", lambda e, o=o, vsl=vsl, lsl=lsl:
                           e.tensor_single_scalar(
                    lsl, vsl, bsb[:, 24 + o * 3 + 1:24 + o * 3 + 2], ALU.mult),
                    sv, 1)
                ph.wait("vector", sv, t1)
                t2 = ph.do("vector", lambda e, o=o, vsl=vsl, lsl=lsl:
                           e.scalar_tensor_tensor(
                    lsl[:, 1:T], vsl[:, 0:T - 1],
                    bsb[:, 24 + o * 3 + 0:24 + o * 3 + 1], lsl[:, 1:T],
                    ALU.mult, ALU.add), sv, 1)
                ph.wait("vector", sv, t2)
                t3 = ph.do("vector", lambda e, o=o, vsl=vsl, lsl=lsl:
                           e.scalar_tensor_tensor(
                    lsl[:, 0:T - 1], vsl[:, 1:T],
                    bsb[:, 24 + o * 3 + 2:24 + o * 3 + 3], lsl[:, 0:T - 1],
                    ALU.mult, ALU.add), sv, 1)
                ph.wait("vector", sv, t3)
                t4 = ph.do("vector", lambda e, o=o, lsl=lsl:
                           e.tensor_single_scalar(
                    lsl, lsl, bsb[:, 48 + o:48 + o + 1], ALU.add), sv, 1)
                ph.wait("vector", sv, t4)
                ph.wait("vector", dxi[par], lx)
                t5 = ph.do("vector", lambda e, par=par, lsl=lsl:
                           e.tensor_tensor(
                    lsl, lsl, xi_[:, par * T:(par + 1) * T], ALU.add), sv, 1)
                ph.wait("vector", sv, t5)
                ph.wait("vector", doo[par], lo)
                t6 = ph.do("vector", lambda e, par=par, lsl=lsl:
                           e.tensor_tensor(
                    lsl, lsl, ot_[:, par * T:(par + 1) * T], ALU.add), sv, 1)
                lastv[o] = t6
                ph.wait("sync", sv, t6)
                xo_dma[par] = ph.do("sync", lambda e, o=o, lsl=lsl:
                                    e.dma_start(
                    xout[o * 128:(o + 1) * 128, :], lsl), dxo[par], 16)
            ph.emit()
    return [(s_, t_) for s_, t_ in
            ((dxo[0], xo_dma[0]), (dxo[1], xo_dma[1])) if t_ is not None]


def build_l3_attn(T):
    return standalone(decl_l3_attn, sub_l3_attn, L3A_CONSTS, T,
                      ["q", "k", "v", "o"])


L3FF_CONSTS = ["ident", "ones", "sel9"]


def decl_l3_ffn(nc, T, px=""):
    w = {}
    w["w1p"] = nc.dram_tensor(px + "w1p", [9, D, 256], F32,
                              kind="ExternalInput").ap()
    w["w2p"] = nc.dram_tensor(px + "w2p", [9, 256, D], F32,
                              kind="ExternalInput").ap()
    w["wo1"] = nc.dram_tensor(px + "wo1", [D, 256], F32,
                              kind="ExternalInput").ap()
    w["wo2"] = nc.dram_tensor(px + "wo2", [256, 9], F32,
                              kind="ExternalInput").ap()
    w["b2p"] = nc.dram_tensor(px + "b2p", [9, D], F32,
                              kind="ExternalInput").ap()
    w["mixk_t"] = nc.dram_tensor(px + "mixk_t", [128, 24], F32,
                                 kind="ExternalInput").ap()
    w["b1p_t"] = nc.dram_tensor(px + "b1p_t", [128, 18], F32,
                                kind="ExternalInput").ap()
    w["bo1_t"] = nc.dram_tensor(px + "bo1_t", [128, 2], F32,
                                kind="ExternalInput").ap()
    w["bo2_r"] = nc.dram_tensor(px + "bo2_r", [1, 9], F32,
                                kind="ExternalInput").ap()
    w["gam_t"] = nc.dram_tensor(px + "gam_t", [128, 8], F32,
                                kind="ExternalInput").ap()
    w["bet_t"] = nc.dram_tensor(px + "bet_t", [128, 8], F32,
                                kind="ExternalInput").ap()
    return w


def sub_l3_ffn(kb, cst, caps, w, scr, xin, xout):
    T = kb.T
    ph = Ph(kb)
    ds = kb.sem("dc")
    dw = kb.sem("dwx")
    sv = kb.sem("v")
    ss = kb.sem("s")
    stt = kb.sem("t")
    scrS = scr["s"]
    w1p, w2p, wo1, wo2, b2p = (w["w1p"], w["w2p"], w["wo1"], w["wo2"],
                               w["b2p"])
    with contextlib.ExitStack() as st0:
        bsb = kb.sb("bvecs", [128, 24 + 18 + 2 + 8 + 8], F32, st=st0)
        OFF = {"mix": 0, "b1": 24, "bo1": 42, "gam": 44, "bet": 52}
        t_b = None
        for name, m in (("mixk_t", 24), ("b1p_t", 18), ("bo1_t", 2),
                        ("gam_t", 8), ("bet_t", 8)):
            key = {"mixk_t": "mix", "b1p_t": "b1", "bo1_t": "bo1",
                   "gam_t": "gam", "bet_t": "bet"}[name]
            t_b = ph.do("sync", lambda e, name=name, key=key, m=m:
                        e.dma_start(bsb[:, OFF[key]:OFF[key] + m],
                                    w[name][:]), ds, 16)
        bo2sb = kb.sb("bo2", [1, 9], BF16, st=st0)
        wo2sb = kb.sb("wo2", [128, 2 * 9], BF16, st=st0)
        b2sb = kb.sb("b2sb", [9, D], BF16, st=st0)
        sel9sb = load_sel(kb, cst, ph, "sel9", 9, st0)
        dgw = kb.sem("dwx")
        t_g0 = ph.do("gpsimd", lambda e: e.dma_start(
            wo2sb[:].rearrange("p (k m) -> p k m", k=2),
            wo2.rearrange("(k p) m -> p k m", p=128)), dgw, 16)
        t_g = ph.do("gpsimd", lambda e: e.dma_start(bo2sb[:], w["bo2_r"][:]),
                    dgw, 16)
        t_g = ph.do("gpsimd", lambda e: e.dma_start(b2sb[:], b2p[:]), dgw, 16)
        for eng in ENGS:
            ph.wait(eng, ds, t_b)
            ph.wait(eng, dgw, t_g)
        TBc = min(2, kb.TC)
        TB = TBc * 512
        NTB = T // TB
        dslot = kb.sem("dslot")
        s14 = contextlib.ExitStack()
        selT = kb.sb("selT", [9, T], BF16, st=s14, side="right")
        th = kb.sb("thbuf", [128, 2 * T], BF16, st=s14, side="right")
        with contextlib.ExitStack() as s1:
            xn = kb.sb("xn", [128, DT * T], BF16, st=s1)
            lt = emit_ln(kb, ph, s1, xin, xn, cst)
            xmix = kb.sb("xmix", [128, DT * T], BF16, st=s14, side="right")
            sl_t = None
            mix_t = {}
            for j in range(DT):
                ph.wait("sync", sv, lt[j])
                sl_t = ph.do("sync", lambda e, j=j: e.dma_start(
                    scrS[j * 128:(j + 1) * 128, :], xn[:, j * T:(j + 1) * T]),
                    dslot, 16)
                xs = xn[:, j * T:(j + 1) * T]
                ms = xmix[:, j * T:(j + 1) * T]
                t1 = ph.do("vector", lambda e, j=j, xs=xs, ms=ms:
                           e.tensor_single_scalar(
                    ms, xs, bsb[:, OFF["mix"] + j * 3 + 1:
                                OFF["mix"] + j * 3 + 2], ALU.mult), sv, 1)
                ph.wait("vector", sv, t1)
                t2 = ph.do("vector", lambda e, j=j, xs=xs, ms=ms:
                           e.scalar_tensor_tensor(
                    ms[:, 1:T], xs[:, 0:T - 1],
                    bsb[:, OFF["mix"] + j * 3 + 0:OFF["mix"] + j * 3 + 1],
                    ms[:, 1:T], ALU.mult, ALU.add), sv, 1)
                ph.wait("vector", sv, t2)
                t3 = ph.do("vector", lambda e, j=j, xs=xs, ms=ms:
                           e.scalar_tensor_tensor(
                    ms[:, 0:T - 1], xs[:, 1:T],
                    bsb[:, OFF["mix"] + j * 3 + 2:OFF["mix"] + j * 3 + 3],
                    ms[:, 0:T - 1], ALU.mult, ALU.add), sv, 1)
                mix_t[j] = t3
            ph.emit()
        mix_done = [(sv, mix_t[DT - 1]), (dslot, sl_t)]
        ev_prev = mix_done

        def evt(o, c, psum, mmw, tb=None):
            ph.wait("scalar", mmw[0], mmw[1])
            et = ph.do("scalar", lambda e: e.activation(
                th[:, o * T + tb * TB + c * 512:o * T + tb * TB + (c + 1) * 512],
                psum[:], AF.Tanh, bias=bsb[:, OFF["bo1"] + o:OFF["bo1"] + o + 1]),
                ss, 1)
            return (ss, et)

        for tb in range(NTB):
            ev_prev = emit_mm_big(
                kb, ph, "th",
                lambda k, c, tb=tb: xmix[:, k * T + tb * TB + c * 512:
                                         k * T + tb * TB + (c + 1) * 512],
                DT, wo1, 256, TBc,
                lambda o, c, psum, mmw, tb=tb: evt(o, c, psum, mmw, tb=tb),
                og_tiles=2, wmode="f32", wait_rhs=ev_prev)
        with contextlib.ExitStack() as s3:
            scb = kb.sb("scb", [128, 2 * 16], F32, st=s3)
            mxb = kb.sb("mxb", [128, 2], F32, st=s3)
            selb = kb.sb("selb", [128, 2 * 16], BF16, st=s3)
            psc = [kb.ps("psc", [128, 9], F32, st=s3) for _ in range(2)]
            pst_ = [kb.ps("pselT", [9, 128], BF16, st=s3) for _ in range(2)]
            for s_, t_ in ev_prev:
                ph.wait("tensor", s_, t_)
            sel_ev = {}
            NB = T // 128
            for b in range(NB):
                par = b % 2
                if b >= 2:
                    ph.wait("tensor", sv, sel_ev[b - 2])
                for k in range(2):
                    mm = ph.do("tensor", lambda e, b=b, k=k, par=par: e.matmul(
                        psc[par][:], th[:, k * T + b * 128:k * T + (b + 1) * 128],
                        wo2sb[:, k * 9:(k + 1) * 9],
                        start=(k == 0), stop=False), None)
                mm = ph.do("tensor", lambda e, b=b, par=par: e.matmul(
                    psc[par][:], cst.ones[0:1, :], bo2sb[:],
                    start=False, stop=True), stt, 1)
                ph.wait("vector", stt, mm)
                c1 = ph.do("vector", lambda e, par=par: e.tensor_copy(
                    scb[:, par * 16:par * 16 + 9], psc[par][:]), sv, 1)
                ph.wait("vector", sv, c1)
                c2 = ph.do("vector", lambda e, par=par: e.tensor_reduce(
                    mxb[:, par:par + 1], scb[:, par * 16:par * 16 + 9], AX,
                    ALU.max), sv, 1)
                ph.wait("vector", sv, c2)
                c3 = ph.do("vector", lambda e, par=par: e.tensor_tensor(
                    selb[:, par * 16:par * 16 + 9],
                    scb[:, par * 16:par * 16 + 9],
                    mxb[:, par:par + 1].broadcast_to([128, 9]), ALU.is_ge),
                    sv, 1)
                ph.wait("tensor", sv, c3)
                tr = ph.do("tensor", lambda e, par=par: e.transpose(
                    pst_[par][:], selb[:, par * 16:par * 16 + 9],
                    cst.ident[:]), stt, 1)
                ph.wait("vector", stt, tr)
                sel_ev[b] = ph.do("vector", lambda e, b=b, par=par:
                                  e.tensor_copy(
                    selT[:, b * 128:(b + 1) * 128], pst_[par][:]), sv, 1)
            ph.emit()
            sel_done = sel_ev[NB - 1]
        coll = kb.sb("coll", [128, DT * T], BF16, st=st0)
        with contextlib.ExitStack() as s4:
            hp = kb.sb("hp", [128, 2 * T], BF16, st=s4)
            sbc = kb.sb("sbc", [128, T], BF16, st=s4)
            psb = [kb.ps("psb", [128, 512], F32, st=s4) for _ in range(2)]
            ppat = [kb.ps("ppat", [128, 512], F32, st=s4) for _ in range(2)]
            psb2 = [kb.ps("psb2", [128, 512], F32, st=s4)]
            psb2_gate = {0: None}
            hp_guard = []
            sbc_guard = None
            coll_ev = {}
            for p in range(9):
                bct = {}
                for c in range(kb.TC):
                    pb2 = c % 2
                    if c >= 2:
                        ph.wait("tensor", sv, bct[c - 2])
                    elif sbc_guard is not None:
                        ph.wait("tensor", sv, sbc_guard)
                    if p == 0 and c == 0:
                        ph.wait("tensor", sv, sel_done)
                    bm = ph.do("tensor", lambda e, p=p, c=c, pb2=pb2: e.matmul(
                        psb[pb2][:], sel9sb[:, p * 128:(p + 1) * 128],
                        selT[:, c * 512:(c + 1) * 512], start=True, stop=True),
                        stt, 1)
                    ph.wait("vector", stt, bm)
                    bct[c] = ph.do("vector", lambda e, c=c, pb2=pb2:
                                   e.tensor_copy(
                        sbc[:, c * 512:(c + 1) * 512], psb[pb2][:]), sv, 1)
                sbc_done = bct[kb.TC - 1]

                def evh(o, c2, psum, mmw, p=p, tb=None):
                    ph.wait("scalar", mmw[0], mmw[1])
                    et = ph.do("scalar", lambda e: e.activation(
                        hp[:, o * T + tb * TB + c2 * 512:
                           o * T + tb * TB + (c2 + 1) * 512], psum[:],
                        AF.Identity,
                        bias=bsb[:, OFF["b1"] + p * 2 + o:
                                 OFF["b1"] + p * 2 + o + 1]), ss, 1)
                    return (ss, et)

                ev_prev2 = hp_guard or [(sv, sbc_done)]
                for tb in range(NTB):
                    ev_prev2 = emit_mm_big(
                        kb, ph, "hp",
                        lambda k, c2, tb=tb: xmix[:, k * T + tb * TB + c2 * 512:
                                                  k * T + tb * TB +
                                                  (c2 + 1) * 512],
                        DT, w1p[p], 256, TBc,
                        lambda o, c2, psum, mmw, p=p, tb=tb: evh(
                            o, c2, psum, mmw, p=p, tb=tb),
                        og_tiles=1, wmode="f32", wait_rhs=ev_prev2)
                mk = {}
                for o in range(2):
                    for s_, t_ in ev_prev2:
                        ph.wait("vector", s_, t_)
                    ph.wait("vector", sv, sbc_done)
                    mk[o] = ph.do("vector", lambda e, o=o: e.tensor_tensor(
                        hp[:, o * T:(o + 1) * T], hp[:, o * T:(o + 1) * T],
                        sbc[:], ALU.mult), sv, 1)
                wsb2 = kb.sb("w2sb", [128, 2 * D], BF16, st=s4) \
                    if p == 0 else wsb2
                if p > 0:
                    s_, t_ = coll_ev[(p - 1, DT * kb.TC - 1)]
                    ph.wait("gpsimd", s_, t_)
                dwt = ph.do("gpsimd", lambda e, p=p: e.dma_start(
                    wsb2[:].rearrange("p (k m) -> p k m", k=2),
                    w2p[p].rearrange("(k pp) m -> pp k m", pp=128)), dw, 16)
                ph.wait("tensor", dw, dwt)
                gi = 0
                for o in range(DT):
                    for c in range(kb.TC):
                        bi = gi % 2
                        if gi >= 2:
                            s_, t_ = coll_ev[(p, gi - 2)]
                            ph.wait("tensor", s_, t_)
                        elif p > 0:
                            s_, t_ = coll_ev[(p - 1, gi + DT * kb.TC - 2)]
                            ph.wait("tensor", s_, t_)
                        if gi == 0:
                            ph.wait("tensor", sv, mk[1])
                        for k in range(2):
                            mm = ph.do("tensor", lambda e, k=k, o=o, c=c,
                                       bi=bi: e.matmul(
                                ppat[bi][:],
                                wsb2[:, k * D + o * 128:k * D + (o + 1) * 128],
                                hp[:, k * T + c * 512:k * T + (c + 1) * 512],
                                start=(k == 0), stop=(k == 1)),
                                stt if k == 1 else None)
                        if p == 0:
                            b2i = 0
                            if psb2_gate[b2i] is not None:
                                ph.wait("tensor", sv, psb2_gate[b2i])
                            mmb = ph.do("tensor", lambda e, o=o, c=c, b2i=b2i:
                                        e.matmul(
                                psb2[b2i][:],
                                b2sb[:, o * 128:(o + 1) * 128],
                                selT[:, c * 512:(c + 1) * 512],
                                start=True, stop=True), stt, 1)
                            ph.wait("scalar", stt, mm)
                            ev0 = ph.do("scalar", lambda e, o=o, c=c, bi=bi:
                                        e.activation(
                                coll[:, o * T + c * 512:o * T + (c + 1) * 512],
                                ppat[bi][:], AF.Identity), ss, 1)
                            ph.wait("vector", ss, ev0)
                            ph.wait("vector", stt, mmb)
                            ev = ph.do("vector", lambda e, o=o, c=c, b2i=b2i:
                                       e.tensor_tensor(
                                coll[:, o * T + c * 512:o * T + (c + 1) * 512],
                                coll[:, o * T + c * 512:o * T + (c + 1) * 512],
                                psb2[b2i][:], ALU.add), sv, 1)
                            psb2_gate[b2i] = ev
                            coll_ev[(p, gi)] = (sv, ev)
                        else:
                            ph.wait("vector", stt, mm)
                            ev = ph.do("vector", lambda e, o=o, c=c, bi=bi:
                                       e.tensor_tensor(
                                coll[:, o * T + c * 512:o * T + (c + 1) * 512],
                                coll[:, o * T + c * 512:o * T + (c + 1) * 512],
                                ppat[bi][:], ALU.add), sv, 1)
                            coll_ev[(p, gi)] = (sv, ev)
                        gi += 1
                hp_guard = [coll_ev[(p, DT * kb.TC - 1)]]
                sbc_guard = mk[1]
            ph.emit()
        s14.close()
        dsl2 = [kb.sem("daq"), kb.sem("dak")]
        with contextlib.ExitStack() as s5:
            slb = kb.sb("slb", [128, T], BF16, st=s5)
            ladd = {}
            for j in range(DT):
                par = j % 2
                if j >= 1:
                    ph.wait("sync", sv, ladd[j - 1])
                lt_ = ph.do("sync", lambda e, j=j: e.dma_start(
                    slb[:], scrS[j * 128:(j + 1) * 128, :]), dsl2[par], 16)
                ph.wait("vector", dsl2[par], lt_)
                s_, t_ = coll_ev[(8, j * kb.TC + kb.TC - 1)]
                ph.wait("vector", s_, t_)
                ladd[j] = ph.do("vector", lambda e, j=j: e.tensor_tensor(
                    coll[:, j * T:(j + 1) * T], coll[:, j * T:(j + 1) * T],
                    slb[:], ALU.add), sv, 1)
            ph.emit()
            xnf = coll
            lnt = emit_ln(kb, ph, s5, None, xnf, cst,
                          gains=(bsb[:, OFF["gam"]:OFF["gam"] + 8],
                                 bsb[:, OFF["bet"]:OFF["bet"] + 8]),
                          wait_first=[(sv, ladd[DT - 1])],
                          xsrc_inplace=True)
            dxi = [kb.sem("dxi0"), kb.sem("dxi1")]
            dxo = [kb.sem("dxo0"), kb.sem("dxo1")]
            xi_ = kb.sb("l3fxi", [128, T], F32, st=s5)
            xo_ = kb.sb("l3fxo", [128, T], F32, st=s5)
            xo_dma = {0: None, 1: None}
            for j in range(DT):
                par = j % 2
                for p2 in (0, 1):
                    if xo_dma[p2] is not None:
                        ph.wait("sync", dxo[p2], xo_dma[p2])
                lx = ph.do("sync", lambda e, j=j: e.dma_start(
                    xi_[:], xin[j * 128:(j + 1) * 128, :]), dxi[par], 16)
                ph.wait("vector", dxi[par], lx)
                ph.wait("vector", sv, lnt[j])
                for p2 in (0, 1):
                    if xo_dma[p2] is not None:
                        ph.wait("vector", dxo[p2], xo_dma[p2])
                t6 = ph.do("vector", lambda e, j=j: e.tensor_tensor(
                    xo_[:], xi_[:], xnf[:, j * T:(j + 1) * T], ALU.add), sv, 1)
                ph.wait("sync", sv, t6)
                xo_dma[par] = ph.do("sync", lambda e, j=j: e.dma_start(
                    xout[j * 128:(j + 1) * 128, :], xo_[:]), dxo[par], 16)
            ph.emit()
    return [(s_, t_) for s_, t_ in
            ((dxo[0], xo_dma[0]), (dxo[1], xo_dma[1])) if t_ is not None]


def build_l3_ffn(T):
    return standalone(decl_l3_ffn, sub_l3_ffn, L3FF_CONSTS, T, ["s"])


L2A_CONSTS = ["ident", "ones", "sel16"]


def decl_l2_attn(nc, T, px=""):
    w = {}
    w["win"] = nc.dram_tensor(px + "win", [D, 4096], F32,
                              kind="ExternalInput").ap()
    w["wx"] = nc.dram_tensor(px + "wx", [2048, 96], F32,
                             kind="ExternalInput").ap()
    w["wdt"] = nc.dram_tensor(px + "wdt", [64, 2048], F32,
                              kind="ExternalInput").ap()
    w["wout2"] = nc.dram_tensor(px + "wout2", [2048, D], F32,
                                kind="ExternalInput").ap()
    for n, m in (("cw", 64), ("cb", 16), ("bdt", 16), ("alog", 256),
                 ("dp", 16)):
        w[n + "_t"] = nc.dram_tensor(px + n + "_t", [128, m], F32,
                                     kind="ExternalInput").ap()
    return w


def sub_l2_attn(kb, cst, caps, w, scr, xin, xout):
    T = kb.T
    ph = Ph(kb)
    ds = kb.sem("dc")
    sv = kb.sem("v")
    ss = kb.sem("s")
    stt = kb.sem("t")
    NJ = 16
    win, wx, wdt, wout = w["win"], w["wx"], w["wdt"], w["wout2"]
    scrXZ, scrXC, scrDT, scrYZ = (scr["xz"], scr["xc"], scr["dts"],
                                  scr["yz"])
    with contextlib.ExitStack() as st0:
        s45 = st0.enter_context(contextlib.ExitStack())
        bsb = kb.sb("bvecs", [128, 64 + 16 * 3 + 256], F32, st=st0)
        OFF = {"cw": 0, "cb": 64, "bdt": 80, "dp": 96, "alog": 112}
        t_b = None
        for n, m in (("cw", 64), ("cb", 16), ("bdt", 16), ("dp", 16),
                     ("alog", 256)):
            t_b = ph.do("sync", lambda e, n=n, m=m: e.dma_start(
                bsb[:, OFF[n]:OFF[n] + m], w[n + "_t"][:]), ds, 16)
        for eng in ENGS:
            ph.wait(eng, ds, t_b)
        ta = ph.do("scalar", lambda e: e.activation(
            bsb[:, OFF["alog"]:OFF["alog"] + 256],
            bsb[:, OFF["alog"]:OFF["alog"] + 256], AF.Exp), ss, 1)
        ph.wait("vector", ss, ta)
        ta2 = ph.do("vector", lambda e: e.tensor_single_scalar(
            bsb[:, OFF["alog"]:OFF["alog"] + 256],
            bsb[:, OFF["alog"]:OFF["alog"] + 256], -1.0, ALU.mult), sv, 1)
        ph.wait("scalar", sv, ta2)
        A_sb = bsb[:, OFF["alog"]:OFF["alog"] + 256]
        TBc = min(2, kb.TC)
        TB = TBc * 512
        NTB = T // TB
        dxz = [kb.sem("dxz0"), kb.sem("dxz1")]
        scope = kb.nc.named_scope
        with kb.nc.named_scope("p12"), contextlib.ExitStack() as s1:
            xn = kb.sb("xn", [128, DT * T], BF16, st=s1)
            lt = emit_ln(kb, ph, s1, xin, xn, cst)
            stg = kb.sb("xzstg", [128, 2 * TB], BF16, st=s1)
            stg_gate = {0: None, 1: None}
            gctr = [0]
            ev_prev = [(sv, lt[-1])]
            for tb in range(NTB):
                def evxz(o, c, psum, mmw, tb=tb):
                    ph.wait("scalar", mmw[0], mmw[1])
                    par = gctr[0] % 2
                    if c == 0 and stg_gate[par] is not None:
                        ph.wait("scalar", dxz[par], stg_gate[par])
                    et = ph.do("scalar", lambda e, c=c, par=par: e.activation(
                        stg[:, par * TB + c * 512:par * TB + (c + 1) * 512],
                        psum[:], AF.Identity), ss, 1)
                    if c == TBc - 1:
                        ph.wait("sync", ss, et)
                        dt_ = ph.do("sync", lambda e, o=o, tb=tb, par=par:
                                    e.dma_start(
                            scrXZ[o * 128:(o + 1) * 128, tb * TB:(tb + 1) * TB],
                            stg[:, par * TB:(par + 1) * TB]), dxz[par], 16)
                        stg_gate[par] = dt_
                        gctr[0] += 1
                    return (ss, et)
                ev_prev = emit_mm_big(
                    kb, ph, "xz",
                    lambda k, c, tb=tb: xn[:, k * T + tb * TB + c * 512:
                                           k * T + tb * TB + (c + 1) * 512],
                    DT, win, 4096, TBc, evxz, og_tiles=2, wmode="f32",
                    wait_rhs=ev_prev)
            ph.emit()
        xz_done = [(dxz[0], stg_gate[0]), (dxz[1], stg_gate[1])]
        dxc = [kb.sem("dxc0"), kb.sem("dxc1")]
        dxco = [kb.sem("dxco0"), kb.sem("dxco1")]
        with kb.nc.named_scope("conv"), contextlib.ExitStack() as s3:
            xcb = kb.sb("xcin", [128, 2 * T], BF16, st=s3)
            acc = kb.sb("xcacc", [128, T], F32, st=s3)
            ob = kb.sb("xcout", [128, 2 * T], BF16, st=s3)
            for eng in ("sync",):
                for s_, t_ in xz_done:
                    ph.wait(eng, s_, t_)
            out_dma = {0: None, 1: None}
            acc_gate = None
            for j in range(NJ):
                par = j % 2
                if out_dma[par] is not None:
                    ph.wait("sync", dxco[par], out_dma[par])
                lt_ = ph.do("sync", lambda e, j=j, par=par: e.dma_start(
                    xcb[:, par * T:(par + 1) * T],
                    scrXZ[j * 128:(j + 1) * 128, :]), dxc[par], 16)
                ph.wait("vector", dxc[par], lt_)
                if acc_gate is not None:
                    ph.wait("vector", ss, acc_gate)
                xs = xcb[:, par * T:(par + 1) * T]
                t1 = ph.do("vector", lambda e, j=j, xs=xs: e.tensor_single_scalar(
                    acc[:], xs, bsb[:, OFF["cw"] + j * 4 + 3:
                                    OFF["cw"] + j * 4 + 4], ALU.mult), sv, 1)
                for kk in range(3):
                    sh = 3 - kk
                    ph.wait("vector", sv, kb.sems["v"].n)
                    t1 = ph.do("vector", lambda e, j=j, kk=kk, sh=sh, xs=xs:
                               e.scalar_tensor_tensor(
                        acc[:, sh:T], xs[:, 0:T - sh],
                        bsb[:, OFF["cw"] + j * 4 + kk:OFF["cw"] + j * 4 + kk + 1],
                        acc[:, sh:T], ALU.mult, ALU.add), sv, 1)
                ph.wait("scalar", sv, t1)
                if out_dma[par] is not None:
                    ph.wait("scalar", dxco[par], out_dma[par])
                t2 = ph.do("scalar", lambda e, j=j, par=par: e.activation(
                    ob[:, par * T:(par + 1) * T], acc[:], AF.Silu,
                    bias=bsb[:, OFF["cb"] + j:OFF["cb"] + j + 1]), ss, 1)
                acc_gate = t2
                ph.wait("sync", ss, t2)
                out_dma[par] = ph.do("sync", lambda e, j=j, par=par:
                                     e.dma_start(
                    scrXC[j * 128:(j + 1) * 128, :],
                    ob[:, par * T:(par + 1) * T]), dxco[par], 16)
            ph.emit()
        xc_done = [(dxco[0], out_dma[0]), (dxco[1], out_dma[1])]
        dwx = kb.sem("dwx")
        dxc2 = [kb.sem("dxc20"), kb.sem("dxc21")]
        dtlow = kb.sb("dtlow", [64, T], BF16, st=s45, side="right")
        scrBC = scr["bc"]
        dbcs = kb.sem("dbcs")
        with kb.nc.named_scope("xp"), contextlib.ExitStack() as s4:
            wxsb = kb.sb("wxsb", [128, NJ * 96], BF16, st=s4)
            Bsb = kb.sb("Bsb", [16, T], BF16, st=s4)
            Csb = kb.sb("Csb", [16, T], BF16, st=s4)
            xck = kb.sb("xck", [128, 2 * NJ * TB], BF16, st=s4)
            pA = kb.ps("pA", [80, 512], F32, st=s4)
            pC = kb.ps("pC", [16, 512], F32, st=s4)
            for s_, t_ in xc_done:
                ph.wait("sync", s_, t_)
                ph.wait("gpsimd", s_, t_)
            wt = ph.do("gpsimd", lambda e: e.dma_start(
                wxsb[:].rearrange("p (k m) -> p k m", k=NJ),
                wx.rearrange("(k p) m -> p k m", p=128)), dwx, 16)
            ph.wait("tensor", dwx, wt)
            ld_gate = {0: None, 1: None}
            evg = []
            for tb2 in range(NTB):
                par = tb2 % 2
                if ld_gate[par] is not None:
                    ph.wait("sync", stt, ld_gate[par])
                lt_ = None
                for k in range(NJ):
                    lt_ = ph.do("sync", lambda e, k=k, tb2=tb2, par=par:
                                e.dma_start(
                        xck[:, par * NJ * TB + k * TB:par * NJ * TB +
                            (k + 1) * TB],
                        scrXC[k * 128:(k + 1) * 128,
                              tb2 * TB:(tb2 + 1) * TB]), dxc2[par], 16)
                ph.wait("tensor", dxc2[par], lt_)
                for c in range(TBc):
                    for s_, t_ in evg:
                        ph.wait("tensor", s_, t_)
                    evg = []
                    for k in range(NJ):
                        mA = ph.do("tensor", lambda e, k=k, c=c, par=par:
                                   e.matmul(
                            pA[:], wxsb[:, k * 96:k * 96 + 80],
                            xck[:, par * NJ * TB + k * TB + c * 512:
                                par * NJ * TB + k * TB + (c + 1) * 512],
                            start=(k == 0), stop=(k == NJ - 1)),
                            stt if k == NJ - 1 else None)
                    for k in range(NJ):
                        mC = ph.do("tensor", lambda e, k=k, c=c, par=par:
                                   e.matmul(
                            pC[:], wxsb[:, k * 96 + 80:k * 96 + 96],
                            xck[:, par * NJ * TB + k * TB + c * 512:
                                par * NJ * TB + k * TB + (c + 1) * 512],
                            start=(k == 0), stop=(k == NJ - 1)),
                            stt if k == NJ - 1 else None)
                    cc = tb2 * TB + c * 512
                    ph.wait("vector", stt, mC)
                    e1 = ph.do("vector", lambda e, cc=cc: e.tensor_copy(
                        dtlow[:, cc:cc + 512], pA[0:64, :]), sv, 1)
                    e2 = ph.do("vector", lambda e, cc=cc: e.tensor_copy(
                        Bsb[:, cc:cc + 512], pA[64:80, :]), sv, 1)
                    e3 = ph.do("vector", lambda e, cc=cc: e.tensor_copy(
                        Csb[:, cc:cc + 512], pC[:]), sv, 1)
                    evg = [(sv, e3)]
                ld_gate[par] = kb.sems["t"].n
            ph.wait("sync", sv, kb.sems["v"].n)
            ph.do("sync", lambda e: e.dma_start(scrBC[0:16, :], Bsb[:]),
                  dbcs, 16)
            bcs_t = ph.do("sync", lambda e: e.dma_start(scrBC[16:32, :],
                                                        Csb[:]), dbcs, 16)
            ph.emit()
        dwdt = kb.sem("dwx")
        ddt = [kb.sem("ddt0"), kb.sem("ddt1")]
        with kb.nc.named_scope("dt"), contextlib.ExitStack() as s5:
            wdsb = kb.sb("wdsb", [64, 2048], BF16, st=s5)
            sp0 = kb.sb("sp0", [128, T], F32, st=s5)
            sp1 = kb.sb("sp1", [128, T], F32, st=s5)
            dtb = kb.sb("dtstg", [128, 2 * T], BF16, st=s5)
            pD = [kb.ps("pD", [128, 512], F32, st=s5) for _ in range(2)]
            ph.wait("gpsimd", dbcs, bcs_t)
            wt2 = ph.do("gpsimd", lambda e: e.dma_start(wdsb[:], wdt[:]),
                        kb.sem("dwx"), 16)
            ph.wait("tensor", dwdt, wt2)
            ph.wait("tensor", sv, kb.sems["v"].n)
            dt_dma = {0: None, 1: None}
            ev_gate = {}
            sp_gate = None
            for o in range(NJ):
                par = o % 2
                for c in range(kb.TC):
                    bi = (o * kb.TC + c) % 2
                    if (o * kb.TC + c) >= 2:
                        s_, t_ = ev_gate[(o * kb.TC + c) - 2]
                        ph.wait("tensor", s_, t_)
                    mm = ph.do("tensor", lambda e, o=o, c=c, bi=bi: e.matmul(
                        pD[bi][:], wdsb[:, o * 128:(o + 1) * 128],
                        dtlow[:, c * 512:(c + 1) * 512],
                        start=True, stop=True, tile_position=(0, 0)), stt, 1)
                    ph.wait("scalar", stt, mm)
                    if sp_gate is not None and c == 0:
                        ph.wait("scalar", sv, sp_gate)
                    ev = ph.do("scalar", lambda e, o=o, c=c, bi=bi:
                               e.activation(
                        sp0[:, c * 512:(c + 1) * 512], pD[bi][:], AF.Identity,
                        bias=bsb[:, OFF["bdt"] + o:OFF["bdt"] + o + 1]), ss, 1)
                    ev_gate[o * kb.TC + c] = (ss, ev)
                ph.wait("scalar", ss, ev)
                t1 = ph.do("scalar", lambda e: e.activation(
                    sp1[:], sp0[:], AF.Abs), ss, 1)
                ph.wait("scalar", ss, t1)
                t2 = ph.do("scalar", lambda e: e.activation(
                    sp1[:], sp1[:], AF.Exp, scale=-1.0), ss, 1)
                ph.wait("scalar", ss, t2)
                t3 = ph.do("scalar", lambda e: e.activation(
                    sp1[:], sp1[:], AF.Ln, bias=1.0), ss, 1)
                ph.wait("scalar", ss, t3)
                t4 = ph.do("scalar", lambda e: e.activation(
                    sp0[:], sp0[:], AF.Relu), ss, 1)
                ph.wait("vector", ss, t4)
                if dt_dma[par] is not None:
                    ph.wait("vector", ddt[par], dt_dma[par])
                t5 = ph.do("vector", lambda e, par=par: e.tensor_tensor(
                    dtb[:, par * T:(par + 1) * T], sp0[:], sp1[:], ALU.add),
                    sv, 1)
                sp_gate = t5
                ph.wait("sync", sv, t5)
                dt_dma[par] = ph.do("sync", lambda e, o=o, par=par:
                                    e.dma_start(
                    scrDT[o * 128:(o + 1) * 128, :],
                    dtb[:, par * T:(par + 1) * T]), ddt[par], 16)
            ph.emit()
        s45.close()
        dt_done = [(ddt[0], dt_dma[0]), (ddt[1], dt_dma[1])]
        JG = 1
        djt = [kb.sem("djt0"), kb.sem("djt1")]
        djx = [kb.sem("djx0"), kb.sem("djx1")]
        djz = [kb.sem("djz0"), kb.sem("djz1")]
        dyz = [kb.sem("dyz0"), kb.sem("dyz1")]
        sp = kb.sem("p")
        with kb.nc.named_scope("scan"), contextlib.ExitStack() as s6:
            dtj = kb.sb("dtj", [128, T], BF16, st=s6)
            dtx = kb.sb("dtx", [128, T], BF16, st=s6)
            yacA = kb.sb("yacA", [128, T], F32, st=s6)
            xtmp = kb.sb("xtmp", [128, T], BF16, st=s6)
            bcB = [kb.sb("bcB", [128, T], BF16, st=s6) for _ in range(2)]
            bcC = [kb.sb("bcC", [128, T], BF16, st=s6) for _ in range(2)]
            dAb = [kb.sb("dAb", [128, T], BF16, st=s6) for _ in range(2)]
            ub = [kb.sb("ub", [128, T], BF16, st=s6) for _ in range(2)]
            hb = [kb.sb("hb", [128, T], BF16, st=s6) for _ in range(2)]
            tm2 = [kb.sb("tm2", [128, T], BF16, st=s6) for _ in range(2)]
            zb = kb.sb("zb", [128, T], BF16, st=s6)
            yzb = kb.sb("yzb", [128, 2 * T], BF16, st=s6)
            dbc = [kb.sem("dbc0"), kb.sem("dbc1")]
            for s_, t_ in dt_done:
                ph.wait("sync", s_, t_)
            ph.wait("sync", dbcs, bcs_t)
            yz_dma = {0: None, 1: None}
            scan_t = {}
            multp_t = {}
            mmt = {}
            ev8_last = None
            pY = [kb.ps("pY%d" % c8, [128, 512], F32, st=s6)
                  for c8 in range(kb.TC)]
            bc_t = {}
            bc_cons = {}
            prev_dtx = None
            prev_yz = {}
            for j in range(NJ):
                lt1 = ph.do("sync", lambda e, j=j: e.dma_start(
                    dtj[:], scrDT[j * 128:(j + 1) * 128, :]), djt[j % 2], 16)
                lt2 = ph.do("sync", lambda e, j=j: e.dma_start(
                    xtmp[:], scrXC[j * 128:(j + 1) * 128, :]), djx[j % 2], 16)
                ph.wait("vector", djt[j % 2], lt1)
                ph.wait("vector", djx[j % 2], lt2)
                if prev_dtx is not None:
                    ph.wait("vector", sv, prev_dtx)
                tdx = ph.do("vector", lambda e: e.tensor_tensor(
                    dtx[:], dtj[:], xtmp[:], ALU.mult), sv, 1)
                prev_dtx = tdx
                ph.wait("sync", sv, tdx)
                ph.wait("scalar", djt[j % 2], lt1)
                for n in range(16):
                    it = j * 16 + n
                    par = it % 2
                    if bc_cons.get(par) is not None:
                        ph.wait("sync", sp, bc_cons[par][0])
                        ph.wait("sync", sv, bc_cons[par][1])
                    ph.do("sync", lambda e, n=n, par=par: e.dma_start(
                        bcB[par][:],
                        scrBC[n:n + 1, :].partition_broadcast(128)),
                        dbc[par], 16)
                    bc_t[par] = ph.do("sync", lambda e, n=n, par=par:
                                      e.dma_start(
                        bcC[par][:],
                        scrBC[16 + n:17 + n, :].partition_broadcast(128)),
                        dbc[par], 16)
                    if scan_t.get(it - 2) is not None:
                        ph.wait("scalar", sv, scan_t[it - 2])
                    tda = ph.do("scalar", lambda e, j=j, n=n, par=par:
                                e.activation(
                        dAb[par][:], dtj[:], AF.Exp,
                        scale=A_sb[:, j * 16 + n:j * 16 + n + 1]), ss, 1)
                    if scan_t.get(it - 2) is not None:
                        ph.wait("vector", sv, scan_t[it - 2])
                    ph.wait("vector", dbc[par], bc_t[par])
                    tu = ph.do("vector", lambda e, par=par: e.tensor_tensor(
                        ub[par][:], dtx[:], bcB[par][:], ALU.mult), sv, 1)
                    ph.wait("vector", ss, tda)
                    ph.wait("vector", sv, tu)
                    if multp_t.get(it - 2) is not None:
                        ph.wait("vector", sp, multp_t[it - 2])
                    tsc = ph.do("vector", lambda e, par=par:
                                e.tensor_tensor_scan(
                        hb[par][:], dAb[par][:], ub[par][:], 0.0,
                        ALU.mult, ALU.add), sv, 1)
                    scan_t[it] = tsc
                    ph.wait("gpsimd", sv, tsc)
                    ph.wait("gpsimd", dbc[par], bc_t[par])
                    if mmt.get(it - 2) is not None:
                        ph.wait("gpsimd", stt, mmt[it - 2])
                    tmp_ = ph.do("gpsimd", lambda e, par=par: e.tensor_tensor(
                        tm2[par][:], hb[par][:], bcC[par][:], ALU.mult), sp, 1)
                    multp_t[it] = tmp_
                    bc_cons[par] = (tmp_, tu)
                    ph.wait("tensor", sp, tmp_)
                    if n == 0 and ev8_last is not None:
                        ph.wait("tensor", ss, ev8_last)
                    for c8 in range(kb.TC):
                        mm_it = ph.do("tensor", lambda e, c8=c8, n=n,
                                      par=par: e.matmul(
                            pY[c8][:], cst.ident[:],
                            tm2[par][:, c8 * 512:(c8 + 1) * 512],
                            start=(n == 0), stop=(n == 15),
                            skip_group_check=True), stt, 1)
                    mmt[it] = mm_it
                ph.wait("scalar", stt, mmt[j * 16 + 15])
                for c8 in range(kb.TC):
                    ev8_last = ph.do("scalar", lambda e, c8=c8: e.activation(
                        yacA[:, c8 * 512:(c8 + 1) * 512], pY[c8][:],
                        AF.Identity), ss, 1)
                ph.wait("vector", ss, ev8_last)
                tD = ph.do("vector", lambda e, j=j: e.scalar_tensor_tensor(
                    yacA[:], xtmp[:], bsb[:, OFF["dp"] + j:OFF["dp"] + j + 1],
                    yacA[:], ALU.mult, ALU.add), sv, 1)
                lt4 = ph.do("sync", lambda e, j=j: e.dma_start(
                    zb[:], scrXZ[2048 + j * 128:2048 + (j + 1) * 128, :]),
                    djz[j % 2], 16)
                ph.wait("scalar", djz[j % 2], lt4)
                if prev_yz.get(j % 2) is not None:
                    ph.wait("scalar", dyz[j % 2], prev_yz[j % 2])
                tz = ph.do("scalar", lambda e: e.activation(
                    zb[:], zb[:], AF.Silu), ss, 1)
                ph.wait("vector", ss, tz)
                ph.wait("vector", sv, tD)
                if prev_yz.get(j % 2) is not None:
                    ph.wait("vector", dyz[j % 2], prev_yz[j % 2])
                tyz = ph.do("vector", lambda e, j=j: e.tensor_tensor(
                    yzb[:, (j % 2) * T:(j % 2 + 1) * T], yacA[:], zb[:],
                    ALU.mult), sv, 1)
                ph.wait("sync", sv, tyz)
                yz_dma[j % 2] = ph.do("sync", lambda e, j=j: e.dma_start(
                    scrYZ[j * 128:(j + 1) * 128, :],
                    yzb[:, (j % 2) * T:(j % 2 + 1) * T]), dyz[j % 2], 16)
                prev_yz[j % 2] = yz_dma[j % 2]
                ph.wait("sync", sv, tyz)
                ph.wait("scalar", sv, tyz)
                ph.wait("vector", sv, tyz)
                ph.wait("gpsimd", sv, tyz)
            ph.emit()
        yz_done = [(dyz[0], yz_dma[0]), (dyz[1], yz_dma[1])]
        kb.nc.push_named_scope("out") if False else None
        fin = emit_ffn(kb, cst, ph, None, ups=[], M_hid=2048,
                       down=dict(w=wout, wmode="f32", og_tiles=1),
                       xin=xin, xout=xout, wait_ln=yz_done, h_src=scrYZ,
                       TBc=2)
    return fin


def build_l2_attn(T):
    return standalone(decl_l2_attn, sub_l2_attn, L2A_CONSTS, T,
                      ["xz", "xc", "dts", "yz", "bc"])


def _bias_tiles(b):
    return np.ascontiguousarray(np.asarray(b, np.float32).reshape(-1, 128).T)


def _hT(w):
    return np.ascontiguousarray(np.asarray(w).T)


FUSED = True


def kernel(**inputs):
    z = inputs
    T = 4096
    NCORE = 8
    consts = host_constants(T)

    x = np.asarray(z["L0_attn_x"], np.float32)
    xT = [np.ascontiguousarray(x[b].T) for b in range(NCORE)]

    wlep = np.asarray(z["L3_attn_w_lepe"], np.float32).reshape(
        8, 128, 3).transpose(1, 0, 2).reshape(128, 24)
    mixt = np.asarray(z["L3_ffn_mix_kernel"], np.float32).T.reshape(
        8, 128, 3).transpose(1, 0, 2).reshape(128, 24)
    b1t = np.asarray(z["L3_ffn_b1"], np.float32).reshape(
        9, 2, 128).transpose(2, 0, 1).reshape(128, 18)
    cwt = np.asarray(z["L2_attn_conv_w"], np.float32).reshape(
        16, 128, 4).transpose(1, 0, 2).reshape(128, 64)
    alt = np.asarray(z["L2_attn_A_log"], np.float32).reshape(
        16, 128, 16).transpose(1, 0, 2).reshape(128, 256)

    launches = [
        (build_l0_attn, {
            "wq": z["L0_attn_Wq"], "wkv": z["L0_attn_Wkv"],
            "wout": z["L0_attn_Wout"], "bout_t": _bias_tiles(z["L0_attn_bout"]),
        }),
        (build_l0_ffn, {
            "wg": _hT(z["L0_ffn_w_gate"]), "wu": _hT(z["L0_ffn_w_up"]),
            "wd": _hT(z["L0_ffn_w_down"]),
        }),
        (build_l1_attn, {
            "wq": z["L1_attn_Wq"], "wk": z["L1_attn_Wk"],
            "wv": z["L1_attn_Wv"], "wo": z["L1_attn_Wo"],
            "w1": z["L1_attn_W1"], "w2": z["L1_attn_W2"],
            "bq_t": _bias_tiles(z["L1_attn_bq"]),
            "bk_t": _bias_tiles(z["L1_attn_bk"]),
            "bv_t": _bias_tiles(z["L1_attn_bv"]),
            "bo_t": _bias_tiles(z["L1_attn_bo"]),
            "b1_t": _bias_tiles(z["L1_attn_b1"]),
            "b2_t": _bias_tiles(z["L1_attn_b2"]),
            "g1_t": _bias_tiles(z["L1_attn_g1"]),
            "be1_t": _bias_tiles(z["L1_attn_beta1"]),
            "g2_t": _bias_tiles(z["L1_attn_g2"]),
            "be2_t": _bias_tiles(z["L1_attn_beta2"]),
        }),
        (build_l1_ffn, {
            "w1": _hT(z["L1_ffn_w1_w"]), "w2": _hT(z["L1_ffn_w2_w"]),
            "h1": _hT(z["L1_ffn_h1_w"]), "h2": _hT(z["L1_ffn_h2_w"]),
            "w3": _hT(z["L1_ffn_w3_w"]),
            "s1_t": _bias_tiles(z["L1_ffn_w1_s"]),
            "s2_t": _bias_tiles(z["L1_ffn_w2_s"]),
            "s3_t": _bias_tiles(z["L1_ffn_h1_s"]),
            "s4_t": _bias_tiles(z["L1_ffn_h2_s"]),
            "s5_t": _bias_tiles(z["L1_ffn_w3_s"]),
        }),
        (build_l2_attn, {
            "win": _hT(z["L2_attn_W_in"]), "wx": _hT(z["L2_attn_W_x"]),
            "wdt": _hT(z["L2_attn_W_dt"]), "wout2": _hT(z["L2_attn_W_out"]),
            "cw_t": np.ascontiguousarray(cwt),
            "cb_t": _bias_tiles(z["L2_attn_conv_b"]),
            "bdt_t": _bias_tiles(z["L2_attn_b_dt"]),
            "alog_t": np.ascontiguousarray(alt),
            "dp_t": _bias_tiles(z["L2_attn_Dp"]),
        }),
        (build_l2_ffn, {
            "fc1q": np.asarray(z["L2_ffn_fc1_q"], np.int32),
            "fc2q": np.asarray(z["L2_ffn_fc2_q"], np.int32),
            "fc1s": np.asarray(z["L2_ffn_fc1_scales"], np.float32),
            "fc2s": np.asarray(z["L2_ffn_fc2_scales"], np.float32),
            "b1_t": _bias_tiles(z["L2_ffn_fc1_bias"]),
            "b2h_t": _bias_tiles(z["L2_ffn_fc2_bias"]),
            "lg_t": _bias_tiles(z["L2_ffn_ln_gamma"]),
            "lb_t": _bias_tiles(z["L2_ffn_ln_beta"]),
        }),
        (build_l3_attn, {
            "wqkv": _hT(z["L3_attn_w_qkv"]),
            "bqkv_t": _bias_tiles(z["L3_attn_b_qkv"]),
            "wlep_t": np.ascontiguousarray(wlep),
            "blep_t": _bias_tiles(z["L3_attn_b_lepe"]),
        }),
        (build_l3_ffn, {
            "w1p": np.asarray(z["L3_ffn_W1"], np.float32),
            "w2p": np.asarray(z["L3_ffn_W2"], np.float32),
            "wo1": np.asarray(z["L3_ffn_Wo1"], np.float32),
            "wo2": np.asarray(z["L3_ffn_Wo2"], np.float32),
            "b2p": np.asarray(z["L3_ffn_b2"], np.float32),
            "mixk_t": np.ascontiguousarray(mixt),
            "b1p_t": np.ascontiguousarray(b1t),
            "bo1_t": _bias_tiles(z["L3_ffn_bo1"]),
            "bo2_r": np.asarray(z["L3_ffn_bo2"], np.float32)[None, :],
            "gam_t": _bias_tiles(z["L3_ffn_gamma"]),
            "bet_t": _bias_tiles(z["L3_ffn_beta"]),
        }),
    ]

    if FUSED:
        nc = build_fused(T)
        cmap = {"c_" + k: consts[k] for k in nc.const_names}
        wall = {}
        for i, (_, _, px) in enumerate(SUBS):
            for k, v in launches[i][1].items():
                wall[px + "_" + k] = v
        in_maps = []
        for b in range(NCORE):
            m = {"xin": xT[b]}
            m.update(wall)
            m.update(cmap)
            in_maps.append(m)
        res = run_bass_kernel_spmd(nc, in_maps, core_ids=list(range(NCORE)))
        xT = [np.asarray(res.results[b]["xout"]) for b in range(NCORE)]
    else:
        for build, wmap in launches:
            nc = build(T)
            cmap = {"c_" + k: consts[k] for k in nc.const_names}
            in_maps = []
            for b in range(NCORE):
                m = {"xin": xT[b]}
                m.update(wmap)
                m.update(cmap)
                in_maps.append(m)
            res = run_bass_kernel_spmd(nc, in_maps, core_ids=list(range(NCORE)))
            xT = [np.asarray(res.results[b]["xout"]) for b in range(NCORE)]

    out = np.stack([xT[b].T for b in range(NCORE)], axis=0)
    return np.ascontiguousarray(out.astype(np.float32))


ALL_CONSTS = ["ident", "ones", "cos0", "sin0", "cosI", "sinI", "perm0",
              "permI", "sel8", "sel9", "sel16", "sel32"]
SUBS = [
    (decl_l0_attn, sub_l0_attn, "a0"),
    (decl_l0_ffn, sub_l0_ffn, "f0"),
    (decl_l1_attn, sub_l1_attn, "a1"),
    (decl_l1_ffn, sub_l1_ffn, "f1"),
    (decl_l2_attn, sub_l2_attn, "a2"),
    (decl_l2_ffn, sub_l2_ffn, "f2"),
    (decl_l3_attn, sub_l3_attn, "a3"),
    (decl_l3_ffn, sub_l3_ffn, "f3"),
]
ALL_SCR = ["q", "k", "v", "o", "s", "s2", "a", "xz", "xc", "dts", "yz", "bc",
           "w1q", "w2q"]


def build_fused(T):
    nc = bass.Bass("TRN2", target_bir_lowering=False)
    xin0 = nc.dram_tensor("xin", [D, T], F32, kind="ExternalInput").ap()
    xoutF = nc.dram_tensor("xout", [D, T], F32, kind="ExternalOutput").ap()
    xA = nc.dram_tensor("xA", [D, T], F32, kind="Internal").ap()
    xB = nc.dram_tensor("xB", [D, T], F32, kind="Internal").ap()
    caps = declare_constants(nc, T, ALL_CONSTS)
    nc.const_names = ALL_CONSTS
    scr = make_scratch(nc, T, ALL_SCR)
    ws = [decl(nc, T, px + "_") for decl, _, px in SUBS]
    with contextlib.ExitStack() as st:
        kb = KB(nc, st, T)
        cst = Cst(kb, caps, resident=["ident", "ones"])
        ph = Ph(kb)
        cld = cst.load(kb, ph)
        cldm = {}
        for s_, t_ in cld:
            cldm[s_] = max(cldm.get(s_, 0), t_)
        for eng in ENGS:
            for s_, t_ in cldm.items():
                ph.wait(eng, s_, t_)
        ph.emit()
        bar = kb.sem("bar")
        bufs = [xA, xB]
        cur = xin0
        for i, (decl, sub, px) in enumerate(SUBS):
            xo = xoutF if i == len(SUBS) - 1 else bufs[i % 2]
            with nc.named_scope(px):
                fin = sub(kb, cst, caps, ws[i], scr, cur, xo)
            phb = Ph(kb)
            for name, sm in list(kb.sems.items()):
                if sm.is_dma and sm.n > 0:
                    phb.wait("sync", sm, sm.n)
            for eng in ENGS:
                phb.do(eng, lambda e: e.nop(), bar, 1)
            target = bar.n
            for eng in ENGS:
                phb.wait(eng, bar, target)
            phb.emit()
            cur = xo
    return nc
